# version 25
# baseline (speedup 1.0000x reference)
import math
import numpy as np
import ml_dtypes
import concourse.bass as bass
import concourse.mybir as mybir
from concourse.bass_utils import run_bass_kernel_spmd

F32 = mybir.dt.float32
BF16 = mybir.dt.bfloat16
F16 = mybir.dt.float16
I32 = mybir.dt.int32
I16 = mybir.dt.int16
AF = mybir.ActivationFunctionType
ALU = mybir.AluOpType
AX = mybir.AxisListType

NCORE = 8
T = 8192
D = 2048
TOK = T // NCORE
NT = TOK // 128
H0 = 16
QLR, KVLR = 1024, 512
DN, DR, DV = 128, 64, 128
DQK = DN + DR
I0 = 5632
GU = 2 * I0
EPS = 1e-5

PHASES = 4
DEBUG = False
ENGINES = ("sync", "vector", "scalar", "tensor", "gpsimd")


class Plan:

    NLANES = 8

    class Op:
        __slots__ = ("fn", "eng", "is_dma", "marked", "waits", "count",
                     "lane", "lane_k")

        def __init__(self, fn, eng, is_dma):
            self.fn, self.eng, self.is_dma = fn, eng, is_dma
            self.marked = False
            self.waits = []
            self.count = None
            self.lane = None
            self.lane_k = None

    def __init__(self, nc):
        self.nc = nc
        self.ops = {e: [] for e in ENGINES}
        self.ord = {e: nc.alloc_semaphore(name=f"ord_{e}") for e in ENGINES}
        self.dma_lanes = {e: [nc.alloc_semaphore(name=f"dma_{e}_{i}")
                              for i in range(Plan.NLANES)]
                          for e in ENGINES}
        self._dma_cnt = {e: 0 for e in ENGINES}
        self._lane_cnt = {e: [0] * Plan.NLANES for e in ENGINES}

    def op(self, eng, fn, deps=(), dma=False):
        o = Plan.Op(fn, eng, dma)
        if dma:
            i = self._dma_cnt[eng]
            self._dma_cnt[eng] = i + 1
            o.lane = i % Plan.NLANES
            self._lane_cnt[eng][o.lane] += 1
            o.lane_k = self._lane_cnt[eng][o.lane]
        for d in deps:
            if d is None:
                continue
            if isinstance(d, (list, tuple)):
                for dd in d:
                    if dd is not None:
                        dd.marked = True
                        o.waits.append(dd)
            else:
                d.marked = True
                o.waits.append(d)
        self.ops[eng].append(o)
        return o

    def dmaop(self, eng, fn, deps=()):
        o = self.op(eng, fn, deps, dma=True)
        o.marked = True
        return o

    def barrier(self):
        lasts = [self.ops[e][-1] for e in ENGINES if self.ops[e]]
        outs = []
        for e in ENGINES:
            outs.append(self.op(e, "nop", deps=lasts))
        return outs

    def emit(self):
        for e in ENGINES:
            c_ord, c_dma = 0, 0
            for o in self.ops[e]:
                if o.marked:
                    if o.is_dma:
                        c_dma += 16
                        o.count = c_dma
                    else:
                        c_ord += 1
                        o.count = c_ord

        def emit_engine(engine_obj, eng_name):
            for o in self.ops[eng_name]:
                need = {}
                for p in o.waits:
                    if p.is_dma:
                        sem = self.dma_lanes[p.eng][p.lane]
                        val = 16 * p.lane_k
                    else:
                        sem = self.ord[p.eng]
                        val = p.count
                    key = id(sem)
                    if key not in need or need[key][1] < val:
                        need[key] = (sem, val)
                if o.is_dma and o.lane_k > 1:
                    key = id(self.dma_lanes[eng_name][o.lane])
                    v = 16 * (o.lane_k - 1)
                    if key not in need or need[key][1] < v:
                        need[key] = (self.dma_lanes[eng_name][o.lane], v)
                for sem, val in need.values():
                    engine_obj.wait_ge(sem, val)
                inst = engine_obj.nop() if o.fn == "nop" else o.fn(engine_obj)
                if o.is_dma:
                    inst.then_inc(self.dma_lanes[eng_name][o.lane], 16)
                elif o.marked:
                    inst.then_inc(self.ord[eng_name], 1)

        with self.nc.Block() as block:
            @block.sync
            def _(sync):
                emit_engine(sync, "sync")

            @block.vector
            def _(vector):
                emit_engine(vector, "vector")

            @block.scalar
            def _(scalar):
                emit_engine(scalar, "scalar")

            @block.tensor
            def _(tensor):
                emit_engine(tensor, "tensor")

            @block.gpsimd
            def _(gpsimd):
                emit_engine(gpsimd, "gpsimd")


def build_program():
    from contextlib import ExitStack
    nc = bass.Bass("TRN2", target_bir_lowering=False)
    P = Plan(nc)
    top = ExitStack()

    def sb(stack, name, shape, dt):
        return stack.enter_context(nc.sbuf_tensor(name, shape, dt))

    def psf(stack, name, banks):
        return stack.enter_context(nc.psum_tensor(name, [128, 512 * banks], F32))

    def psb(stack, name, banks):
        return stack.enter_context(nc.psum_tensor(name, [128, 1024 * banks], BF16))

    def din(name, shape, dt):
        return nc.dram_tensor(name, shape, dt, kind="ExternalInput").ap()

    def dint(name, shape, dt):
        return nc.dram_tensor(name, shape, dt).ap()

    x_in = din("x_own", [TOK, D], F32)
    wqaT = din("wqaT", [D, QLR], F32)
    bqa_row = din("bqa_row", [1, QLR], F32)
    gq_bc = din("gq_bc", [128, QLR], F32)
    bq_bc = din("bq_bc", [128, QLR], F32)
    wqbT = din("wqbT", [QLR, 2 * DQK], F32)
    bqb_nope = din("bqb_nope", [128, 2], F32)
    bqb_rope = din("bqb_rope", [64, 2], F32)
    wkvaT = din("wkvaT", [D, KVLR + DR], F32)
    bkva_cols = din("bkva_cols", [128, 4], F32)
    bkva_rope = din("bkva_rope", [64, 1], F32)
    wkvbT = din("wkvbT", [KVLR, 2 * (DN + DV)], F32)
    bkvb_nope = din("bkvb_nope", [128, 2], F32)
    bkvb_v = din("bkvb_v", [1, 2 * DV], F32)
    woutT = din("woutT", [D, D], F32)
    bout_row = din("bout_row", [1, D], F32)
    csA_t = din("csA_t", [64, T], F32)
    csB_t = din("csB_t", [64, T], F32)
    csA_own = din("csA_own", [64, TOK], F32)
    csB_own = din("csB_own", [64, TOK], F32)
    mask_h = din("mask_h", [128, 1920], BF16)
    eye_h = din("eye_h", [128, 128], BF16)
    if PHASES >= 2:
        gup_pk = din("gup_pk", [GU // 8, 256], I32)
        gup_scT = din("gup_scT", [GU // 8, 16], F16)
        dn_pk = din("dn_pk", [D // 8, 704], I32)
        dn_scT = din("dn_scT", [D // 8, 44], F16)
    if PHASES >= 3:
        w_qkv = din("w_qkv", [D, 3 * D], F32)
        w_o = din("w_o", [D, D], F32)
    if PHASES >= 4:
        w1T = din("w1T", [D, 4096], F32)
        b1_cols = din("b1_cols", [128, 32], F32)
        w2T = din("w2T", [4096, 4096], F32)
        b2_cols = din("b2_cols", [128, 32], F32)
        w3T = din("w3T", [4096, D], F32)
        b3_row = din("b3_row", [1, D], F32)

    out = nc.dram_tensor("out", [TOK, D], F32, kind="ExternalOutput").ap()
    if DEBUG:
        dbg_gout = nc.dram_tensor("dbg_gout", [NCORE * 1600, TOK], BF16,
                                  kind="ExternalOutput").ap()
        dbg_y = nc.dram_tensor("dbg_y", [2 * DV, T], BF16,
                               kind="ExternalOutput").ap()
        dbg_attn = nc.dram_tensor("dbg_attn", [TOK, D], F32,
                                  kind="ExternalOutput").ap()
        dbg_yf = nc.dram_tensor("dbg_yf", [NCORE * 2 * DV, T], BF16,
                                kind="ExternalOutput").ap()
        if PHASES >= 2:
            dbg_wg = nc.dram_tensor("dbg_wg", [NCORE * D, GU // 8], BF16,
                                    kind="ExternalOutput").ap()
            dbg_wd = nc.dram_tensor("dbg_wd", [NCORE * I0, D // 8], BF16,
                                    kind="ExternalOutput").ap()
            dbg_ffn = nc.dram_tensor("dbg_ffn", [512, D], F32,
                                     kind="ExternalOutput").ap()

    gin = dint("gin", [QLR + KVLR + DR, TOK], BF16)
    gout = dint("gout", [NCORE * (QLR + KVLR + DR), TOK], BF16)
    ygath = dint("ygath", [2 * DV, T], BF16)
    yfull = dint("yfull", [NCORE * 2 * DV, T], BF16)
    if PHASES >= 2:
        ffn_dram = dint("ffn_dram", [512, D], F32)
        wg_gin = dint("wg_gin", [D, GU // 8], BF16)
        wg_gout = dint("wg_gout", [NCORE * D, GU // 8], BF16)
        wd_gin = dint("wd_gin", [I0, D // 8], BF16)
        wd_gout = dint("wd_gout", [NCORE * I0, D // 8], BF16)
    if PHASES >= 3:
        kin = dint("kin", [D, TOK], BF16)
        kfull = dint("kfull", [NCORE * D, TOK], BF16)
        rs_in = dint("rs_in", [NCORE * H0, TOK], F32)
        rs_out = dint("rs_out", [H0, TOK], F32)
        vdram = dint("vdram", [D, TOK], BF16)
    if PHASES >= 4:
        ffnD_dram = dint("ffnD_dram", [512, D], F32)
        n4_dram = dint("n4_dram", [D, TOK], BF16)

    if PHASES >= 4:
        w1_bfd = dint("w1_bfd", [D, 4096], BF16)
        w2_bfd = dint("w2_bfd", [4096, 4096], BF16)
        w3_bfd = dint("w3_bfd", [4096, D], BF16)

    xres = sb(top, "xres", [128, NT * D], F32)
    rt_sh1 = sb(top, "rt_sh1", [32, 512], F32)
    rt_sh2 = sb(top, "rt_sh2", [32, 512], F32)
    ones_row = sb(top, "ones_row", [1, 128], BF16)
    ones_col = sb(top, "ones_col", [128, 1], BF16)
    eye_bf = sb(top, "eye_bf", [128, 128], BF16)
    SC = {}

    CONST_VALS = [EPS, 1e-6, 1.0 / math.sqrt(float(DQK)), 1.0 / math.sqrt(128.0)]
    cst_sb = sb(top, "cst_sb", [128, len(CONST_VALS)], F32)
    const_dep = {}
    for i, vv in enumerate(CONST_VALS):
        o = P.op("vector", lambda v, i=i, vv=vv: v.memset(cst_sb[:, i:i + 1], vv))
        nc.const_aps.register(F32, vv, cst_sb[:, i:i + 1])
        const_dep[vv] = o

    op_ones = P.op("vector", lambda v: v.memset(ones_row[:, :], 1.0))
    op_ones_col = P.op("vector", lambda v: v.memset(ones_col[:, :], 1.0))
    ld_eye = P.dmaop("sync", lambda s: s.dma_start(out=eye_bf[:, :], in_=eye_h))

    def alloc_stats(stack, tag, n):
        return {k: sb(stack, f"{tag}_{k}", [128, n], F32)
                for k in ("s", "sq", "mean", "msq", "var", "rstd")}

    def ln_stats(st, src, cols, deps, d, eps=EPS, war_deps=(), sqd=None):
        sq_dump = SC["sq_dump"] if sqd is None else sqd
        """cols: list of (idx, col_offset). Fills st mean/rstd at [:, idx]."""
        col_ops = []
        for idx, off in cols:
            col_ops.append(P.op("vector", lambda v, idx=idx, off=off: v.reduce_sum(
                st["s"][:, idx:idx + 1], src[:, off:off + d], axis=AX.X),
                deps=list(deps) + list(war_deps)))
            col_ops.append(P.op("scalar", lambda s, idx=idx, off=off, sq_dump=sq_dump: s.activation(
                sq_dump[:, 0:d], src[:, off:off + d], AF.Square,
                accum_out=st["sq"][:, idx:idx + 1]),
                deps=list(deps) + list(war_deps)))
        lo = min(i for i, _ in cols)
        hi = max(i for i, _ in cols) + 1
        m1 = P.op("vector", lambda v: v.tensor_scalar_mul(
            st["mean"][:, lo:hi], st["s"][:, lo:hi], 1.0 / d), deps=col_ops)
        m2 = P.op("vector", lambda v: v.tensor_scalar_mul(
            st["msq"][:, lo:hi], st["sq"][:, lo:hi], 1.0 / d), deps=col_ops)
        m3 = P.op("vector", lambda v: v.tensor_mul(
            st["var"][:, lo:hi], st["mean"][:, lo:hi], st["mean"][:, lo:hi]),
            deps=[m1])
        m4 = P.op("vector", lambda v: v.tensor_sub(
            st["var"][:, lo:hi], st["msq"][:, lo:hi], st["var"][:, lo:hi]),
            deps=[m2, m3])
        m5 = P.op("scalar", lambda s: s.activation(
            st["rstd"][:, lo:hi], st["var"][:, lo:hi], AF.Ln, bias=eps),
            deps=[m4, const_dep[eps]] + list(war_deps))
        m6 = P.op("scalar", lambda s: s.activation(
            st["rstd"][:, lo:hi], st["rstd"][:, lo:hi], AF.Exp, scale=-0.5),
            deps=[m5])
        return [m6]

    tr_prev = {}

    def transpose_to(ps_t, slot, src_bf, src_off, dst_bf, dst_off, deps):
        tr = P.op("tensor", lambda te: te.transpose(
            ps_t[:, slot * 1024:slot * 1024 + 128],
            src_bf[:, src_off:src_off + 128], eye_bf[:, :]),
            deps=list(deps) + [ld_eye, tr_prev.get((id(ps_t), slot))])
        ev = P.op("scalar", lambda s: s.copy(
            dst_bf[:, dst_off:dst_off + 128],
            ps_t[:, slot * 1024:slot * 1024 + 128]), deps=[tr])
        tr_prev[(id(ps_t), slot)] = ev
        return ev

    def rope_ops(dst_bf, dst_off, src_f32, src_off, csA, csB, cs_off, w,
                 deps, t1, t2):
        o1 = P.op("vector", lambda v: v.tensor_mul(
            t1[:, 0:w], src_f32[0:32, src_off:src_off + w],
            csA[0:32, cs_off:cs_off + w]), deps=deps)
        o2 = P.op("vector", lambda v: v.tensor_mul(
            t2[:, 0:w], src_f32[32:64, src_off:src_off + w],
            csA[32:64, cs_off:cs_off + w]), deps=deps)
        o3 = P.op("vector", lambda v: v.tensor_sub(
            dst_bf[0:32, dst_off:dst_off + w], t1[:, 0:w], t2[:, 0:w]),
            deps=[o1, o2])
        o4 = P.op("vector", lambda v: v.tensor_mul(
            t1[:, 0:w], src_f32[32:64, src_off:src_off + w],
            csB[32:64, cs_off:cs_off + w]), deps=[o3])
        o5 = P.op("vector", lambda v: v.tensor_mul(
            t2[:, 0:w], src_f32[0:32, src_off:src_off + w],
            csB[0:32, cs_off:cs_off + w]), deps=[o3])
        o6 = P.op("vector", lambda v: v.tensor_add(
            dst_bf[32:64, dst_off:dst_off + w], t1[:, 0:w], t2[:, 0:w]),
            deps=[o4, o5])
        return o6

    A1 = ExitStack()
    SC["sq_dump"] = sb(A1, "sq_dumpA", [128, D], F32)
    ln_tmp = SC["ln_tmp"] = sb(A1, "ln_tmpA", [128, QLR], F32)
    tm_bf = SC["tm_bf"] = sb(A1, "tm_bfA", [128, D], BF16)
    n1_fm = sb(A1, "n1_fm", [128, 16 * TOK], BF16)
    tr_ps = psb(A1, "tr_ps", 2)

    x_loads = [P.dmaop("sync", lambda s, t=t: s.dma_start(
        out=xres[:, t * D:(t + 1) * D], in_=x_in[t * 128:(t + 1) * 128, :]))
        for t in range(NT)]

    stA = alloc_stats(A1, "stA", NT)
    stQL = alloc_stats(A1, "stQL", NT)
    n1_stats = ln_stats(stA, xres, [(t, t * D) for t in range(NT)], x_loads, D)
    n1_done = []
    prev_apply = None
    for t in range(NT):
        a = P.op("vector", lambda v, t=t, tm_bf=tm_bf: v.tensor_scalar(
            tm_bf[:, 0:D], xres[:, t * D:(t + 1) * D],
            stA["mean"][:, t:t + 1], stA["rstd"][:, t:t + 1],
            ALU.subtract, ALU.mult), deps=n1_stats + [prev_apply])
        last = None
        for ic in range(16):
            last = transpose_to(tr_ps, ic % 2, tm_bf, ic * 128,
                                n1_fm, ic * TOK + t * 128, [a])
        prev_apply = last
        n1_done.append(last)
    n1_all = n1_done[-1]

    ql_bf = sb(A1, "ql_bf", [128, NT * QLR], BF16)
    qlb_bf = sb(A1, "qlb_bf", [128, 2 * 128], BF16)
    bqa_bf = sb(A1, "bqa_bf", [1, QLR], BF16)
    gq_sb = sb(A1, "gq_sb", [128, QLR], BF16)
    bq_sb = sb(A1, "bq_sb", [128, QLR], BF16)
    bkva_sb = sb(A1, "bkva_sb", [128, 4], F32)
    bkvar_sb = sb(A1, "bkvar_sb", [64, 1], F32)
    kv_fm = sb(A1, "kv_fm", [128, 4 * TOK], BF16)
    kvr_f32 = sb(A1, "kvr_f32", [64, 512], F32)
    kvr_bf = sb(A1, "kvr_bf", [64, TOK], BF16)
    cskA_sb = sb(A1, "cskA_sb", [64, 512], F32)
    cskB_sb = sb(A1, "cskB_sb", [64, 512], F32)
    rt1 = rt_sh1
    rt2 = rt_sh2
    ql_ps = psf(A1, "ql_ps", 2)
    kv_ps = psf(A1, "kv_ps", 2)
    wqa_bf = sb(A1, "wqa_bf", [128, 16 * QLR], BF16)
    wkva_bf = sb(A1, "wkva_bf", [128, 16 * (KVLR + DR)], BF16)

    ld_wqa = P.dmaop("gpsimd", lambda g: g.dma_start(
        out=wqa_bf[:, :].rearrange("p (a o) -> p a o", o=QLR), in_=wqaT.rearrange("(a p) o -> p a o", p=128)))
    ld_wkva = P.dmaop("gpsimd", lambda g: g.dma_start(
        out=wkva_bf[:, :].rearrange("p (a o) -> p a o", o=KVLR + DR), in_=wkvaT.rearrange("(a p) o -> p a o", p=128)))
    ld_bqa = P.dmaop("gpsimd", lambda g: g.dma_start(out=bqa_bf[:, :], in_=bqa_row))
    pc_wout = pc_wqkv = pc_wo = pc_w1 = pc_w2 = pc_w3 = None
    ld_gq = P.dmaop("gpsimd", lambda g: g.dma_start(out=gq_sb[:, :], in_=gq_bc))
    ld_bq = P.dmaop("gpsimd", lambda g: g.dma_start(out=bq_sb[:, :], in_=bq_bc))
    ld_bkva = P.dmaop("sync", lambda s: s.dma_start(out=bkva_sb[:, :], in_=bkva_cols))
    ld_bkvar = P.dmaop("sync", lambda s: s.dma_start(out=bkvar_sb[:, :], in_=bkva_rope))

    ql_evacs = []
    ps_user = [None, None]
    for t in range(NT):
        for oc in range(2):
            slot = (t * 2 + oc) % 2
            mm = None
            for ic in range(16):
                deps = [ld_wqa, n1_done[t], ps_user[slot]] if ic == 0 else []
                mm = P.op("tensor", lambda te, t=t, oc=oc, ic=ic, slot=slot,
                          first=(ic == 0): te.matmul(
                    ql_ps[:, slot * 512:slot * 512 + 512],
                    n1_fm[:, ic * TOK + t * 128:ic * TOK + t * 128 + 128],
                    wqa_bf[:, ic * QLR + oc * 512:ic * QLR + oc * 512 + 512],
                    start=first, stop=False), deps=deps)
            mmb = P.op("tensor", lambda te, oc=oc, slot=slot: te.matmul(
                ql_ps[:, slot * 512:slot * 512 + 512], ones_row[:, :],
                bqa_bf[:, oc * 512:oc * 512 + 512],
                start=False, stop=True), deps=[mm, ld_bqa, op_ones])
            ev = P.op("vector", lambda v, t=t, oc=oc, slot=slot: v.tensor_copy(
                ql_bf[:, t * QLR + oc * 512:t * QLR + oc * 512 + 512],
                ql_ps[:, slot * 512:slot * 512 + 512]), deps=[mmb])
            ps_user[slot] = ev
            ql_evacs.append(ev)

    ql_stats = ln_stats(stQL, ql_bf, [(t, t * QLR) for t in range(NT)],
                        [ql_evacs[-1]], QLR)
    prev_tr = None
    ql_done = []
    qlb_user = [None, None]
    gin_ql_dmas = []
    for t in range(NT):
        a1 = P.op("vector", lambda v, t=t, ln_tmp=ln_tmp: v.scalar_tensor_tensor(
            ln_tmp[:, 0:QLR], ql_bf[:, t * QLR:(t + 1) * QLR],
            stQL["mean"][:, t:t + 1], gq_sb[:, :],
            ALU.subtract, ALU.mult), deps=ql_stats + [ld_gq])
        a3 = P.op("vector", lambda v, t=t, ln_tmp=ln_tmp, tm_bf=tm_bf: v.scalar_tensor_tensor(
            tm_bf[:, 0:QLR], ln_tmp[:, 0:QLR],
            stQL["rstd"][:, t:t + 1], bq_sb[:, :],
            ALU.mult, ALU.add), deps=[a1, ld_bq, prev_tr])
        last = None
        for ic in range(8):
            sl8 = ic % 2
            ev = transpose_to(tr_ps, sl8, tm_bf, ic * 128,
                              qlb_bf, sl8 * 128,
                              [a3, qlb_user[sl8]])
            dd = P.dmaop("scalar", lambda sc, ic=ic, t=t, sl8=sl8: sc.dma_start(
                out=gin[ic * 128:(ic + 1) * 128, t * 128:(t + 1) * 128],
                in_=qlb_bf[:, sl8 * 128:sl8 * 128 + 128]), deps=[ev])
            qlb_user[sl8] = dd
            gin_ql_dmas.append(dd)
            last = ev
        prev_tr = last
        ql_done.append(last)

    kv_evacs = []
    ps_user = [None, None]
    rope_ev = []
    rope_k = []
    for cc in range(5):
        width = 64 if cc == 4 else 128
        for c in range(2):
            slot = (cc * 2 + c) % 2
            mm = None
            for ic in range(16):
                deps = [ld_wkva, n1_all, ps_user[slot]] if ic == 0 else []
                mm = P.op("tensor", lambda te, cc=cc, width=width, c=c, ic=ic,
                          slot=slot, first=(ic == 0): te.matmul(
                    kv_ps[0:width, slot * 512:slot * 512 + 512],
                    wkva_bf[:, ic * 576 + cc * 128:ic * 576 + cc * 128 + width],
                    n1_fm[:, ic * TOK + c * 512:ic * TOK + c * 512 + 512],
                    start=first, stop=(ic == 15)), deps=deps)
            if width == 128:
                ev = P.op("scalar", lambda s, cc=cc, c=c, slot=slot: s.activation(
                    kv_fm[:, cc * TOK + c * 512:cc * TOK + c * 512 + 512],
                    kv_ps[:, slot * 512:slot * 512 + 512], AF.Identity,
                    bias=bkva_sb[:, cc:cc + 1]), deps=[mm, ld_bkva])
            else:
                ev = P.op("scalar", lambda s, c=c, slot=slot: s.activation(
                    kvr_f32[:, 0:512],
                    kv_ps[0:64, slot * 512:slot * 512 + 512], AF.Identity,
                    bias=bkvar_sb[:, 0:1]), deps=[mm, ld_bkvar] + rope_k)
                rope_ev.append(ev)
                ld_ck = P.dmaop("sync", lambda s2, c=c: s2.dma_start(
                    out=cskA_sb[:, :], in_=csA_own[:, c * 512:(c + 1) * 512]),
                    deps=rope_k)
                ld_sk = P.dmaop("sync", lambda s2, c=c: s2.dma_start(
                    out=cskB_sb[:, :], in_=csB_own[:, c * 512:(c + 1) * 512]),
                    deps=rope_k)
                rope_k.append(rope_ops(kvr_bf, c * 512, kvr_f32, 0,
                                       cskA_sb, cskB_sb, 0, 512,
                                       [ev, ld_ck, ld_sk], rt1, rt2))
            ps_user[slot] = ev
            kv_evacs.append(ev)


    gin_dmas = list(gin_ql_dmas)
    for cc in range(4):
        gin_dmas.append(P.dmaop("scalar", lambda g, cc=cc: g.dma_start(
            out=gin[QLR + cc * 128:QLR + (cc + 1) * 128, :],
            in_=kv_fm[:, cc * TOK:(cc + 1) * TOK]), deps=[kv_evacs[-1]]))
    gin_dmas.append(P.dmaop("scalar", lambda g: g.dma_start(
        out=gin[QLR + KVLR:QLR + KVLR + DR, :], in_=kvr_bf[:, :]),
        deps=rope_k))
    ag1 = P.op("gpsimd", lambda g: g.collective_compute(
        "AllGather", ALU.bypass, replica_groups=[list(range(NCORE))],
        ins=[gin.opt()], outs=[gout.opt()]), deps=gin_dmas)

    barA1 = P.barrier()
    A1.close()
    brow_bf = sb(top, "brow_bf", [1, D], BF16)

    A3 = ExitStack()
    wqb_bf = sb(A3, "wqb_bf", [128, 8 * 192], BF16)
    wkvb_bf = sb(A3, "wkvb_bf", [128, 4 * 256], BF16)
    bqbn_sb = sb(A3, "bqbn_sb", [128, 2], F32)
    bqbr_sb = sb(A3, "bqbr_sb", [64, 2], F32)
    bkvbn_sb = sb(A3, "bkvbn_sb", [128, 2], F32)
    bkvbv_bf = sb(A3, "bkvbv_bf", [1, 2 * DV], BF16)
    csqA_sb = sb(A3, "csqA_sb", [64, 512], F32)
    csqB_sb = sb(A3, "csqB_sb", [64, 512], F32)
    gq_buf = sb(A3, "gq_buf", [128, 2 * 8 * 512], BF16)
    gkv_buf = sb(A3, "gkv_buf", [128, 2 * 4 * 512], BF16)
    Qn = sb(A3, "Qn", [128, T], BF16)
    Qr = sb(A3, "Qr", [64, T], BF16)
    Kn = sb(A3, "Kn", [128, T], BF16)
    Kr = sb(A3, "Kr", [64, T], BF16)
    Vt = sb(A3, "Vt", [128, (T // 128) * DV], BF16)
    qr_f32 = sb(A3, "qr_f32", [64, 512], F32)
    pt_bf = sb(A3, "pt_bf", [128, 2 * 1024], BF16)
    invd_f = sb(A3, "invd_f", [128, 1024], F32)
    yb_bf = sb(A3, "yb_bf", [128, 2 * 1024], BF16)
    rt1b = rt_sh1
    rt2b = rt_sh2
    mask_bf = sb(A3, "mask_bf", [128, 1920], BF16)
    ones128_bf = sb(A3, "ones128_bf", [128, 128], BF16)

    ps_all = psf(A3, "ps_all", 8)
    ld_mask = P.dmaop("sync", lambda s: s.dma_start(out=mask_bf[:, :], in_=mask_h),
                      deps=barA1)
    op_ones128 = P.op("vector", lambda v: v.memset(ones128_bf[:, :], 1.0))

    ld_b1 = P.dmaop("sync", lambda s: s.dma_start(out=bqbn_sb[:, :], in_=bqb_nope), deps=barA1)
    ld_b2 = P.dmaop("sync", lambda s: s.dma_start(out=bqbr_sb[:, :], in_=bqb_rope), deps=barA1)
    ld_b3 = P.dmaop("sync", lambda s: s.dma_start(out=bkvbn_sb[:, :], in_=bkvb_nope), deps=barA1)
    ld_b4 = P.dmaop("gpsimd", lambda g: g.dma_start(out=bkvbv_bf[:, :], in_=bkvb_v), deps=barA1)

    a2a_dmas = []
    head_bar = barA1
    for hloc in range(2):
        ld_wqb = P.dmaop("gpsimd", lambda g, hloc=hloc: g.dma_start(
            out=wqb_bf[:, :].rearrange("p (a o) -> p a o", o=192),
            in_=wqbT[:, hloc * 192:(hloc + 1) * 192]
                .rearrange("(a p) o -> p a o", p=128)),
            deps=head_bar)
        ld_wkvb = P.dmaop("gpsimd", lambda g, hloc=hloc: g.dma_start(
            out=wkvb_bf[:, :].rearrange("p (a o) -> p a o", o=256),
            in_=wkvbT[:, hloc * 256:(hloc + 1) * 256]
                .rearrange("(a p) o -> p a o", p=128)),
            deps=head_bar)
        slot_user = [None, None]
        proj_last = []
        prev_evr = None
        prev_evn = None
        kn_prev_ev = None
        v_prev_ev = None
        prev_rope = None
        for c in range(16):
            sl = c % 2
            rk, co = c // 2, (c % 2) * 512
            deps_ld = [ag1, slot_user[sl]] + head_bar
            lds = []
            for ic in range(8):
                lds.append(P.dmaop("sync", lambda s, ic=ic, rk=rk, co=co, sl=sl: s.dma_start(
                    out=gq_buf[:, (sl * 8 + ic) * 512:(sl * 8 + ic) * 512 + 512],
                    in_=gout[rk * 1600 + ic * 128:rk * 1600 + (ic + 1) * 128, co:co + 512]),
                    deps=deps_ld))
            for ic in range(4):
                lds.append(P.dmaop("sync", lambda s, ic=ic, rk=rk, co=co, sl=sl: s.dma_start(
                    out=gkv_buf[:, (sl * 4 + ic) * 512:(sl * 4 + ic) * 512 + 512],
                    in_=gout[rk * 1600 + QLR + ic * 128:rk * 1600 + QLR + (ic + 1) * 128, co:co + 512]),
                    deps=deps_ld))
            ld_kr = P.dmaop("sync", lambda s, c=c, rk=rk, co=co: s.dma_start(
                out=Kr[:, c * 512:c * 512 + 512],
                in_=gout[rk * 1600 + QLR + KVLR:rk * 1600 + QLR + KVLR + DR, co:co + 512]),
                deps=deps_ld)
            lds.append(ld_kr)
            ld_cq = P.dmaop("sync", lambda s, c=c: s.dma_start(
                out=csqA_sb[:, :], in_=csA_t[:, c * 512:(c + 1) * 512]),
                deps=[prev_rope] + head_bar)
            ld_sq = P.dmaop("sync", lambda s, c=c: s.dma_start(
                out=csqB_sb[:, :], in_=csB_t[:, c * 512:(c + 1) * 512]),
                deps=[prev_rope] + head_bar)

            mm = None
            for ic in range(8):
                deps = (lds + [ld_wqb, prev_evr] + head_bar) if ic == 0 else []
                mm = P.op("tensor", lambda te, ic=ic, sl=sl, hloc=hloc,
                          first=(ic == 0): te.matmul(
                    ps_all[0:64, 0:512],
                    wqb_bf[:, ic * 192:ic * 192 + 64],
                    gq_buf[:, (sl * 8 + ic) * 512:(sl * 8 + ic) * 512 + 512],
                    start=first, stop=(ic == 7)), deps=deps)
            evr = P.op("scalar", lambda s, hloc=hloc: s.activation(
                qr_f32[:, 0:512], ps_all[0:64, 0:512], AF.Identity,
                bias=bqbr_sb[:, hloc:hloc + 1]), deps=[mm, ld_b2, prev_rope])
            prev_evr = evr
            rp = rope_ops(Qr, c * 512, qr_f32, 0, csqA_sb, csqB_sb, 0,
                          512, [evr, ld_cq, ld_sq], rt1b, rt2b)
            prev_rope = rp
            mm2 = None
            for ic in range(8):
                deps = ([prev_evn] if prev_evn is not None else (lds + [ld_wqb])) if ic == 0 else []
                mm2 = P.op("tensor", lambda te, ic=ic, sl=sl, hloc=hloc,
                           first=(ic == 0): te.matmul(
                    ps_all[:, 512:1024],
                    wqb_bf[:, ic * 192 + 64:ic * 192 + 192],
                    gq_buf[:, (sl * 8 + ic) * 512:(sl * 8 + ic) * 512 + 512],
                    start=first, stop=(ic == 7)), deps=deps)
            evn = P.op("scalar", lambda s, c=c, hloc=hloc: s.activation(
                Qn[:, c * 512:c * 512 + 512], ps_all[:, 512:1024], AF.Identity,
                bias=bqbn_sb[:, hloc:hloc + 1]), deps=[mm2, ld_b1])
            prev_evn = evn
            mm3 = None
            for ic in range(4):
                deps = (lds + [ld_wkvb, kn_prev_ev] + head_bar) if ic == 0 else []
                mm3 = P.op("tensor", lambda te, ic=ic, sl=sl, hloc=hloc,
                           first=(ic == 0): te.matmul(
                    ps_all[:, 1024:1536],
                    wkvb_bf[:, ic * 256:ic * 256 + 128],
                    gkv_buf[:, (sl * 4 + ic) * 512:(sl * 4 + ic) * 512 + 512],
                    start=first, stop=(ic == 3)), deps=deps)
            kn_ev = P.op("scalar", lambda s, c=c, hloc=hloc: s.activation(
                Kn[:, c * 512:c * 512 + 512], ps_all[:, 1024:1536], AF.Identity,
                bias=bkvbn_sb[:, hloc:hloc + 1]), deps=[mm3, ld_b3])
            kn_prev_ev = kn_ev
            v_evs = []
            last_v_mm = None
            for s4 in range(4):
                tt = c * 4 + s4
                mm4 = None
                for ic in range(4):
                    deps = ([v_prev_ev, ld_b4, op_ones] + head_bar) if ic == 0 else []
                    mm4 = P.op("tensor", lambda te, ic=ic, sl=sl, s4=s4, hloc=hloc,
                               first=(ic == 0): te.matmul(
                        ps_all[:, 1536:1664],
                        gkv_buf[:, (sl * 4 + ic) * 512 + s4 * 128:(sl * 4 + ic) * 512 + s4 * 128 + 128],
                        wkvb_bf[:, ic * 256 + 128:ic * 256 + 256],
                        start=first, stop=False), deps=deps)
                mm4b = P.op("tensor", lambda te, hloc=hloc: te.matmul(
                    ps_all[:, 1536:1664], ones_row[:, :],
                    bkvbv_bf[:, hloc * DV:hloc * DV + DV],
                    start=False, stop=True), deps=[mm4])
                last_v_mm = mm4b
                v_ev = P.op("vector", lambda v, tt=tt: v.tensor_copy(
                    Vt[:, tt * DV:tt * DV + DV], ps_all[:, 1536:1664]), deps=[mm4b])
                v_prev_ev = v_ev
                v_evs.append(v_ev)
            slot_user[sl] = [last_v_mm, evn]
            proj_last += [rp, evn, kn_ev, ld_kr] + v_evs

        ps_s_user = [None, None]
        pt_user = [None, None]
        y_user = None
        dn_user = None
        yb_user = [None, None]
        for qc in range(8):
            nkt = min((qc + 1) * 8, 64)
            av_last = None
            dn_last = None
            pend = None
            for kt in range(nkt):
                sl = kt % 2
                bdry = (kt * 128 >= qc * 1024)
                mm_last = []
                for h2 in range(2):
                    co = h2 * 512
                    deps = ([ps_s_user[sl]] + (proj_last if (qc == 0 and kt == 0)
                                               else [])) if h2 == 0 else []
                    smm1 = P.op("tensor", lambda te, kt=kt, qc=qc, sl=sl, co=co: te.matmul(
                        ps_all[:, sl * 1024 + co:sl * 1024 + co + 512],
                        Kn[:, kt * 128:kt * 128 + 128],
                        Qn[:, qc * 1024 + co:qc * 1024 + co + 512],
                        start=True, stop=False), deps=deps)
                    smm2 = P.op("tensor", lambda te, kt=kt, qc=qc, sl=sl, co=co,
                                bdry=bdry: te.matmul(
                        ps_all[:, sl * 1024 + co:sl * 1024 + co + 512],
                        Kr[:, kt * 128:kt * 128 + 128],
                        Qr[:, qc * 1024 + co:qc * 1024 + co + 512],
                        start=False, stop=not bdry), deps=[smm1])
                    mm = smm2
                    if bdry:
                        r0 = 896 - (kt * 128 - qc * 1024) + co
                        mm = P.op("tensor", lambda te, sl=sl, r0=r0, co=co: te.matmul(
                            ps_all[:, sl * 1024 + co:sl * 1024 + co + 512],
                            eye_bf[:, :], mask_bf[:, r0:r0 + 512],
                            start=False, stop=True), deps=[smm2, ld_mask, ld_eye])
                    mm_last.append(mm)
                if pend is not None:
                    dn_last, av_last = pend()
                    pend = None
                ex = P.op("scalar", lambda s, sl=sl: s.activation(
                    pt_bf[:, sl * 1024:sl * 1024 + 1024],
                    ps_all[:, sl * 1024:sl * 1024 + 1024], AF.Exp,
                    scale=1.0 / math.sqrt(float(DQK))),
                    deps=mm_last + [pt_user[sl], const_dep[1.0 / math.sqrt(float(DQK))]])
                ps_s_user[sl] = ex

                def mk_pend(kt=kt, sl=sl, ex=ex, qc=qc, nkt=nkt):
                    def emit():
                        dnl, avl = [], []
                        for h2 in range(2):
                            co = h2 * 512
                            dnl.append(P.op("tensor", lambda te, co=co: te.matmul(
                                ps_all[0:128, 3072 + co:3072 + co + 512],
                                ones128_bf[:, :],
                                pt_bf[:, sl * 1024 + co:sl * 1024 + co + 512],
                                start=(kt == 0), stop=(kt == nkt - 1)),
                                deps=[ex, op_ones128] + ([dn_user] if kt == 0 and h2 == 0 else [])))
                            avl.append(P.op("tensor", lambda te, co=co: te.matmul(
                                ps_all[:, 2048 + co:2048 + co + 512],
                                Vt[:, kt * DV:kt * DV + DV],
                                pt_bf[:, sl * 1024 + co:sl * 1024 + co + 512],
                                start=(kt == 0), stop=(kt == nkt - 1)),
                                deps=[ex] + ([y_user] if kt == 0 and h2 == 0 else [])))
                        pt_user[sl] = dnl + avl
                        return dnl, avl
                    return emit
                pend = mk_pend()
            dn_last, av_last = pend()
            rc = P.op("vector", lambda v: v.reciprocal(
                invd_f[:, :], ps_all[:, 3072:4096]), deps=dn_last)
            dn_user = rc
            ysl = qc % 2
            ymul = P.op("vector", lambda v, ysl=ysl: v.tensor_mul(
                yb_bf[:, ysl * 1024:ysl * 1024 + 1024], ps_all[:, 2048:3072],
                invd_f[:, :]), deps=[rc, yb_user[ysl]] + av_last)
            y_user = ymul
            yd = P.dmaop("scalar", lambda g, qc=qc, hloc=hloc, ysl=ysl: g.dma_start(
                out=ygath[hloc * 128:hloc * 128 + 128, qc * 1024:qc * 1024 + 1024],
                in_=yb_bf[:, ysl * 1024:ysl * 1024 + 1024]), deps=[ymul])
            yb_user[ysl] = yd
            a2a_dmas.append(yd)
        head_bar = P.barrier()

    a2a = P.op("gpsimd", lambda g: g.collective_compute(
        "AllGather", ALU.bypass, replica_groups=[list(range(NCORE))],
        ins=[ygath.opt()], outs=[yfull.opt()]), deps=a2a_dmas)

    if DEBUG:
        P.dmaop("gpsimd", lambda g: g.dma_start(out=dbg_gout, in_=gout), deps=[ag1])
        P.dmaop("gpsimd", lambda g: g.dma_start(out=dbg_y, in_=ygath), deps=a2a_dmas)
        P.dmaop("gpsimd", lambda g: g.dma_start(out=dbg_yf, in_=yfull), deps=[a2a])
    barA3 = P.barrier()
    A3.close()
    SC["sq_dump"] = sqd_sh = sb(top, "sqd_sh", [128, D], F32)

    B1 = ExitStack()
    pk_sb = sb(B1, "pk_sb", [128, 2 * 704], I32)
    scB_sb = sb(B1, "scB_sb", [128, 44], F16)
    scfB_sb = sb(B1, "scfB_sb", [128, 44], F32)
    dq1 = sb(B1, "dq1", [128, 704], I32)
    dq2 = sb(B1, "dq2", [128, 704], I32)
    dq3 = sb(B1, "dq3", [128, 704], I32)
    dq4 = sb(B1, "dq4", [128, 704], I32)
    bitsB = sb(B1, "bitsB", [128, I0], I16)
    denseB = sb(B1, "denseB", [128, I0], BF16)
    trbB = sb(B1, "trbB", [128, 2 * 128], BF16)
    trB_ps = psb(B1, "trB_ps", 2)

    wgq_dmas = []
    wdq_dmas = []
    b1s = {"pk_user": [None, None], "dense_user": None, "sc_user": None,
           "trb_user": [None, None]}

    def dequant_chunk(ld_pk, ld_sc, pksl, wi32, ngr, out_dmas, dst, doff_fn):
        w = wi32
        cast = P.op("vector", lambda v: v.tensor_copy(
            scfB_sb[:, 0:ngr], scB_sb[:, 0:ngr]), deps=[ld_sc])
        b1s["sc_user"] = cast
        last_bits = None
        for sg in range(8):
            sh1 = (6 - 4 * sg, ALU.logical_shift_left) if 4 * sg < 6 else (4 * sg - 6, ALU.logical_shift_right)
            sh2 = (12 - 4 * sg, ALU.logical_shift_left) if sg <= 3 else (4 * sg - 12, ALU.logical_shift_right)
            d1 = P.op("vector", lambda v, sh1=sh1, w=w, pksl=pksl: v.tensor_scalar(
                dq1[:, 0:w], pk_sb[:, pksl * 704:pksl * 704 + w],
                sh1[0], 448, sh1[1], ALU.bitwise_and),
                deps=[ld_pk] if sg == 0 else [])
            d2 = P.op("vector", lambda v, sh2=sh2, w=w, pksl=pksl: v.tensor_scalar(
                dq2[:, 0:w], pk_sb[:, pksl * 704:pksl * 704 + w],
                sh2[0], 32768, sh2[1], ALU.bitwise_and), deps=[])
            d3 = P.op("vector", lambda v, w=w: v.tensor_scalar(
                dq3[:, 0:w], dq1[:, 0:w], 64, 16128, ALU.is_ge, ALU.mult),
                deps=[])
            d4 = P.op("vector", lambda v, w=w: v.tensor_single_scalar(
                dq4[:, 0:w], dq1[:, 0:w], 128, ALU.is_ge), deps=[])
            d4b = P.op("vector", lambda v, w=w: v.tensor_mul(
                dq4[:, 0:w], dq4[:, 0:w], dq1[:, 0:w]), deps=[])
            d5 = P.op("vector", lambda v, w=w: v.tensor_add(
                dq3[:, 0:w], dq3[:, 0:w], dq4[:, 0:w]), deps=[])
            last_bits = P.op("vector", lambda v, sg=sg, w=w: v.tensor_sub(
                bitsB[:, sg:8 * w:8], dq3[:, 0:w], dq2[:, 0:w]), deps=[])
        mul_ops = []
        for g in range(ngr):
            mul_ops.append(P.op("vector", lambda v, g=g: v.tensor_scalar_mul(
                denseB[:, g * 128:(g + 1) * 128],
                bitsB[:, g * 128:(g + 1) * 128].bitcast(BF16),
                scfB_sb[:, g:g + 1]),
                deps=[last_bits, cast] + ([b1s["dense_user"]] if g == 0 else [])))
        last_tr = None
        for ic in range(ngr):
            sl2 = ic % 2
            ev = transpose_to(trB_ps, sl2, denseB, ic * 128,
                              trbB, sl2 * 128, [mul_ops[ic], b1s["trb_user"][sl2]])
            rows, cols = doff_fn(ic)
            dd = P.dmaop("scalar", lambda sc2, rows=rows, cols=cols, sl2=sl2: sc2.dma_start(
                out=dst[rows[0]:rows[1], cols[0]:cols[1]],
                in_=trbB[:, sl2 * 128:sl2 * 128 + 128]), deps=[ev])
            b1s["trb_user"][sl2] = dd
            out_dmas.append(dd)
            last_tr = ev
        b1s["dense_user"] = last_tr
        return last_tr

    def emit_b1_chunk(i):
        if i < 11:
            oc, sl = i, i % 2
            ld_pk = P.dmaop("sync", lambda s2, oc=oc, sl=sl: s2.dma_start(
                out=pk_sb[:, sl * 704:sl * 704 + 256],
                in_=gup_pk[oc * 128:(oc + 1) * 128, :]),
                deps=barA3 + [b1s["pk_user"][sl]])
            ld_sc = P.dmaop("sync", lambda s2, oc=oc: s2.dma_start(
                out=scB_sb[:, 0:16], in_=gup_scT[oc * 128:(oc + 1) * 128, :]),
                deps=barA3 + [b1s["sc_user"]])
            b1s["pk_user"][sl] = dequant_chunk(
                ld_pk, ld_sc, sl, 256, 16, wgq_dmas, wg_gin,
                lambda ic, oc=oc: ((ic * 128, (ic + 1) * 128),
                                   (oc * 128, (oc + 1) * 128)))
        else:
            oc, sl = i - 11, (i - 11) % 2
            ld_pk = P.dmaop("sync", lambda s2, oc=oc, sl=sl: s2.dma_start(
                out=pk_sb[:, sl * 704:sl * 704 + 704],
                in_=dn_pk[oc * 128:(oc + 1) * 128, :]),
                deps=[b1s["pk_user"][sl]])
            ld_sc = P.dmaop("sync", lambda s2, oc=oc: s2.dma_start(
                out=scB_sb[:, 0:44], in_=dn_scT[oc * 128:(oc + 1) * 128, :]),
                deps=[b1s["sc_user"]])
            b1s["pk_user"][sl] = dequant_chunk(
                ld_pk, ld_sc, sl, 704, 44, wdq_dmas, wd_gin,
                lambda ic, oc=oc: ((ic * 128, (ic + 1) * 128),
                                   (oc * 128, (oc + 1) * 128)))

    b1_sched = {0: [], 1: [0, 1], 2: [2, 3], 3: [4, 5], 4: [6, 7],
                5: [8, 9], 6: [10, 11], 7: [12]}

    A5 = ExitStack()
    ln_tmp = sqd_sh
    wout_bf = sb(A5, "wout_bf", [128, 16 * D], BF16)
    ysl_bf = sb(A5, "ysl_bf", [128, 2 * D], BF16)
    attn_tm = sb(A5, "attn_tm", [128, D], F32)
    ps_w = psf(A5, "ps_w", 4)
    stW = alloc_stats(A5, "stW", NT)

    ld_wout = P.dmaop("gpsimd", lambda g: g.dma_start(
        out=wout_bf[:, :].rearrange("p (a o) -> p a o", o=D),
        in_=woutT.rearrange("(a p) o -> p a o", p=128)),
        deps=barA3)
    ld_bout = P.dmaop("gpsimd", lambda g: g.dma_start(out=brow_bf[:, :], in_=bout_row),
                      deps=barA3)
    ysl_user = [None, None]
    wps_user = [None] * 4
    res_done = []
    prev_apply = None
    for t in range(NT):
        sl = t % 2
        def _ldy(s, t=t, sl=sl):
            off = s.partition_id() * TOK + t * 128
            return s.dma_start(
                out=ysl_bf[:, sl * D:(sl + 1) * D].rearrange("p (a c) -> p a c", c=128),
                in_=yfull.rearrange("(a p) c -> p a c", p=128)[:, :, bass.ds(off, 128)])
        ldy = P.dmaop("sync", _ldy, deps=[a2a, ysl_user[sl]] + barA3)
        evs = []
        for oc in range(4):
            mm = None
            for ic in range(16):
                deps = [ldy, ld_wout, wps_user[oc]] if ic == 0 else []
                mm = P.op("tensor", lambda te, oc=oc, ic=ic, sl=sl,
                          first=(ic == 0): te.matmul(
                    ps_w[:, oc * 512:oc * 512 + 512],
                    ysl_bf[:, sl * D + ic * 128:sl * D + ic * 128 + 128],
                    wout_bf[:, ic * D + oc * 512:ic * D + oc * 512 + 512],
                    start=first, stop=False), deps=deps)
            mmb = P.op("tensor", lambda te, oc=oc: te.matmul(
                ps_w[:, oc * 512:oc * 512 + 512], ones_row[:, :],
                brow_bf[:, oc * 512:oc * 512 + 512],
                start=False, stop=True), deps=[mm, ld_bout, op_ones])
            ev = P.op("vector", lambda v, oc=oc: v.tensor_copy(
                attn_tm[:, oc * 512:oc * 512 + 512],
                ps_w[:, oc * 512:oc * 512 + 512]),
                deps=[mmb, prev_apply])
            wps_user[oc] = ev
            evs.append(ev)
        ysl_user[sl] = evs[-1]
        stt = ln_stats(stW, attn_tm, [(t, 0)], evs, D,
                       war_deps=[prev_apply])
        ap1 = P.op("vector", lambda v, t=t, ln_tmp=ln_tmp: v.tensor_scalar(
            ln_tmp[:, 0:D], attn_tm[:, 0:D],
            stW["mean"][:, t:t + 1], stW["rstd"][:, t:t + 1],
            ALU.subtract, ALU.mult), deps=stt)
        ap2 = P.op("vector", lambda v, t=t, ln_tmp=ln_tmp: v.tensor_add(
            xres[:, t * D:(t + 1) * D], xres[:, t * D:(t + 1) * D],
            ln_tmp[:, 0:D]), deps=[ap1])
        prev_apply = ap2
        res_done.append(ap2)
        for j in b1_sched[t]:
            emit_b1_chunk(j)

    ag3a = P.op("gpsimd", lambda g: g.collective_compute(
        "AllGather", ALU.bypass, replica_groups=[list(range(NCORE))],
        ins=[wg_gin.opt()], outs=[wg_gout.opt()]), deps=wgq_dmas)
    ag3b = P.op("gpsimd", lambda g: g.collective_compute(
        "AllGather", ALU.bypass, replica_groups=[list(range(NCORE))],
        ins=[wd_gin.opt()], outs=[wd_gout.opt()]), deps=wdq_dmas)

    barA5 = P.barrier()
    A5.close()
    B1.close()

    if PHASES >= 2:
        barB1 = barA5

        B2 = ExitStack()
        ln_tmpB = sb(B2, "ln_tmpB", [128, D], F32)
        tm_bfB = sb(B2, "tm_bfB", [128, D], BF16)
        n2_fm = sb(B2, "n2_fm", [128, 16 * TOK], BF16)
        h_fm = sb(B2, "h_fm", [128, 44 * 512], BF16)
        wt_bf = sb(B2, "wt_bf", [128, 2 * 32 * 128], BF16)
        wdt_bf = sb(B2, "wdt_bf", [128, 2 * 512], BF16)
        sil_bf = sb(B2, "sil_bf", [128, 512], BF16)
        ffn_bnc = sb(B2, "ffn_bnc", [128, 2 * 512], F32)
        ffn_t = sb(B2, "ffn_t", [128, D], F32)
        sq4 = sb(B2, "sq4", [128, 16], F32)
        stB = alloc_stats(B2, "stB", NT)
        stB2 = alloc_stats(B2, "stB2", NT)
        B2t = ExitStack()
        trB2_ps = psb(B2t, "trB2_ps", 2)

        n2_stats = ln_stats(stB, xres, [(t, t * D) for t in range(NT)],
                            barB1, D)
        n2_done = []
        prev_applyB = None
        for t in range(NT):
            a = P.op("vector", lambda v, t=t, tm_bfB=tm_bfB: v.tensor_scalar(
                tm_bfB[:, 0:D], xres[:, t * D:(t + 1) * D],
                stB["mean"][:, t:t + 1], stB["rstd"][:, t:t + 1],
                ALU.subtract, ALU.mult), deps=n2_stats + [prev_applyB])
            last = None
            for ic in range(16):
                last = transpose_to(trB2_ps, ic % 2, tm_bfB, ic * 128,
                                    n2_fm, ic * TOK + t * 128, [a])
            prev_applyB = last
            n2_done.append(last)
        n2_all = n2_done[-1]
        B2t.close()
        ps_g = psf(B2, "ps_g", 2)
        ps_u = psf(B2, "ps_u", 2)
        ps_d = psf(B2, "ps_d", 4)

        wt_user = [None, None]
        sil_user = None
        gps_user = [None, None]
        ups_user = [None, None]
        h_user = None
        bnc_user = [None, None]
        rd_user = [None] * 4
        prev_sq = None
        wdt_user = [None, None]
        dps_user = [None] * 4
        res_doneB = []
        prev_applyB2 = None
        for tc in range(2):
            wr_dmas = [[] for _ in range(4)]
            h_ops = []
            for ocp in range(44):
                wsl = ocp % 2
                lds = []
                for half, ch in ((0, ocp), (1, 44 + ocp)):
                    ro, colr = ch // 11, (ch % 11) * 128
                    lds.append(P.dmaop("sync" if half == 0 else "scalar",
                                       lambda s2, ro=ro, colr=colr, half=half, wsl=wsl: s2.dma_start(
                        out=wt_bf[:, (wsl * 32 + half * 16) * 128:(wsl * 32 + half * 16 + 16) * 128]
                            .rearrange("p (a c) -> p a c", c=128),
                        in_=wg_gout[ro * D:(ro + 1) * D, colr:colr + 128]
                            .rearrange("(a p) c -> p a c", p=128)),
                        deps=[ag3a, wt_user[wsl]]))
                gsl = ocp % 2
                mmg = None
                mmu = None
                for ic in range(16):
                    depg = (lds + [n2_all, gps_user[gsl]]) if ic == 0 else []
                    mmg = P.op("tensor", lambda te, ic=ic, tc=tc, wsl=wsl, gsl=gsl,
                               first=(ic == 0): te.matmul(
                        ps_g[:, gsl * 512:gsl * 512 + 512],
                        wt_bf[:, (wsl * 32 + ic) * 128:(wsl * 32 + ic) * 128 + 128],
                        n2_fm[:, ic * TOK + tc * 512:ic * TOK + tc * 512 + 512],
                        start=first, stop=(ic == 15)), deps=depg)
                for ic in range(16):
                    depu = [ups_user[gsl]] if ic == 0 else []
                    mmu = P.op("tensor", lambda te, ic=ic, tc=tc, wsl=wsl, gsl=gsl,
                               first=(ic == 0): te.matmul(
                        ps_u[:, gsl * 512:gsl * 512 + 512],
                        wt_bf[:, (wsl * 32 + 16 + ic) * 128:(wsl * 32 + 16 + ic) * 128 + 128],
                        n2_fm[:, ic * TOK + tc * 512:ic * TOK + tc * 512 + 512],
                        start=first, stop=(ic == 15)), deps=depu)
                sil = P.op("scalar", lambda s2, gsl=gsl: s2.activation(
                    sil_bf[:, :], ps_g[:, gsl * 512:gsl * 512 + 512], AF.Silu),
                    deps=[mmg, sil_user])
                gps_user[gsl] = sil
                hm = P.op("vector", lambda v, ocp=ocp, gsl=gsl: v.tensor_mul(
                    h_fm[:, ocp * 512:(ocp + 1) * 512], sil_bf[:, :],
                    ps_u[:, gsl * 512:gsl * 512 + 512]), deps=[sil, mmu, h_user])
                ups_user[gsl] = hm
                sil_user = hm
                wt_user[wsl] = hm
                h_ops.append(hm)
            ev_d = []
            for od in range(4):
                last_mm = {}
                for ic in range(44):
                    dsl = ic % 2
                    ldd = []
                    for hh in range(2):
                        ro2 = od * 2 + hh
                        ldd.append(P.dmaop("sync" if hh == 0 else "scalar",
                                           lambda s2, ro2=ro2, ic=ic, dsl=dsl, hh=hh: s2.dma_start(
                            out=wdt_bf[:, dsl * 512 + hh * 256:dsl * 512 + hh * 256 + 256],
                            in_=wd_gout[ro2 * I0 + ic * 128:ro2 * I0 + (ic + 1) * 128, :]),
                            deps=[ag3b, wdt_user[dsl]]))
                    for t128 in range(4):
                        dep = (ldd + h_ops + ([dps_user[t128]] if ic == 0 else [])) if (ic == 0 or t128 == 0) else []
                        if ic > 0 and t128 == 0:
                            dep = ldd
                        last_mm[t128] = P.op("tensor", lambda te, ic=ic, t128=t128, od=od, dsl=dsl,
                                             first=(ic == 0): te.matmul(
                            ps_d[:, t128 * 512:t128 * 512 + 512],
                            h_fm[:, ic * 512 + t128 * 128:ic * 512 + t128 * 128 + 128],
                            wdt_bf[:, dsl * 512:dsl * 512 + 512],
                            start=first, stop=(ic == 43)), deps=dep)
                    wdt_user[dsl] = last_mm[3]
                for t128 in range(4):
                    bsl = (od * 4 + t128) % 2
                    ev = P.op("vector", lambda v, t128=t128, bsl=bsl: v.tensor_copy(
                        ffn_bnc[:, bsl * 512:bsl * 512 + 512],
                        ps_d[:, t128 * 512:t128 * 512 + 512]),
                        deps=[last_mm[t128], bnc_user[bsl]])
                    wd2 = P.dmaop("scalar", lambda s2, t128=t128, od=od, bsl=bsl: s2.dma_start(
                        out=ffn_dram[t128 * 128:(t128 + 1) * 128, od * 512:od * 512 + 512],
                        in_=ffn_bnc[:, bsl * 512:bsl * 512 + 512]),
                        deps=[ev, rd_user[t128]])
                    bnc_user[bsl] = wd2
                    dps_user[t128] = ev
                    wr_dmas[t128].append(wd2)
                    ev_d.append(wd2)
            h_user = ev_d[-1]
            for t128 in range(4):
                t = tc * 4 + t128
                rdb = P.dmaop("sync", lambda s2, t128=t128: s2.dma_start(
                    out=ffn_t[:, :], in_=ffn_dram[t128 * 128:(t128 + 1) * 128, :]),
                    deps=wr_dmas[t128] + [prev_applyB2, prev_sq])
                rd_user[t128] = rdb
                rd_done = rdb
                sq = P.op("scalar", lambda s2, t128=t128: s2.activation(
                    SC["sq_dump"][:, 0:D], ffn_t[:, :],
                    AF.Square, accum_out=sq4[:, t128:t128 + 1]), deps=[rdb])
                prev_sq = sq
                c1 = P.op("vector", lambda v, t128=t128: v.tensor_scalar_mul(
                    sq4[:, 4 + t128:5 + t128], sq4[:, t128:t128 + 1], 1.0 / D),
                    deps=[sq])
                c2 = P.op("scalar", lambda s2, t128=t128: s2.activation(
                    sq4[:, 8 + t128:9 + t128], sq4[:, 4 + t128:5 + t128],
                    AF.Ln, bias=1e-6), deps=[c1, const_dep[1e-6]])
                c3 = P.op("scalar", lambda s2, t128=t128: s2.activation(
                    sq4[:, 12 + t128:13 + t128], sq4[:, 8 + t128:9 + t128],
                    AF.Exp, scale=-0.5), deps=[c2])
                y2 = P.op("vector", lambda v, t128=t128, ln_tmpB=ln_tmpB: v.tensor_scalar_mul(
                    ln_tmpB[:, 0:D], ffn_t[:, :],
                    sq4[:, 12 + t128:13 + t128]), deps=[c3, prev_applyB2, rdb])
                stt = ln_stats(stB2, ln_tmpB, [(t, 0)], [y2], D,
                               war_deps=[prev_applyB2])
                sqd = SC["sq_dump"]
                ap1 = P.op("vector", lambda v, t=t, ln_tmpB=ln_tmpB, sqd=sqd: v.tensor_scalar(
                    sqd[:, 0:D], ln_tmpB[:, 0:D],
                    stB2["mean"][:, t:t + 1], stB2["rstd"][:, t:t + 1],
                    ALU.subtract, ALU.mult), deps=stt)
                ap2 = P.op("vector", lambda v, t=t, sqd=sqd: v.tensor_add(
                    xres[:, t * D:(t + 1) * D], xres[:, t * D:(t + 1) * D],
                    sqd[:, 0:D]), deps=[ap1])
                prev_applyB2 = ap2
                res_doneB.append(ap2)
        res_done = res_doneB
        if DEBUG:
            P.dmaop("gpsimd", lambda g: g.dma_start(out=dbg_wg, in_=wg_gout), deps=[ag3a])
            P.dmaop("gpsimd", lambda g: g.dma_start(out=dbg_wd, in_=wd_gout), deps=[ag3b])
            P.dmaop("gpsimd", lambda g: g.dma_start(out=dbg_ffn, in_=ffn_dram),
                    deps=res_doneB)
        barB = P.barrier()
        B2.close()
        barA5 = barB

    if PHASES >= 3:
        barB2 = res_done[-1]
        C0 = ExitStack()
        Q_fm = sb(C0, "Q_fm", [128, 16 * TOK], BF16)
        stC = alloc_stats(C0, "stC", NT)
        stC2 = alloc_stats(C0, "stC2", NT)

        C1 = ExitStack()
        tm_bfC = sb(C1, "tm_bfC", [128, D], BF16)
        n3_fm = sb(C1, "n3_fm", [128, 16 * TOK], BF16)
        wq_bf = sb(C1, "wq_bf", [128, 2 * 16 * 256], BF16)
        kbnc = sb(C1, "kbnc", [128, 2 * 512], BF16)
        vbnc = sb(C1, "vbnc", [128, 2 * 512], BF16)
        trC_ps = psb(C1, "trC_ps", 2)
        ps_qkv = psf(C1, "ps_qkv", 2)

        n3_stats = ln_stats(stC, xres, [(t, t * D) for t in range(NT)],
                            [barB2] + barA5, D)
        n3_done = []
        prev_applyC = None
        for t in range(NT):
            a = P.op("vector", lambda v, t=t, tm_bfC=tm_bfC: v.tensor_scalar(
                tm_bfC[:, 0:D], xres[:, t * D:(t + 1) * D],
                stC["mean"][:, t:t + 1], stC["rstd"][:, t:t + 1],
                ALU.subtract, ALU.mult), deps=n3_stats + [prev_applyC])
            last = None
            for ic in range(16):
                last = transpose_to(trC_ps, ic % 2, tm_bfC, ic * 128,
                                    n3_fm, ic * TOK + t * 128, [a])
            prev_applyC = last
            n3_done.append(last)
        n3_all = n3_done[-1]

        kin_dmas = []
        vdram_dmas = []
        vbnc_user = [None, None]
        wqsl_user = [None, None]
        qkv_ps_user = [None, None]
        kbnc_user = [None, None]
        for pi, part in enumerate((1, 0, 2)):
            for oc2 in range(8):
                wsl = (pi * 8 + oc2) % 2
                ldw = P.dmaop("gpsimd", lambda g, part=part, oc2=oc2, wsl=wsl: g.dma_start(
                    out=wq_bf[:, wsl * 4096:(wsl + 1) * 4096]
                        .rearrange("p (a c) -> p a c", c=256),
                    in_=w_qkv[:, part * D + oc2 * 256:part * D + oc2 * 256 + 256]
                        .rearrange("(a p) c -> p a c", p=128)),
                    deps=[wqsl_user[wsl]] + barA5)
                for sub in range(2):
                  oc = oc2 * 2 + sub
                  for tc in range(2):
                    psl = (oc * 2 + tc) % 2
                    mm = None
                    for ic in range(16):
                        dep = [ldw, n3_all, qkv_ps_user[psl]] if ic == 0 else []
                        mm = P.op("tensor", lambda te, ic=ic, tc=tc, wsl=wsl, psl=psl,
                                  sub=sub, first=(ic == 0): te.matmul(
                            ps_qkv[:, psl * 512:psl * 512 + 512],
                            wq_bf[:, wsl * 4096 + ic * 256 + sub * 128:wsl * 4096 + ic * 256 + sub * 128 + 128],
                            n3_fm[:, ic * TOK + tc * 512:ic * TOK + tc * 512 + 512],
                            start=first, stop=(ic == 15)), deps=dep)
                    if part == 1:
                        ksl = (oc * 2 + tc) % 2
                        ev = P.op("vector", lambda v, psl=psl, ksl=ksl: v.tensor_copy(
                            kbnc[:, ksl * 512:ksl * 512 + 512],
                            ps_qkv[:, psl * 512:psl * 512 + 512]),
                            deps=[mm, kbnc_user[ksl]])
                        kd = P.dmaop("scalar", lambda s2, oc=oc, tc=tc, ksl=ksl: s2.dma_start(
                            out=kin[oc * 128:(oc + 1) * 128, tc * 512:tc * 512 + 512],
                            in_=kbnc[:, ksl * 512:ksl * 512 + 512]), deps=[ev])
                        kbnc_user[ksl] = kd
                        kin_dmas.append(kd)
                        qkv_ps_user[psl] = ev
                    elif part == 0:
                        ev = P.op("vector", lambda v, oc=oc, tc=tc, psl=psl: v.tensor_copy(
                            Q_fm[:, oc * TOK + tc * 512:oc * TOK + tc * 512 + 512],
                            ps_qkv[:, psl * 512:psl * 512 + 512]), deps=[mm])
                        qkv_ps_user[psl] = ev
                    else:
                        vsl = (oc * 2 + tc) % 2
                        ev = P.op("vector", lambda v, psl=psl, vsl=vsl: v.tensor_copy(
                            vbnc[:, vsl * 512:vsl * 512 + 512],
                            ps_qkv[:, psl * 512:psl * 512 + 512]),
                            deps=[mm, vbnc_user[vsl]])
                        vd = P.dmaop("scalar", lambda s2, oc=oc, tc=tc, vsl=vsl: s2.dma_start(
                            out=vdram[oc * 128:(oc + 1) * 128, tc * 512:tc * 512 + 512],
                            in_=vbnc[:, vsl * 512:vsl * 512 + 512]), deps=[ev])
                        vbnc_user[vsl] = vd
                        vdram_dmas.append(vd)
                        qkv_ps_user[psl] = ev
                    wqsl_user[wsl] = mm

        ag4 = P.op("gpsimd", lambda g: g.collective_compute(
            "AllGather", ALU.bypass, replica_groups=[list(range(NCORE))],
            ins=[kin.opt()], outs=[kfull.opt()]), deps=kin_dmas)

        pc_w1 = pc_w2 = pc_w3 = None
        if PHASES >= 4:
            pc_w1 = P.dmaop("gpsimd", lambda g: g.dma_start(out=w1_bfd, in_=w1T),
                            deps=barA5)
            pc_w2 = P.dmaop("gpsimd", lambda g: g.dma_start(out=w2_bfd, in_=w2T),
                            deps=barA5)
            pc_w3 = P.dmaop("gpsimd", lambda g: g.dma_start(out=w3_bfd, in_=w3T),
                            deps=barA5)

        barC1 = P.barrier()
        C1.close()

        C2 = ExitStack()
        K_h = sb(C2, "K_h", [128, 2 * T], BF16)
        P_bf = sb(C2, "P_bf", [128, 2 * T], BF16)
        P_acc = sb(C2, "P_acc", [128, T], BF16)
        dacc = sb(C2, "dacc", [128, 16], F32)
        dnm = sb(C2, "dnm", [128, 4], F32)
        cs_bnc = sb(C2, "cs_bnc", [1, 2 * 512], F32)
        ps_sC = psf(C2, "ps_sC", 4)
        ps_cs = C2.enter_context(nc.psum_tensor("ps_cs", [1, 1024], F32))

        rs_dmas = []
        last_csdma = None
        kh_user = [None, None]
        sC_user = [None, None]
        p_user = [None, None]
        stt_user = None
        csps_user = [None, None]
        for h in range(H0):
            khsl = h % 2
            kh_lds = []
            for ro in range(NCORE):
                kh_lds.append(P.dmaop("sync", lambda s2, ro=ro, h=h, khsl=khsl: s2.dma_start(
                    out=K_h[:, khsl * T + ro * TOK:khsl * T + (ro + 1) * TOK],
                    in_=kfull[ro * D + h * 128:ro * D + (h + 1) * 128, :]),
                    deps=[ag4, kh_user[khsl]]))
            last_stt = None
            for qt in range(NT):
                psl = qt % 2
                dsl = (qt % 2) * 8
                ex_user = None
                for kc2 in range(8):
                    sl = kc2 % 2
                    dep = (kh_lds + [sC_user[sl]]) if kc2 <= 1 else [sC_user[sl]]
                    mma = P.op("tensor", lambda te, h=h, qt=qt, kc2=kc2, sl=sl, khsl=khsl: te.matmul(
                        ps_sC[:, sl * 1024:sl * 1024 + 512],
                        Q_fm[:, h * TOK + qt * 128:h * TOK + qt * 128 + 128],
                        K_h[:, khsl * T + kc2 * 1024:khsl * T + kc2 * 1024 + 512],
                        start=True, stop=True), deps=dep)
                    mmb = P.op("tensor", lambda te, h=h, qt=qt, kc2=kc2, sl=sl, khsl=khsl: te.matmul(
                        ps_sC[:, sl * 1024 + 512:sl * 1024 + 1024],
                        Q_fm[:, h * TOK + qt * 128:h * TOK + qt * 128 + 128],
                        K_h[:, khsl * T + kc2 * 1024 + 512:khsl * T + kc2 * 1024 + 1024],
                        start=True, stop=True), deps=[])
                    ex = P.op("scalar", lambda s2, kc2=kc2, sl=sl, psl=psl, dsl=dsl: s2.activation(
                        P_bf[:, psl * T + kc2 * 1024:psl * T + kc2 * 1024 + 1024],
                        ps_sC[:, sl * 1024:sl * 1024 + 1024], AF.Exp,
                        scale=1.0 / math.sqrt(128.0),
                        accum_out=dacc[:, dsl + kc2:dsl + kc2 + 1]),
                        deps=[mmb, const_dep[1.0 / math.sqrt(128.0)]]
                             + ([p_user[psl]] if kc2 <= 1 else []))
                    sC_user[sl] = ex
                    ex_user = ex
                rd = P.op("vector", lambda v, dsl=dsl: v.reduce_sum(
                    dnm[:, 0:1], dacc[:, dsl:dsl + 8], axis=AX.X), deps=[ex_user])
                rc = P.op("vector", lambda v: v.reciprocal(
                    dnm[:, 1:2], dnm[:, 0:1]), deps=[rd])
                if qt == 0:
                    stt = P.op("vector", lambda v, psl=psl: v.tensor_scalar_mul(
                        P_acc[:, :], P_bf[:, psl * T:psl * T + T], dnm[:, 1:2]),
                        deps=[rc, last_csdma])
                else:
                    stt = P.op("vector", lambda v, psl=psl: v.scalar_tensor_tensor(
                        P_acc[:, :], P_bf[:, psl * T:psl * T + T], dnm[:, 1:2],
                        P_acc[:, :], ALU.mult, ALU.add), deps=[rc, stt_user])
                p_user[psl] = stt
                stt_user = stt
                last_stt = stt
            for kc in range(16):
                csl = kc % 2
                csm = P.op("tensor", lambda te, kc=kc, csl=csl: te.matmul(
                    ps_cs[:, csl * 512:csl * 512 + 512], ones_col[:, :],
                    P_acc[:, kc * 512:kc * 512 + 512],
                    start=True, stop=True),
                    deps=[last_stt, op_ones_col, csps_user[csl]])
                cse = P.op("scalar", lambda s2, csl=csl: s2.copy(
                    cs_bnc[:, csl * 512:csl * 512 + 512],
                    ps_cs[:, csl * 512:csl * 512 + 512]), deps=[csm])
                csps_user[csl] = cse
                csd = P.dmaop("scalar", lambda s2, h=h, kc=kc, csl=csl: s2.dma_start(
                    out=rs_in[(kc // 2) * 16 + h:(kc // 2) * 16 + h + 1,
                              (kc % 2) * 512:(kc % 2) * 512 + 512],
                    in_=cs_bnc[:, csl * 512:csl * 512 + 512]), deps=[cse])
                rs_dmas.append(csd)
                last_csdma = csd
            kh_user[khsl] = ex_user

        rs = P.op("gpsimd", lambda g: g.collective_compute(
            "ReduceScatter", ALU.add, replica_groups=[list(range(NCORE))],
            ins=[rs_in.opt()], outs=[rs_out.opt()]), deps=rs_dmas)

        barC2 = P.barrier()
        C2.close()

        C3 = ExitStack()
        V_rd = sb(C3, "V_rd", [128, 16 * TOK], BF16)
        cs1 = sb(C3, "cs1", [1, TOK], F32)
        cs1b = sb(C3, "cs1b", [1, TOK], BF16)
        csb = sb(C3, "csb", [128, TOK], BF16)
        wo_bf = sb(C3, "wo_bf", [128, 2 * 16 * 128], BF16)
        attn1_fm = sb(C3, "attn1_fm", [128, 16 * TOK], BF16)
        attn_tmC = sb(C3, "attn_tmC", [128, D], BF16)
        trC2_ps = psb(C3, "trC2_ps", 2)
        ps_bc = psf(C3, "ps_bc", 1)
        ps_wo = psf(C3, "ps_wo", 2)

        ld_v = P.dmaop("sync", lambda s2: s2.dma_start(
            out=V_rd[:, :].rearrange("p (a c) -> p a c", c=TOK),
            in_=vdram.rearrange("(a p) c -> p a c", p=128)),
            deps=vdram_dmas + barC2)
        cs1_user = None
        bc_userC = None
        vs_done = []
        for h in range(H0):
            ld1 = P.dmaop("sync", lambda s2, h=h: s2.dma_start(
                out=cs1[:, :], in_=rs_out[h:h + 1, :]), deps=[rs, cs1_user])
            cp1 = P.op("vector", lambda v: v.tensor_copy(cs1b[:, :], cs1[:, :]),
                       deps=[ld1, cs1_user])
            bch = []
            for half in range(2):
                bcm = P.op("tensor", lambda te, half=half: te.matmul(
                    ps_bc[:, 0:512], ones_row[:, :],
                    cs1b[:, half * 512:half * 512 + 512],
                    start=True, stop=True), deps=[cp1, op_ones, bc_userC] + barC2)
                bce = P.op("vector", lambda v, half=half: v.tensor_copy(
                    csb[:, half * 512:half * 512 + 512], ps_bc[:, 0:512]),
                    deps=[bcm])
                bc_userC = bce
                bch.append(bce)
            vsc = P.op("vector", lambda v, h=h: v.tensor_mul(
                V_rd[:, h * TOK:(h + 1) * TOK], V_rd[:, h * TOK:(h + 1) * TOK],
                csb[:, :]), deps=bch + [ld_v])
            cs1_user = vsc
            vs_done.append(vsc)

        woldu = [None, None]
        wops_user = [None, None]
        a1fm_done = []
        for oc in range(16):
            wsl = oc % 2
            ldwo = P.dmaop("gpsimd", lambda g, oc=oc, wsl=wsl: g.dma_start(
                out=wo_bf[:, wsl * 2048:(wsl + 1) * 2048]
                    .rearrange("p (a c) -> p a c", c=128),
                in_=w_o[:, oc * 128:(oc + 1) * 128]
                    .rearrange("(a p) c -> p a c", p=128)),
                deps=[woldu[wsl]] + barC2)
            for tc in range(2):
                psl = (oc * 2 + tc) % 2
                mm = None
                for hh in range(16):
                    dep = ([ldwo, wops_user[psl]] + vs_done) if hh == 0 else []
                    mm = P.op("tensor", lambda te, hh=hh, tc=tc, wsl=wsl, psl=psl,
                              first=(hh == 0): te.matmul(
                        ps_wo[:, psl * 512:psl * 512 + 512],
                        wo_bf[:, wsl * 2048 + hh * 128:wsl * 2048 + hh * 128 + 128],
                        V_rd[:, hh * TOK + tc * 512:hh * TOK + tc * 512 + 512],
                        start=first, stop=(hh == 15)), deps=dep)
                ev = P.op("vector", lambda v, oc=oc, tc=tc, psl=psl: v.tensor_copy(
                    attn1_fm[:, oc * TOK + tc * 512:oc * TOK + tc * 512 + 512],
                    ps_wo[:, psl * 512:psl * 512 + 512]), deps=[mm])
                wops_user[psl] = ev
                woldu[wsl] = mm
                a1fm_done.append(ev)

        res_doneC = []
        prev_applyC2 = None
        for t in range(NT):
            trs = []
            for ic in range(16):
                last = transpose_to(trC2_ps, ic % 2, attn1_fm, ic * TOK + t * 128,
                                    attn_tmC, ic * 128,
                                    [a1fm_done[-1], prev_applyC2])
                trs.append(last)
            stt2 = ln_stats(stC2, attn_tmC, [(t, 0)], trs, D,
                            war_deps=[prev_applyC2])
            sqd = SC["sq_dump"]
            ap1 = P.op("vector", lambda v, t=t, sqd=sqd: v.tensor_scalar(
                sqd[:, 0:D], attn_tmC[:, 0:D],
                stC2["mean"][:, t:t + 1], stC2["rstd"][:, t:t + 1],
                ALU.subtract, ALU.mult), deps=stt2)
            ap2 = P.op("vector", lambda v, t=t, sqd=sqd: v.tensor_add(
                xres[:, t * D:(t + 1) * D], xres[:, t * D:(t + 1) * D],
                sqd[:, 0:D]), deps=[ap1])
            prev_applyC2 = ap2
            res_doneC.append(ap2)
        res_done = res_doneC
        barC = P.barrier()
        C3.close()
        C0.close()
        barA5 = barC

    if PHASES >= 4:
        barC_ = res_done[-1]
        DD = ExitStack()
        n4_fm = sb(DD, "n4_fm", [128, 16 * 512], BF16)
        h1_fm = sb(DD, "h1_fm", [128, 32 * 512], BF16)
        h2_fm = sb(DD, "h2_fm", [128, 32 * 512], BF16)
        w1_bf = sb(DD, "w1_bf", [128, 2 * 16 * 128], BF16)
        w2_bf = sb(DD, "w2_bf", [128, 2 * 32 * 128], BF16)
        w3_bf = sb(DD, "w3_bf", [128, 2 * 512], BF16)
        b1_sb = sb(DD, "b1_sb", [128, 32], F32)
        b2_sb = sb(DD, "b2_sb", [128, 32], F32)
        stD = alloc_stats(DD, "stD", NT)
        stD2 = alloc_stats(DD, "stD2", NT)
        Dt = ExitStack()
        tm_bfD = sb(Dt, "tm_bfD", [128, D], BF16)
        trD_ps = psb(Dt, "trD_ps", 2)

        ld_b1 = P.dmaop("sync", lambda s2: s2.dma_start(out=b1_sb[:, :], in_=b1_cols),
                        deps=barA5)
        ld_b2D = P.dmaop("sync", lambda s2: s2.dma_start(out=b2_sb[:, :], in_=b2_cols),
                         deps=barA5)
        ld_b3 = P.dmaop("gpsimd", lambda g: g.dma_start(out=brow_bf[:, :], in_=b3_row),
                        deps=barA5)

        n4b = sb(Dt, "n4b", [128, 2 * 128], BF16)
        n4_stats = ln_stats(stD, xres, [(t, t * D) for t in range(NT)],
                            [barC_] + barA5, D)
        n4_dmas = []
        n4b_user = [None, None]
        prev_applyD = None
        for t in range(NT):
            a = P.op("vector", lambda v, t=t, tm_bfD=tm_bfD: v.tensor_scalar(
                tm_bfD[:, 0:D], xres[:, t * D:(t + 1) * D],
                stD["mean"][:, t:t + 1], stD["rstd"][:, t:t + 1],
                ALU.subtract, ALU.mult), deps=n4_stats + [prev_applyD])
            last = None
            for ic in range(16):
                sl9 = ic % 2
                ev = transpose_to(trD_ps, sl9, tm_bfD, ic * 128,
                                  n4b, sl9 * 128, [a, n4b_user[sl9]])
                dd = P.dmaop("scalar", lambda s2, ic=ic, t=t, sl9=sl9: s2.dma_start(
                    out=n4_dram[ic * 128:(ic + 1) * 128, t * 128:(t + 1) * 128],
                    in_=n4b[:, sl9 * 128:sl9 * 128 + 128]), deps=[ev])
                n4b_user[sl9] = dd
                n4_dmas.append(dd)
                last = ev
            prev_applyD = last
        Dt.close()
        ffn_tD = sb(DD, "ffn_tD", [128, D], F32)
        ffn_bncD = sb(DD, "ffn_bncD", [128, 2 * 512], F32)
        ps_h = psf(DD, "ps_h", 2)
        ps_d4 = psf(DD, "ps_d4", 4)

        w1_user = [None, None]
        w2_user = [None, None]
        w3_user = [None, None]
        hps_user = [None, None]
        d4ps_user = [None] * 4
        h1_user = None
        h2_user = None
        bncD_user = [None, None]
        rdD_user = None
        prev_sqD = None
        prev_applyD2 = None
        res_doneD = []
        n4ld_user = None
        for tc in range(2):
            wrD = [[] for _ in range(4)]
            ldn4 = P.dmaop("sync", lambda s2, tc=tc: s2.dma_start(
                out=n4_fm[:, :].rearrange("p (a c) -> p a c", c=512),
                in_=n4_dram[:, tc * 512:(tc + 1) * 512]
                    .rearrange("(a p) c -> p a c", p=128)),
                deps=n4_dmas + [n4ld_user])
            h1_ops = []
            for oc in range(32):
                wsl1 = oc % 2
                ldw = P.dmaop("sync", lambda g, oc=oc, wsl1=wsl1: g.dma_start(
                    out=w1_bf[:, wsl1 * 2048:(wsl1 + 1) * 2048]
                        .rearrange("p (a c) -> p a c", c=128),
                    in_=w1_bfd[:, oc * 128:(oc + 1) * 128]
                        .rearrange("(a p) c -> p a c", p=128)),
                    deps=[w1_user[wsl1], pc_w1] + barA5)
                psl = oc % 2
                mm = None
                for ic in range(16):
                    dep = [ldw, ldn4, hps_user[psl]] if ic == 0 else []
                    mm = P.op("tensor", lambda te, ic=ic, tc=tc, psl=psl, wsl1=wsl1,
                              first=(ic == 0): te.matmul(
                        ps_h[:, psl * 512:psl * 512 + 512],
                        w1_bf[:, wsl1 * 2048 + ic * 128:wsl1 * 2048 + ic * 128 + 128],
                        n4_fm[:, ic * 512:ic * 512 + 512],
                        start=first, stop=(ic == 15)), deps=dep)
                ev = P.op("scalar", lambda s2, oc=oc, psl=psl: s2.activation(
                    h1_fm[:, oc * 512:(oc + 1) * 512],
                    ps_h[:, psl * 512:psl * 512 + 512], AF.Relu,
                    bias=b1_sb[:, oc:oc + 1]),
                    deps=[mm, ld_b1, h1_user])
                hps_user[psl] = ev
                w1_user[wsl1] = mm
                h1_ops.append(ev)
            h2_ops = []
            for oc in range(32):
                wsl2 = oc % 2
                ldw = P.dmaop("scalar", lambda g, oc=oc, wsl2=wsl2: g.dma_start(
                    out=w2_bf[:, wsl2 * 4096:(wsl2 + 1) * 4096]
                        .rearrange("p (a c) -> p a c", c=128),
                    in_=w2_bfd[:, oc * 128:(oc + 1) * 128]
                        .rearrange("(a p) c -> p a c", p=128)),
                    deps=[w2_user[wsl2], pc_w2] + barA5)
                psl = oc % 2
                mm = None
                for ic in range(32):
                    dep = ([ldw, hps_user[psl]] + h1_ops) if ic == 0 else []
                    mm = P.op("tensor", lambda te, ic=ic, psl=psl, wsl2=wsl2,
                              first=(ic == 0): te.matmul(
                        ps_h[:, psl * 512:psl * 512 + 512],
                        w2_bf[:, wsl2 * 4096 + ic * 128:wsl2 * 4096 + ic * 128 + 128],
                        h1_fm[:, ic * 512:(ic + 1) * 512],
                        start=first, stop=(ic == 31)), deps=dep)
                ev = P.op("scalar", lambda s2, oc=oc, psl=psl: s2.activation(
                    h2_fm[:, oc * 512:(oc + 1) * 512],
                    ps_h[:, psl * 512:psl * 512 + 512], AF.Relu,
                    bias=b2_sb[:, oc:oc + 1]),
                    deps=[mm, ld_b2D, h2_user])
                hps_user[psl] = ev
                w2_user[wsl2] = mm
                h2_ops.append(ev)
            h1_user = h2_ops[-1]
            n4ld_user = [w1_user[0], w1_user[1]]
            evD = []
            for od in range(4):
                last_mm = {}
                for ic in range(32):
                    dsl = ic % 2
                    ldw3 = P.dmaop("sync", lambda s2, od=od, ic=ic, dsl=dsl: s2.dma_start(
                        out=w3_bf[:, dsl * 512:(dsl + 1) * 512],
                        in_=w3_bfd[ic * 128:(ic + 1) * 128, od * 512:od * 512 + 512]),
                        deps=[w3_user[dsl], pc_w3] + barA5)
                    for t128 in range(4):
                        if ic == 0:
                            dep = [ldw3, d4ps_user[t128]] + h2_ops
                        elif t128 == 0:
                            dep = [ldw3]
                        else:
                            dep = []
                        last_mm[t128] = P.op("tensor", lambda te, ic=ic, t128=t128, dsl=dsl,
                                             first=(ic == 0): te.matmul(
                            ps_d4[:, t128 * 512:t128 * 512 + 512],
                            h2_fm[:, ic * 512 + t128 * 128:ic * 512 + t128 * 128 + 128],
                            w3_bf[:, dsl * 512:(dsl + 1) * 512],
                            start=first, stop=False), deps=dep)
                    w3_user[dsl] = last_mm[3]
                for t128 in range(4):
                    mmb = P.op("tensor", lambda te, t128=t128, od=od: te.matmul(
                        ps_d4[:, t128 * 512:t128 * 512 + 512], ones_row[:, :],
                        brow_bf[:, od * 512:od * 512 + 512],
                        start=False, stop=True),
                        deps=[last_mm[t128], ld_b3, op_ones])
                    bsl = (od * 4 + t128) % 2
                    ev = P.op("vector", lambda v, t128=t128, bsl=bsl: v.tensor_copy(
                        ffn_bncD[:, bsl * 512:bsl * 512 + 512],
                        ps_d4[:, t128 * 512:t128 * 512 + 512]),
                        deps=[mmb, bncD_user[bsl]])
                    wd2 = P.dmaop("scalar", lambda s2, t128=t128, od=od, bsl=bsl: s2.dma_start(
                        out=ffnD_dram[t128 * 128:(t128 + 1) * 128, od * 512:od * 512 + 512],
                        in_=ffn_bncD[:, bsl * 512:bsl * 512 + 512]),
                        deps=[ev, rdD_user])
                    bncD_user[bsl] = wd2
                    d4ps_user[t128] = ev
                    wrD[t128].append(wd2)
            h2_user = d4ps_user[3]
            for t128 in range(4):
                t = tc * 4 + t128
                rdb = P.dmaop("sync", lambda s2, t128=t128: s2.dma_start(
                    out=ffn_tD[:, :], in_=ffnD_dram[t128 * 128:(t128 + 1) * 128, :]),
                    deps=wrD[t128] + [prev_applyD2, prev_sqD])
                rdD_user = rdb
                stt3 = ln_stats(stD2, ffn_tD, [(t, 0)], [rdb], D,
                                war_deps=[prev_applyD2])
                prev_sqD = stt3[0]
                sqd = SC["sq_dump"]
                ap1 = P.op("vector", lambda v, t=t, sqd=sqd: v.tensor_scalar(
                    sqd[:, 0:D], ffn_tD[:, 0:D],
                    stD2["mean"][:, t:t + 1], stD2["rstd"][:, t:t + 1],
                    ALU.subtract, ALU.mult), deps=stt3)
                ap2 = P.op("vector", lambda v, t=t, sqd=sqd: v.tensor_add(
                    xres[:, t * D:(t + 1) * D], xres[:, t * D:(t + 1) * D],
                    sqd[:, 0:D]), deps=[ap1])
                prev_applyD2 = ap2
                res_doneD.append(ap2)
        res_done = res_doneD
        barD = P.barrier()
        DD.close()
        barA5 = barD

    st_dmas = []
    for t in range(NT):
        st_dmas.append(P.dmaop("sync", lambda s, t=t: s.dma_start(
            out=out[t * 128:(t + 1) * 128, :],
            in_=xres[:, t * D:(t + 1) * D]), deps=[res_done[t]] + barA5))
    P.op("sync", "nop", deps=st_dmas)

    P.emit()
    top.close()
    return nc



def _bf(x):
    return np.ascontiguousarray(np.asarray(x).astype(ml_dtypes.bfloat16))


def _f32(x):
    return np.ascontiguousarray(np.asarray(x).astype(np.float32))


def _prep_inputs(inputs):
    x = np.asarray(inputs["L0_attn_x"]).reshape(T, D)
    wq_a = np.asarray(inputs["L0_attn_wq_a"])
    bq_a = np.asarray(inputs["L0_attn_bq_a"])
    g_q = np.asarray(inputs["L0_attn_g_q"])
    b_q = np.asarray(inputs["L0_attn_b_q"])
    wq_b = np.asarray(inputs["L0_attn_wq_b"])
    bq_b = np.asarray(inputs["L0_attn_bq_b"])
    wkv_a = np.asarray(inputs["L0_attn_wkv_a"])
    bkv_a = np.asarray(inputs["L0_attn_bkv_a"])
    wkv_b = np.asarray(inputs["L0_attn_wkv_b"])
    bkv_b = np.asarray(inputs["L0_attn_bkv_b"])
    wout = np.asarray(inputs["L0_attn_wout"])
    bout = np.asarray(inputs["L0_attn_bout"])

    j = np.arange(32, dtype=np.float64)
    inv = 1.0 / (10000.0 ** (2.0 * j / 64.0))
    tpos = np.arange(T, dtype=np.float64)
    ang = inv[:, None] * tpos[None, :]
    cos_t = np.cos(ang).astype(np.float32)
    sin_t = np.sin(ang).astype(np.float32)
    csA = _f32(np.vstack([cos_t, sin_t]))
    csB = _f32(np.vstack([sin_t, cos_t]))

    mask = np.full((128, 1920), -400.0, np.float32)
    for p in range(128):
        mask[p, 896 + p:] = 0.0

    shared = {
        "wqaT": _f32(wq_a.T),
        "bqa_row": _f32(bq_a.reshape(1, QLR)),
        "gq_bc": _f32(np.broadcast_to(g_q.reshape(1, QLR), (128, QLR))),
        "bq_bc": _f32(np.broadcast_to(b_q.reshape(1, QLR), (128, QLR))),
        "wkvaT": _f32(wkv_a.T),
        "bkva_cols": _f32(bkv_a[:KVLR].reshape(4, 128).T),
        "bkva_rope": _f32(bkv_a[KVLR:].reshape(1, 64).T),
        "woutT": _f32(wout.T),
        "bout_row": _f32(bout.reshape(1, D)),
        "csA_t": csA,
        "csB_t": csB,
        "mask_h": _bf(mask),
        "eye_h": _bf(np.eye(128, dtype=np.float32)),
    }
    if PHASES >= 3:
        shared["w_qkv"] = _f32(inputs["L1_attn_w_qkv"])
        shared["w_o"] = _f32(inputs["L1_attn_w_o"])
    if PHASES >= 4:
        shared["w1T"] = _f32(np.asarray(inputs["L1_ffn_W1"]).T)
        shared["b1_cols"] = _f32(np.asarray(inputs["L1_ffn_b1"]).reshape(32, 128).T)
        shared["w2T"] = _f32(np.asarray(inputs["L1_ffn_W2"]).T)
        shared["b2_cols"] = _f32(np.asarray(inputs["L1_ffn_b2"]).reshape(32, 128).T)
        shared["w3T"] = _f32(np.asarray(inputs["L1_ffn_W3"]).T)
        shared["b3_row"] = _f32(np.asarray(inputs["L1_ffn_b3"]).reshape(1, D))

    in_maps = []
    for r in range(NCORE):
        m = dict(shared)
        m["x_own"] = _f32(x[r * TOK:(r + 1) * TOK])
        m["csA_own"] = _f32(csA[:, r * TOK:(r + 1) * TOK])
        m["csB_own"] = _f32(csB[:, r * TOK:(r + 1) * TOK])
        h0 = 2 * r
        m["wqbT"] = _f32(wq_b[h0 * DQK:(h0 + 2) * DQK, :].T)
        bslice = bq_b[h0 * DQK:(h0 + 2) * DQK].reshape(2, DQK)
        m["bqb_rope"] = _f32(bslice[:, :DR].T)
        m["bqb_nope"] = _f32(bslice[:, DR:].T)
        m["wkvbT"] = _f32(wkv_b[h0 * (DN + DV):(h0 + 2) * (DN + DV), :].T)
        if PHASES >= 2:
            gup = np.asarray(inputs["L0_ffn_gate_up_packed"])
            gsc = np.asarray(inputs["L0_ffn_gate_up_scales"])
            dpk = np.asarray(inputs["L0_ffn_down_packed"])
            dsc = np.asarray(inputs["L0_ffn_down_scales"])
            m["gup_pk"] = np.ascontiguousarray(gup[r * (GU // 8):(r + 1) * (GU // 8)])
            m["gup_scT"] = np.ascontiguousarray(gsc.T[r * (GU // 8):(r + 1) * (GU // 8)])
            m["dn_pk"] = np.ascontiguousarray(dpk[r * (D // 8):(r + 1) * (D // 8)])
            m["dn_scT"] = np.ascontiguousarray(dsc.T[r * (D // 8):(r + 1) * (D // 8)])
        kb = bkv_b[h0 * (DN + DV):(h0 + 2) * (DN + DV)].reshape(2, DN + DV)
        m["bkvb_nope"] = _f32(kb[:, :DN].T)
        m["bkvb_v"] = _f32(kb[:, DN:].reshape(1, 2 * DV))
        in_maps.append(m)
    return in_maps


def kernel(**inputs):
    in_maps = _prep_inputs(inputs)
    nc = build_program()
    res = run_bass_kernel_spmd(nc, in_maps, core_ids=list(range(NCORE)))
    rows = [res.results[r]["out"] for r in range(NCORE)]
    return np.concatenate(rows, axis=0).reshape(1, T, D).astype(np.float32)


# version 31
# speedup vs baseline: 1.0210x; 1.0210x over previous
import math
import numpy as np
import ml_dtypes
import concourse.bass as bass
import concourse.mybir as mybir
from concourse.bass_utils import run_bass_kernel_spmd

F32 = mybir.dt.float32
BF16 = mybir.dt.bfloat16
F16 = mybir.dt.float16
I32 = mybir.dt.int32
I16 = mybir.dt.int16
AF = mybir.ActivationFunctionType
ALU = mybir.AluOpType
AX = mybir.AxisListType

NCORE = 8
T = 8192
D = 2048
TOK = T // NCORE
NT = TOK // 128
H0 = 16
QLR, KVLR = 1024, 512
DN, DR, DV = 128, 64, 128
DQK = DN + DR
I0 = 5632
GU = 2 * I0
EPS = 1e-5

PHASES = 4
DEBUG = False
ENGINES = ("sync", "vector", "scalar", "tensor", "gpsimd")


class Plan:

    NLANES = 8

    class Op:
        __slots__ = ("fn", "eng", "is_dma", "marked", "waits", "count",
                     "lane", "lane_k")

        def __init__(self, fn, eng, is_dma):
            self.fn, self.eng, self.is_dma = fn, eng, is_dma
            self.marked = False
            self.waits = []
            self.count = None
            self.lane = None
            self.lane_k = None

    def __init__(self, nc):
        self.nc = nc
        self.ops = {e: [] for e in ENGINES}
        self.ord = {e: nc.alloc_semaphore(name=f"ord_{e}") for e in ENGINES}
        self.dma_lanes = {e: [nc.alloc_semaphore(name=f"dma_{e}_{i}")
                              for i in range(Plan.NLANES)]
                          for e in ENGINES}
        self._dma_cnt = {e: 0 for e in ENGINES}
        self._lane_cnt = {e: [0] * Plan.NLANES for e in ENGINES}

    def op(self, eng, fn, deps=(), dma=False):
        o = Plan.Op(fn, eng, dma)
        if dma:
            i = self._dma_cnt[eng]
            self._dma_cnt[eng] = i + 1
            o.lane = i % Plan.NLANES
            self._lane_cnt[eng][o.lane] += 1
            o.lane_k = self._lane_cnt[eng][o.lane]
        for d in deps:
            if d is None:
                continue
            if isinstance(d, (list, tuple)):
                for dd in d:
                    if dd is not None:
                        dd.marked = True
                        o.waits.append(dd)
            else:
                d.marked = True
                o.waits.append(d)
        self.ops[eng].append(o)
        return o

    def dmaop(self, eng, fn, deps=()):
        o = self.op(eng, fn, deps, dma=True)
        o.marked = True
        return o

    def barrier(self):
        lasts = [self.ops[e][-1] for e in ENGINES if self.ops[e]]
        outs = []
        for e in ENGINES:
            outs.append(self.op(e, "nop", deps=lasts))
        return outs

    def emit(self):
        for e in ENGINES:
            c_ord, c_dma = 0, 0
            for o in self.ops[e]:
                if o.marked:
                    if o.is_dma:
                        c_dma += 16
                        o.count = c_dma
                    else:
                        c_ord += 1
                        o.count = c_ord

        def emit_engine(engine_obj, eng_name):
            for o in self.ops[eng_name]:
                need = {}
                for p in o.waits:
                    if p.is_dma:
                        sem = self.dma_lanes[p.eng][p.lane]
                        val = 16 * p.lane_k
                    else:
                        sem = self.ord[p.eng]
                        val = p.count
                    key = id(sem)
                    if key not in need or need[key][1] < val:
                        need[key] = (sem, val)
                if o.is_dma and o.lane_k > 1:
                    key = id(self.dma_lanes[eng_name][o.lane])
                    v = 16 * (o.lane_k - 1)
                    if key not in need or need[key][1] < v:
                        need[key] = (self.dma_lanes[eng_name][o.lane], v)
                for sem, val in need.values():
                    engine_obj.wait_ge(sem, val)
                inst = engine_obj.nop() if o.fn == "nop" else o.fn(engine_obj)
                if o.is_dma:
                    inst.then_inc(self.dma_lanes[eng_name][o.lane], 16)
                elif o.marked:
                    inst.then_inc(self.ord[eng_name], 1)

        with self.nc.Block() as block:
            @block.sync
            def _(sync):
                emit_engine(sync, "sync")

            @block.vector
            def _(vector):
                emit_engine(vector, "vector")

            @block.scalar
            def _(scalar):
                emit_engine(scalar, "scalar")

            @block.tensor
            def _(tensor):
                emit_engine(tensor, "tensor")

            @block.gpsimd
            def _(gpsimd):
                emit_engine(gpsimd, "gpsimd")


def build_program():
    from contextlib import ExitStack
    nc = bass.Bass("TRN2", target_bir_lowering=False)
    P = Plan(nc)
    top = ExitStack()

    def sb(stack, name, shape, dt):
        return stack.enter_context(nc.sbuf_tensor(name, shape, dt))

    def psf(stack, name, banks):
        return stack.enter_context(nc.psum_tensor(name, [128, 512 * banks], F32))

    def psb(stack, name, banks):
        return stack.enter_context(nc.psum_tensor(name, [128, 1024 * banks], BF16))

    def din(name, shape, dt):
        return nc.dram_tensor(name, shape, dt, kind="ExternalInput").ap()

    def dint(name, shape, dt):
        return nc.dram_tensor(name, shape, dt).ap()

    x_in = din("x_own", [TOK, D], F32)
    wqaT = din("wqaT", [D, QLR], F32)
    bqa_row = din("bqa_row", [1, QLR], F32)
    gq_bc = din("gq_bc", [128, QLR], F32)
    bq_bc = din("bq_bc", [128, QLR], F32)
    wqbT = din("wqbT", [QLR, 2 * DQK], F32)
    bqb_nope = din("bqb_nope", [128, 2], F32)
    bqb_rope = din("bqb_rope", [64, 2], F32)
    wkvaT = din("wkvaT", [D, KVLR + DR], F32)
    bkva_cols = din("bkva_cols", [128, 4], F32)
    bkva_rope = din("bkva_rope", [64, 1], F32)
    wkvbT = din("wkvbT", [KVLR, 2 * (DN + DV)], F32)
    bkvb_nope = din("bkvb_nope", [128, 2], F32)
    bkvb_v = din("bkvb_v", [1, 2 * DV], F32)
    woutT = din("woutT", [D, D], F32)
    bout_row = din("bout_row", [1, D], F32)
    csA_t = din("csA_t", [64, T], F32)
    csB_t = din("csB_t", [64, T], F32)
    csA_own = din("csA_own", [64, TOK], F32)
    csB_own = din("csB_own", [64, TOK], F32)
    mask_h = din("mask_h", [128, 1920], BF16)
    eye_h = din("eye_h", [128, 128], BF16)
    if PHASES >= 2:
        gup_pk = din("gup_pk", [GU // 8, 256], I32)
        gup_scT = din("gup_scT", [GU // 8, 16], F16)
        dn_pk = din("dn_pk", [D // 8, 704], I32)
        dn_scT = din("dn_scT", [D // 8, 44], F16)
    if PHASES >= 3:
        w_qkv = din("w_qkv", [D, 3 * D], F32)
        w_o = din("w_o", [D, D], F32)
    if PHASES >= 4:
        w1T = din("w1T", [D, 4096], F32)
        b1_cols = din("b1_cols", [128, 32], F32)
        w2T = din("w2T", [4096, 4096], F32)
        b2_cols = din("b2_cols", [128, 32], F32)
        w3T = din("w3T", [4096, D], F32)
        b3_row = din("b3_row", [1, D], F32)

    out = nc.dram_tensor("out", [TOK, D], F32, kind="ExternalOutput").ap()
    if DEBUG:
        dbg_gout = nc.dram_tensor("dbg_gout", [NCORE * 1600, TOK], BF16,
                                  kind="ExternalOutput").ap()
        dbg_y = nc.dram_tensor("dbg_y", [2 * DV, T], BF16,
                               kind="ExternalOutput").ap()
        dbg_attn = nc.dram_tensor("dbg_attn", [TOK, D], F32,
                                  kind="ExternalOutput").ap()
        dbg_yf = nc.dram_tensor("dbg_yf", [NCORE * 2 * DV, T], BF16,
                                kind="ExternalOutput").ap()
        if PHASES >= 2:
            dbg_wg = nc.dram_tensor("dbg_wg", [NCORE * D, GU // 8], BF16,
                                    kind="ExternalOutput").ap()
            dbg_wd = nc.dram_tensor("dbg_wd", [NCORE * I0, D // 8], BF16,
                                    kind="ExternalOutput").ap()
            dbg_ffn = nc.dram_tensor("dbg_ffn", [512, D], F32,
                                     kind="ExternalOutput").ap()

    gin = dint("gin", [QLR + KVLR + DR, TOK], BF16)
    gout = dint("gout", [NCORE * (QLR + KVLR + DR), TOK], BF16)
    ygath = dint("ygath", [2 * DV, T], BF16)
    yfull = dint("yfull", [NCORE * 2 * DV, T], BF16)
    if PHASES >= 2:
        ffn_dram = dint("ffn_dram", [512, D], F32)
        wg_gin = dint("wg_gin", [D, GU // 8], BF16)
        wg_gout = dint("wg_gout", [NCORE * D, GU // 8], BF16)
        wd_gin = dint("wd_gin", [I0, D // 8], BF16)
        wd_gout = dint("wd_gout", [NCORE * I0, D // 8], BF16)
    if PHASES >= 3:
        kin = dint("kin", [D, TOK], BF16)
        kfull = dint("kfull", [NCORE * D, TOK], BF16)
        rs_in = dint("rs_in", [NCORE * H0, TOK], F32)
        rs_out = dint("rs_out", [H0, TOK], F32)
        vdram = dint("vdram", [D, TOK], BF16)
    if PHASES >= 4:
        ffnD_dram = dint("ffnD_dram", [512, D], F32)
        n4_dram = dint("n4_dram", [D, TOK], BF16)


    xres = sb(top, "xres", [128, NT * D], F32)
    rt_sh1 = sb(top, "rt_sh1", [32, 512], F32)
    rt_sh2 = sb(top, "rt_sh2", [32, 512], F32)
    ones_row = sb(top, "ones_row", [1, 128], BF16)
    ones_col = sb(top, "ones_col", [128, 1], BF16)
    eye_bf = sb(top, "eye_bf", [128, 128], BF16)
    SC = {}

    CONST_VALS = [EPS, 1e-6, 1.0 / math.sqrt(float(DQK)), 1.0 / math.sqrt(128.0)]
    cst_sb = sb(top, "cst_sb", [128, len(CONST_VALS)], F32)
    const_dep = {}
    for i, vv in enumerate(CONST_VALS):
        o = P.op("vector", lambda v, i=i, vv=vv: v.memset(cst_sb[:, i:i + 1], vv))
        nc.const_aps.register(F32, vv, cst_sb[:, i:i + 1])
        const_dep[vv] = o

    op_ones = P.op("vector", lambda v: v.memset(ones_row[:, :], 1.0))
    op_ones_col = P.op("vector", lambda v: v.memset(ones_col[:, :], 1.0))
    ld_eye = P.dmaop("sync", lambda s: s.dma_start(out=eye_bf[:, :], in_=eye_h))

    def alloc_stats(stack, tag, n):
        return {k: sb(stack, f"{tag}_{k}", [128, n], F32)
                for k in ("s", "sq", "mean", "msq", "var", "rstd")}

    def ln_stats(st, src, cols, deps, d, eps=EPS, war_deps=(), sqd=None):
        sq_dump = SC["sq_dump"] if sqd is None else sqd
        """cols: list of (idx, col_offset). Fills st mean/rstd at [:, idx]."""
        col_ops = []
        for idx, off in cols:
            col_ops.append(P.op("vector", lambda v, idx=idx, off=off: v.reduce_sum(
                st["s"][:, idx:idx + 1], src[:, off:off + d], axis=AX.X),
                deps=list(deps) + list(war_deps)))
            col_ops.append(P.op("scalar", lambda s, idx=idx, off=off, sq_dump=sq_dump: s.activation(
                sq_dump[:, 0:d], src[:, off:off + d], AF.Square,
                accum_out=st["sq"][:, idx:idx + 1]),
                deps=list(deps) + list(war_deps)))
        lo = min(i for i, _ in cols)
        hi = max(i for i, _ in cols) + 1
        m1 = P.op("vector", lambda v: v.tensor_scalar_mul(
            st["mean"][:, lo:hi], st["s"][:, lo:hi], 1.0 / d), deps=col_ops)
        m2 = P.op("vector", lambda v: v.tensor_scalar_mul(
            st["msq"][:, lo:hi], st["sq"][:, lo:hi], 1.0 / d), deps=col_ops)
        m3 = P.op("vector", lambda v: v.tensor_mul(
            st["var"][:, lo:hi], st["mean"][:, lo:hi], st["mean"][:, lo:hi]),
            deps=[m1])
        m4 = P.op("vector", lambda v: v.tensor_sub(
            st["var"][:, lo:hi], st["msq"][:, lo:hi], st["var"][:, lo:hi]),
            deps=[m2, m3])
        m5 = P.op("scalar", lambda s: s.activation(
            st["rstd"][:, lo:hi], st["var"][:, lo:hi], AF.Ln, bias=eps),
            deps=[m4, const_dep[eps]] + list(war_deps))
        m6 = P.op("scalar", lambda s: s.activation(
            st["rstd"][:, lo:hi], st["rstd"][:, lo:hi], AF.Exp, scale=-0.5),
            deps=[m5])
        return [m6]

    tr_prev = {}

    def transpose_to(ps_t, slot, src_bf, src_off, dst_bf, dst_off, deps):
        tr = P.op("tensor", lambda te: te.transpose(
            ps_t[:, slot * 1024:slot * 1024 + 128],
            src_bf[:, src_off:src_off + 128], eye_bf[:, :]),
            deps=list(deps) + [ld_eye, tr_prev.get((id(ps_t), slot))])
        ev = P.op("scalar", lambda s: s.copy(
            dst_bf[:, dst_off:dst_off + 128],
            ps_t[:, slot * 1024:slot * 1024 + 128]), deps=[tr])
        tr_prev[(id(ps_t), slot)] = ev
        return ev

    def rope_ops(dst_bf, dst_off, src_f32, src_off, csA, csB, cs_off, w,
                 deps, t1, t2):
        o1 = P.op("vector", lambda v: v.tensor_mul(
            t1[:, 0:w], src_f32[0:32, src_off:src_off + w],
            csA[0:32, cs_off:cs_off + w]), deps=deps)
        o2 = P.op("vector", lambda v: v.tensor_mul(
            t2[:, 0:w], src_f32[32:64, src_off:src_off + w],
            csA[32:64, cs_off:cs_off + w]), deps=deps)
        o3 = P.op("vector", lambda v: v.tensor_sub(
            dst_bf[0:32, dst_off:dst_off + w], t1[:, 0:w], t2[:, 0:w]),
            deps=[o1, o2])
        o4 = P.op("vector", lambda v: v.tensor_mul(
            t1[:, 0:w], src_f32[32:64, src_off:src_off + w],
            csB[32:64, cs_off:cs_off + w]), deps=[o3])
        o5 = P.op("vector", lambda v: v.tensor_mul(
            t2[:, 0:w], src_f32[0:32, src_off:src_off + w],
            csB[0:32, cs_off:cs_off + w]), deps=[o3])
        o6 = P.op("vector", lambda v: v.tensor_add(
            dst_bf[32:64, dst_off:dst_off + w], t1[:, 0:w], t2[:, 0:w]),
            deps=[o4, o5])
        return o6

    A1 = ExitStack()
    SC["sq_dump"] = sb(A1, "sq_dumpA", [128, D], F32)
    ln_tmp = SC["ln_tmp"] = sb(A1, "ln_tmpA", [128, QLR], F32)
    tm_bf = SC["tm_bf"] = sb(A1, "tm_bfA", [128, D], BF16)
    n1_fm = sb(A1, "n1_fm", [128, 16 * TOK], BF16)
    tr_ps = psb(A1, "tr_ps", 2)

    x_loads = [P.dmaop("sync", lambda s, t=t: s.dma_start(
        out=xres[:, t * D:(t + 1) * D], in_=x_in[t * 128:(t + 1) * 128, :]))
        for t in range(NT)]

    stA = alloc_stats(A1, "stA", NT)
    stQL = alloc_stats(A1, "stQL", NT)
    n1_stats = ln_stats(stA, xres, [(t, t * D) for t in range(NT)], x_loads, D)
    n1_done = []
    prev_apply = None
    for t in range(NT):
        a = P.op("vector", lambda v, t=t, tm_bf=tm_bf: v.tensor_scalar(
            tm_bf[:, 0:D], xres[:, t * D:(t + 1) * D],
            stA["mean"][:, t:t + 1], stA["rstd"][:, t:t + 1],
            ALU.subtract, ALU.mult), deps=n1_stats + [prev_apply])
        last = None
        for ic in range(16):
            last = transpose_to(tr_ps, ic % 2, tm_bf, ic * 128,
                                n1_fm, ic * TOK + t * 128, [a])
        prev_apply = last
        n1_done.append(last)
    n1_all = n1_done[-1]

    ql_bf = sb(A1, "ql_bf", [128, NT * QLR], BF16)
    qlb_bf = sb(A1, "qlb_bf", [128, 2 * 128], BF16)
    bqa_bf = sb(A1, "bqa_bf", [1, QLR], BF16)
    gq_sb = sb(A1, "gq_sb", [128, QLR], BF16)
    bq_sb = sb(A1, "bq_sb", [128, QLR], BF16)
    bkva_sb = sb(A1, "bkva_sb", [128, 4], F32)
    bkvar_sb = sb(A1, "bkvar_sb", [64, 1], F32)
    kv_fm = sb(A1, "kv_fm", [128, 4 * TOK], BF16)
    kvr_f32 = sb(A1, "kvr_f32", [64, 512], F32)
    kvr_bf = sb(A1, "kvr_bf", [64, TOK], BF16)
    cskA_sb = sb(A1, "cskA_sb", [64, 512], F32)
    cskB_sb = sb(A1, "cskB_sb", [64, 512], F32)
    rt1 = rt_sh1
    rt2 = rt_sh2
    ql_ps = psf(A1, "ql_ps", 2)
    kv_ps = psf(A1, "kv_ps", 2)
    wqa_bf = sb(A1, "wqa_bf", [128, 16 * QLR], BF16)
    wkva_bf = sb(A1, "wkva_bf", [128, 16 * (KVLR + DR)], BF16)

    ld_wqa = P.dmaop("gpsimd", lambda g: g.dma_start(
        out=wqa_bf[:, :].rearrange("p (a o) -> p a o", o=QLR), in_=wqaT.rearrange("(a p) o -> p a o", p=128)))
    ld_wkva = P.dmaop("gpsimd", lambda g: g.dma_start(
        out=wkva_bf[:, :].rearrange("p (a o) -> p a o", o=KVLR + DR), in_=wkvaT.rearrange("(a p) o -> p a o", p=128)))
    ld_bqa = P.dmaop("gpsimd", lambda g: g.dma_start(out=bqa_bf[:, :], in_=bqa_row))
    pc_wout = pc_wqkv = pc_wo = pc_w1 = pc_w2 = pc_w3 = None
    ld_gq = P.dmaop("gpsimd", lambda g: g.dma_start(out=gq_sb[:, :], in_=gq_bc))
    ld_bq = P.dmaop("gpsimd", lambda g: g.dma_start(out=bq_sb[:, :], in_=bq_bc))
    ld_bkva = P.dmaop("sync", lambda s: s.dma_start(out=bkva_sb[:, :], in_=bkva_cols))
    ld_bkvar = P.dmaop("sync", lambda s: s.dma_start(out=bkvar_sb[:, :], in_=bkva_rope))

    ql_evacs = []
    ps_user = [None, None]
    for t in range(NT):
        for oc in range(2):
            slot = (t * 2 + oc) % 2
            mm = None
            for ic in range(16):
                deps = [ld_wqa, n1_done[t], ps_user[slot]] if ic == 0 else []
                mm = P.op("tensor", lambda te, t=t, oc=oc, ic=ic, slot=slot,
                          first=(ic == 0): te.matmul(
                    ql_ps[:, slot * 512:slot * 512 + 512],
                    n1_fm[:, ic * TOK + t * 128:ic * TOK + t * 128 + 128],
                    wqa_bf[:, ic * QLR + oc * 512:ic * QLR + oc * 512 + 512],
                    start=first, stop=False), deps=deps)
            mmb = P.op("tensor", lambda te, oc=oc, slot=slot: te.matmul(
                ql_ps[:, slot * 512:slot * 512 + 512], ones_row[:, :],
                bqa_bf[:, oc * 512:oc * 512 + 512],
                start=False, stop=True), deps=[mm, ld_bqa, op_ones])
            ev = P.op("vector", lambda v, t=t, oc=oc, slot=slot: v.tensor_copy(
                ql_bf[:, t * QLR + oc * 512:t * QLR + oc * 512 + 512],
                ql_ps[:, slot * 512:slot * 512 + 512]), deps=[mmb])
            ps_user[slot] = ev
            ql_evacs.append(ev)

    ql_stats = ln_stats(stQL, ql_bf, [(t, t * QLR) for t in range(NT)],
                        [ql_evacs[-1]], QLR)
    prev_tr = None
    ql_done = []
    qlb_user = [None, None]
    gin_ql_dmas = []
    for t in range(NT):
        a1 = P.op("vector", lambda v, t=t, ln_tmp=ln_tmp: v.scalar_tensor_tensor(
            ln_tmp[:, 0:QLR], ql_bf[:, t * QLR:(t + 1) * QLR],
            stQL["mean"][:, t:t + 1], gq_sb[:, :],
            ALU.subtract, ALU.mult), deps=ql_stats + [ld_gq])
        a3 = P.op("vector", lambda v, t=t, ln_tmp=ln_tmp, tm_bf=tm_bf: v.scalar_tensor_tensor(
            tm_bf[:, 0:QLR], ln_tmp[:, 0:QLR],
            stQL["rstd"][:, t:t + 1], bq_sb[:, :],
            ALU.mult, ALU.add), deps=[a1, ld_bq, prev_tr])
        last = None
        for ic in range(8):
            sl8 = ic % 2
            ev = transpose_to(tr_ps, sl8, tm_bf, ic * 128,
                              qlb_bf, sl8 * 128,
                              [a3, qlb_user[sl8]])
            dd = P.dmaop("scalar", lambda sc, ic=ic, t=t, sl8=sl8: sc.dma_start(
                out=gin[ic * 128:(ic + 1) * 128, t * 128:(t + 1) * 128],
                in_=qlb_bf[:, sl8 * 128:sl8 * 128 + 128]), deps=[ev])
            qlb_user[sl8] = dd
            gin_ql_dmas.append(dd)
            last = ev
        prev_tr = last
        ql_done.append(last)

    kv_evacs = []
    ps_user = [None, None]
    rope_ev = []
    rope_k = []
    for cc in range(5):
        width = 64 if cc == 4 else 128
        for c in range(2):
            slot = (cc * 2 + c) % 2
            mm = None
            for ic in range(16):
                deps = [ld_wkva, n1_all, ps_user[slot]] if ic == 0 else []
                mm = P.op("tensor", lambda te, cc=cc, width=width, c=c, ic=ic,
                          slot=slot, first=(ic == 0): te.matmul(
                    kv_ps[0:width, slot * 512:slot * 512 + 512],
                    wkva_bf[:, ic * 576 + cc * 128:ic * 576 + cc * 128 + width],
                    n1_fm[:, ic * TOK + c * 512:ic * TOK + c * 512 + 512],
                    start=first, stop=(ic == 15)), deps=deps)
            if width == 128:
                ev = P.op("scalar", lambda s, cc=cc, c=c, slot=slot: s.activation(
                    kv_fm[:, cc * TOK + c * 512:cc * TOK + c * 512 + 512],
                    kv_ps[:, slot * 512:slot * 512 + 512], AF.Identity,
                    bias=bkva_sb[:, cc:cc + 1]), deps=[mm, ld_bkva])
            else:
                ev = P.op("scalar", lambda s, c=c, slot=slot: s.activation(
                    kvr_f32[:, 0:512],
                    kv_ps[0:64, slot * 512:slot * 512 + 512], AF.Identity,
                    bias=bkvar_sb[:, 0:1]), deps=[mm, ld_bkvar] + rope_k)
                rope_ev.append(ev)
                ld_ck = P.dmaop("sync", lambda s2, c=c: s2.dma_start(
                    out=cskA_sb[:, :], in_=csA_own[:, c * 512:(c + 1) * 512]),
                    deps=rope_k)
                ld_sk = P.dmaop("sync", lambda s2, c=c: s2.dma_start(
                    out=cskB_sb[:, :], in_=csB_own[:, c * 512:(c + 1) * 512]),
                    deps=rope_k)
                rope_k.append(rope_ops(kvr_bf, c * 512, kvr_f32, 0,
                                       cskA_sb, cskB_sb, 0, 512,
                                       [ev, ld_ck, ld_sk], rt1, rt2))
            ps_user[slot] = ev
            kv_evacs.append(ev)


    gin_dmas = list(gin_ql_dmas)
    for cc in range(4):
        gin_dmas.append(P.dmaop("scalar", lambda g, cc=cc: g.dma_start(
            out=gin[QLR + cc * 128:QLR + (cc + 1) * 128, :],
            in_=kv_fm[:, cc * TOK:(cc + 1) * TOK]), deps=[kv_evacs[-1]]))
    gin_dmas.append(P.dmaop("scalar", lambda g: g.dma_start(
        out=gin[QLR + KVLR:QLR + KVLR + DR, :], in_=kvr_bf[:, :]),
        deps=rope_k))
    ag1 = P.op("gpsimd", lambda g: g.collective_compute(
        "AllGather", ALU.bypass, replica_groups=[list(range(NCORE))],
        ins=[gin.opt()], outs=[gout.opt()]), deps=gin_dmas)

    barA1 = P.barrier()
    A1.close()
    brow_bf = sb(top, "brow_bf", [1, D], BF16)

    A3 = ExitStack()
    wqb_bf = sb(A3, "wqb_bf", [128, 8 * 192], BF16)
    wkvb_bf = sb(A3, "wkvb_bf", [128, 4 * 256], BF16)
    bqbn_sb = sb(A3, "bqbn_sb", [128, 2], F32)
    bqbr_sb = sb(A3, "bqbr_sb", [64, 2], F32)
    bkvbn_sb = sb(A3, "bkvbn_sb", [128, 2], F32)
    bkvbv_bf = sb(A3, "bkvbv_bf", [1, 2 * DV], BF16)
    csqA_sb = sb(A3, "csqA_sb", [64, 512], F32)
    csqB_sb = sb(A3, "csqB_sb", [64, 512], F32)
    gq_buf = sb(A3, "gq_buf", [128, 2 * 8 * 512], BF16)
    gkv_buf = sb(A3, "gkv_buf", [128, 2 * 4 * 512], BF16)
    Qn = sb(A3, "Qn", [128, T], BF16)
    Qr = sb(A3, "Qr", [64, T], BF16)
    Kn = sb(A3, "Kn", [128, T], BF16)
    Kr = sb(A3, "Kr", [64, T], BF16)
    Vt = sb(A3, "Vt", [128, (T // 128) * DV], BF16)
    qr_f32 = sb(A3, "qr_f32", [64, 512], F32)
    pt_bf = sb(A3, "pt_bf", [128, 2 * 1024], BF16)
    invd_f = sb(A3, "invd_f", [128, 1024], F32)
    yb_bf = sb(A3, "yb_bf", [128, 2 * 1024], BF16)
    rt1b = rt_sh1
    rt2b = rt_sh2
    mask_bf = sb(A3, "mask_bf", [128, 1920], BF16)
    ones128_bf = sb(A3, "ones128_bf", [128, 128], BF16)

    ps_all = psf(A3, "ps_all", 8)
    ld_mask = P.dmaop("sync", lambda s: s.dma_start(out=mask_bf[:, :], in_=mask_h),
                      deps=barA1)
    op_ones128 = P.op("vector", lambda v: v.memset(ones128_bf[:, :], 1.0))

    ld_b1 = P.dmaop("sync", lambda s: s.dma_start(out=bqbn_sb[:, :], in_=bqb_nope), deps=barA1)
    ld_b2 = P.dmaop("sync", lambda s: s.dma_start(out=bqbr_sb[:, :], in_=bqb_rope), deps=barA1)
    ld_b3 = P.dmaop("sync", lambda s: s.dma_start(out=bkvbn_sb[:, :], in_=bkvb_nope), deps=barA1)
    ld_b4 = P.dmaop("gpsimd", lambda g: g.dma_start(out=bkvbv_bf[:, :], in_=bkvb_v), deps=barA1)

    a2a_dmas = []
    head_bar = barA1
    for hloc in range(2):
        ld_wqb = P.dmaop("gpsimd", lambda g, hloc=hloc: g.dma_start(
            out=wqb_bf[:, :].rearrange("p (a o) -> p a o", o=192),
            in_=wqbT[:, hloc * 192:(hloc + 1) * 192]
                .rearrange("(a p) o -> p a o", p=128)),
            deps=head_bar)
        ld_wkvb = P.dmaop("gpsimd", lambda g, hloc=hloc: g.dma_start(
            out=wkvb_bf[:, :].rearrange("p (a o) -> p a o", o=256),
            in_=wkvbT[:, hloc * 256:(hloc + 1) * 256]
                .rearrange("(a p) o -> p a o", p=128)),
            deps=head_bar)
        slot_user = [None, None]
        proj_last = []
        prev_evr = None
        prev_evn = None
        kn_prev_ev = None
        v_prev_ev = None
        prev_rope = None
        for c in range(16):
            sl = c % 2
            rk, co = c // 2, (c % 2) * 512
            deps_ld = [ag1, slot_user[sl]] + head_bar
            lds = []
            for ic in range(8):
                lds.append(P.dmaop("sync", lambda s, ic=ic, rk=rk, co=co, sl=sl: s.dma_start(
                    out=gq_buf[:, (sl * 8 + ic) * 512:(sl * 8 + ic) * 512 + 512],
                    in_=gout[rk * 1600 + ic * 128:rk * 1600 + (ic + 1) * 128, co:co + 512]),
                    deps=deps_ld))
            for ic in range(4):
                lds.append(P.dmaop("sync", lambda s, ic=ic, rk=rk, co=co, sl=sl: s.dma_start(
                    out=gkv_buf[:, (sl * 4 + ic) * 512:(sl * 4 + ic) * 512 + 512],
                    in_=gout[rk * 1600 + QLR + ic * 128:rk * 1600 + QLR + (ic + 1) * 128, co:co + 512]),
                    deps=deps_ld))
            ld_kr = P.dmaop("sync", lambda s, c=c, rk=rk, co=co: s.dma_start(
                out=Kr[:, c * 512:c * 512 + 512],
                in_=gout[rk * 1600 + QLR + KVLR:rk * 1600 + QLR + KVLR + DR, co:co + 512]),
                deps=deps_ld)
            lds.append(ld_kr)
            ld_cq = P.dmaop("sync", lambda s, c=c: s.dma_start(
                out=csqA_sb[:, :], in_=csA_t[:, c * 512:(c + 1) * 512]),
                deps=[prev_rope] + head_bar)
            ld_sq = P.dmaop("sync", lambda s, c=c: s.dma_start(
                out=csqB_sb[:, :], in_=csB_t[:, c * 512:(c + 1) * 512]),
                deps=[prev_rope] + head_bar)

            mm = None
            for ic in range(8):
                deps = (lds + [ld_wqb, prev_evr] + head_bar) if ic == 0 else []
                mm = P.op("tensor", lambda te, ic=ic, sl=sl, hloc=hloc,
                          first=(ic == 0): te.matmul(
                    ps_all[0:64, 0:512],
                    wqb_bf[:, ic * 192:ic * 192 + 64],
                    gq_buf[:, (sl * 8 + ic) * 512:(sl * 8 + ic) * 512 + 512],
                    start=first, stop=(ic == 7)), deps=deps)
            evr = P.op("scalar", lambda s, hloc=hloc: s.activation(
                qr_f32[:, 0:512], ps_all[0:64, 0:512], AF.Identity,
                bias=bqbr_sb[:, hloc:hloc + 1]), deps=[mm, ld_b2, prev_rope])
            prev_evr = evr
            rp = rope_ops(Qr, c * 512, qr_f32, 0, csqA_sb, csqB_sb, 0,
                          512, [evr, ld_cq, ld_sq], rt1b, rt2b)
            prev_rope = rp
            mm2 = None
            for ic in range(8):
                deps = ([prev_evn] if prev_evn is not None else (lds + [ld_wqb])) if ic == 0 else []
                mm2 = P.op("tensor", lambda te, ic=ic, sl=sl, hloc=hloc,
                           first=(ic == 0): te.matmul(
                    ps_all[:, 512:1024],
                    wqb_bf[:, ic * 192 + 64:ic * 192 + 192],
                    gq_buf[:, (sl * 8 + ic) * 512:(sl * 8 + ic) * 512 + 512],
                    start=first, stop=(ic == 7)), deps=deps)
            evn = P.op("scalar", lambda s, c=c, hloc=hloc: s.activation(
                Qn[:, c * 512:c * 512 + 512], ps_all[:, 512:1024], AF.Identity,
                bias=bqbn_sb[:, hloc:hloc + 1]), deps=[mm2, ld_b1])
            prev_evn = evn
            mm3 = None
            for ic in range(4):
                deps = (lds + [ld_wkvb, kn_prev_ev] + head_bar) if ic == 0 else []
                mm3 = P.op("tensor", lambda te, ic=ic, sl=sl, hloc=hloc,
                           first=(ic == 0): te.matmul(
                    ps_all[:, 1024:1536],
                    wkvb_bf[:, ic * 256:ic * 256 + 128],
                    gkv_buf[:, (sl * 4 + ic) * 512:(sl * 4 + ic) * 512 + 512],
                    start=first, stop=(ic == 3)), deps=deps)
            kn_ev = P.op("scalar", lambda s, c=c, hloc=hloc: s.activation(
                Kn[:, c * 512:c * 512 + 512], ps_all[:, 1024:1536], AF.Identity,
                bias=bkvbn_sb[:, hloc:hloc + 1]), deps=[mm3, ld_b3])
            kn_prev_ev = kn_ev
            v_evs = []
            last_v_mm = None
            for s4 in range(4):
                tt = c * 4 + s4
                mm4 = None
                for ic in range(4):
                    deps = ([v_prev_ev, ld_b4, op_ones] + head_bar) if ic == 0 else []
                    mm4 = P.op("tensor", lambda te, ic=ic, sl=sl, s4=s4, hloc=hloc,
                               first=(ic == 0): te.matmul(
                        ps_all[:, 1536:1664],
                        gkv_buf[:, (sl * 4 + ic) * 512 + s4 * 128:(sl * 4 + ic) * 512 + s4 * 128 + 128],
                        wkvb_bf[:, ic * 256 + 128:ic * 256 + 256],
                        start=first, stop=False), deps=deps)
                mm4b = P.op("tensor", lambda te, hloc=hloc: te.matmul(
                    ps_all[:, 1536:1664], ones_row[:, :],
                    bkvbv_bf[:, hloc * DV:hloc * DV + DV],
                    start=False, stop=True), deps=[mm4])
                last_v_mm = mm4b
                v_ev = P.op("vector", lambda v, tt=tt: v.tensor_copy(
                    Vt[:, tt * DV:tt * DV + DV], ps_all[:, 1536:1664]), deps=[mm4b])
                v_prev_ev = v_ev
                v_evs.append(v_ev)
            slot_user[sl] = [last_v_mm, evn]
            proj_last += [rp, evn, kn_ev, ld_kr] + v_evs

        ps_s_user = [None, None]
        pt_user = [None, None]
        y_user = None
        dn_user = None
        yb_user = [None, None]
        for qc in range(8):
            nkt = min((qc + 1) * 8, 64)
            av_last = None
            dn_last = None
            pend = None
            for kt in range(nkt):
                sl = kt % 2
                bdry = (kt * 128 >= qc * 1024)
                mm_last = []
                for h2 in range(2):
                    co = h2 * 512
                    deps = ([ps_s_user[sl]] + (proj_last if (qc == 0 and kt == 0)
                                               else [])) if h2 == 0 else []
                    smm1 = P.op("tensor", lambda te, kt=kt, qc=qc, sl=sl, co=co: te.matmul(
                        ps_all[:, sl * 1024 + co:sl * 1024 + co + 512],
                        Kn[:, kt * 128:kt * 128 + 128],
                        Qn[:, qc * 1024 + co:qc * 1024 + co + 512],
                        start=True, stop=False), deps=deps)
                    smm2 = P.op("tensor", lambda te, kt=kt, qc=qc, sl=sl, co=co,
                                bdry=bdry: te.matmul(
                        ps_all[:, sl * 1024 + co:sl * 1024 + co + 512],
                        Kr[:, kt * 128:kt * 128 + 128],
                        Qr[:, qc * 1024 + co:qc * 1024 + co + 512],
                        start=False, stop=not bdry), deps=[smm1])
                    mm = smm2
                    if bdry:
                        r0 = 896 - (kt * 128 - qc * 1024) + co
                        mm = P.op("tensor", lambda te, sl=sl, r0=r0, co=co: te.matmul(
                            ps_all[:, sl * 1024 + co:sl * 1024 + co + 512],
                            eye_bf[:, :], mask_bf[:, r0:r0 + 512],
                            start=False, stop=True), deps=[smm2, ld_mask, ld_eye])
                    mm_last.append(mm)
                if pend is not None:
                    dn_last, av_last = pend()
                    pend = None
                ex = P.op("scalar", lambda s, sl=sl: s.activation(
                    pt_bf[:, sl * 1024:sl * 1024 + 1024],
                    ps_all[:, sl * 1024:sl * 1024 + 1024], AF.Exp,
                    scale=1.0 / math.sqrt(float(DQK))),
                    deps=mm_last + [pt_user[sl], const_dep[1.0 / math.sqrt(float(DQK))]])
                ps_s_user[sl] = ex

                def mk_pend(kt=kt, sl=sl, ex=ex, qc=qc, nkt=nkt):
                    def emit():
                        dnl, avl = [], []
                        for h2 in range(2):
                            co = h2 * 512
                            dnl.append(P.op("tensor", lambda te, co=co: te.matmul(
                                ps_all[0:128, 3072 + co:3072 + co + 512],
                                ones128_bf[:, :],
                                pt_bf[:, sl * 1024 + co:sl * 1024 + co + 512],
                                start=(kt == 0), stop=(kt == nkt - 1)),
                                deps=[ex, op_ones128] + ([dn_user] if kt == 0 and h2 == 0 else [])))
                            avl.append(P.op("tensor", lambda te, co=co: te.matmul(
                                ps_all[:, 2048 + co:2048 + co + 512],
                                Vt[:, kt * DV:kt * DV + DV],
                                pt_bf[:, sl * 1024 + co:sl * 1024 + co + 512],
                                start=(kt == 0), stop=(kt == nkt - 1)),
                                deps=[ex] + ([y_user] if kt == 0 and h2 == 0 else [])))
                        pt_user[sl] = dnl + avl
                        return dnl, avl
                    return emit
                pend = mk_pend()
            dn_last, av_last = pend()
            rc = P.op("vector", lambda v: v.reciprocal(
                invd_f[:, :], ps_all[:, 3072:4096]), deps=dn_last)
            dn_user = rc
            ysl = qc % 2
            ymul = P.op("vector", lambda v, ysl=ysl: v.tensor_mul(
                yb_bf[:, ysl * 1024:ysl * 1024 + 1024], ps_all[:, 2048:3072],
                invd_f[:, :]), deps=[rc, yb_user[ysl]] + av_last)
            y_user = ymul
            yd = P.dmaop("scalar", lambda g, qc=qc, hloc=hloc, ysl=ysl: g.dma_start(
                out=ygath[hloc * 128:hloc * 128 + 128, qc * 1024:qc * 1024 + 1024],
                in_=yb_bf[:, ysl * 1024:ysl * 1024 + 1024]), deps=[ymul])
            yb_user[ysl] = yd
            a2a_dmas.append(yd)
        head_bar = P.barrier()

    a2a = P.op("gpsimd", lambda g: g.collective_compute(
        "AllGather", ALU.bypass, replica_groups=[list(range(NCORE))],
        ins=[ygath.opt()], outs=[yfull.opt()]), deps=a2a_dmas)

    if DEBUG:
        P.dmaop("gpsimd", lambda g: g.dma_start(out=dbg_gout, in_=gout), deps=[ag1])
        P.dmaop("gpsimd", lambda g: g.dma_start(out=dbg_y, in_=ygath), deps=a2a_dmas)
        P.dmaop("gpsimd", lambda g: g.dma_start(out=dbg_yf, in_=yfull), deps=[a2a])
    barA3 = P.barrier()
    A3.close()
    SC["sq_dump"] = sqd_sh = sb(top, "sqd_sh", [128, D], F32)

    A5 = ExitStack()
    ln_tmp = sb(A5, "ln_tmpA5", [128, D], F32)
    wout_bf = sb(A5, "wout_bf", [128, 16 * D], BF16)
    ysl_bf = sb(A5, "ysl_bf", [128, 2 * D], BF16)
    attn_tm = sb(A5, "attn_tm", [128, D], F32)
    ps_w = psf(A5, "ps_w", 4)
    stW = alloc_stats(A5, "stW", NT)

    ld_wout = P.dmaop("gpsimd", lambda g: g.dma_start(
        out=wout_bf[:, :].rearrange("p (a o) -> p a o", o=D),
        in_=woutT.rearrange("(a p) o -> p a o", p=128)),
        deps=barA3)
    ld_bout = P.dmaop("gpsimd", lambda g: g.dma_start(out=brow_bf[:, :], in_=bout_row),
                      deps=barA3)
    ysl_user = [None, None]
    wps_user = [None] * 4
    res_done = []
    prev_apply = None
    for t in range(NT):
        sl = t % 2
        def _ldy(s, t=t, sl=sl):
            off = s.partition_id() * TOK + t * 128
            return s.dma_start(
                out=ysl_bf[:, sl * D:(sl + 1) * D].rearrange("p (a c) -> p a c", c=128),
                in_=yfull.rearrange("(a p) c -> p a c", p=128)[:, :, bass.ds(off, 128)])
        ldy = P.dmaop("sync", _ldy, deps=[a2a, ysl_user[sl]] + barA3)
        evs = []
        for oc in range(4):
            mm = None
            for ic in range(16):
                deps = [ldy, ld_wout, wps_user[oc]] if ic == 0 else []
                mm = P.op("tensor", lambda te, oc=oc, ic=ic, sl=sl,
                          first=(ic == 0): te.matmul(
                    ps_w[:, oc * 512:oc * 512 + 512],
                    ysl_bf[:, sl * D + ic * 128:sl * D + ic * 128 + 128],
                    wout_bf[:, ic * D + oc * 512:ic * D + oc * 512 + 512],
                    start=first, stop=False), deps=deps)
            mmb = P.op("tensor", lambda te, oc=oc: te.matmul(
                ps_w[:, oc * 512:oc * 512 + 512], ones_row[:, :],
                brow_bf[:, oc * 512:oc * 512 + 512],
                start=False, stop=True), deps=[mm, ld_bout, op_ones])
            ev = P.op("vector", lambda v, oc=oc: v.tensor_copy(
                attn_tm[:, oc * 512:oc * 512 + 512],
                ps_w[:, oc * 512:oc * 512 + 512]),
                deps=[mmb, prev_apply])
            wps_user[oc] = ev
            evs.append(ev)
        ysl_user[sl] = evs[-1]
        if DEBUG:
            dmp = P.dmaop("gpsimd", lambda g, t=t: g.dma_start(
                out=dbg_attn[t * 128:(t + 1) * 128, :], in_=attn_tm[:, :]),
                deps=evs)
            evs = evs + [dmp]
        stt = ln_stats(stW, attn_tm, [(t, 0)], evs, D,
                       war_deps=[prev_apply])
        ap1 = P.op("vector", lambda v, t=t, ln_tmp=ln_tmp: v.tensor_scalar(
            ln_tmp[:, 0:D], attn_tm[:, 0:D],
            stW["mean"][:, t:t + 1], stW["rstd"][:, t:t + 1],
            ALU.subtract, ALU.mult), deps=stt)
        ap2 = P.op("vector", lambda v, t=t, ln_tmp=ln_tmp: v.tensor_add(
            xres[:, t * D:(t + 1) * D], xres[:, t * D:(t + 1) * D],
            ln_tmp[:, 0:D]), deps=[ap1])
        prev_apply = ap2
        res_done.append(ap2)

    barA5 = P.barrier()
    A5.close()

    if PHASES >= 2:
        B1 = ExitStack()
        pk_sb = sb(B1, "pk_sb", [128, 2 * 704], I32)
        scB_sb = sb(B1, "scB_sb", [128, 44], F16)
        scfB_sb = sb(B1, "scfB_sb", [128, 44], F32)
        dq1 = sb(B1, "dq1", [128, 704], I32)
        dq2 = sb(B1, "dq2", [128, 704], I32)
        dq3 = sb(B1, "dq3", [128, 704], I32)
        dq4 = sb(B1, "dq4", [128, 704], I32)
        dq5 = sb(B1, "dq5", [128, 704], I32)
        bitsB = sb(B1, "bitsB", [128, I0], I16)
        denseB = sb(B1, "denseB", [128, I0], BF16)
        trbB = sb(B1, "trbB", [128, 2 * 128], BF16)
        trB_ps = psb(B1, "trB_ps", 2)

        wgq_dmas = []
        pk_user = [None, None]
        dense_user = None
        sc_user = None
        bits_war = None

        def dequant_chunk(ld_pk, ld_sc, pksl, wi32, ngr, out_dmas, dst, doff_fn):
            nonlocal dense_user, sc_user, bits_war
            w = wi32
            cast = P.op("vector", lambda v: v.tensor_copy(
                scfB_sb[:, 0:ngr], scB_sb[:, 0:ngr]), deps=[ld_sc])
            sc_user = cast
            last_bits = None
            for sg in range(8):
                sh1 = (6 - 4 * sg, ALU.logical_shift_left) if 4 * sg < 6 else                       (4 * sg - 6, ALU.logical_shift_right)
                sh2 = (12 - 4 * sg, ALU.logical_shift_left) if sg <= 3 else (4 * sg - 12, ALU.logical_shift_right)
                d1 = P.op("vector", lambda v, sh1=sh1, w=w, pksl=pksl: v.tensor_scalar(
                    dq1[:, 0:w], pk_sb[:, pksl * 704:pksl * 704 + w],
                    sh1[0], 448, sh1[1], ALU.bitwise_and),
                    deps=[ld_pk] if sg == 0 else [])
                d2 = P.op("vector", lambda v, sh2=sh2, w=w, pksl=pksl: v.tensor_scalar(
                    dq2[:, 0:w], pk_sb[:, pksl * 704:pksl * 704 + w],
                    sh2[0], 32768, sh2[1], ALU.bitwise_and), deps=[])
                d3 = P.op("vector", lambda v, w=w: v.tensor_scalar(
                    dq3[:, 0:w], dq1[:, 0:w], 64, 16128, ALU.is_ge, ALU.mult),
                    deps=[])
                d4 = P.op("vector", lambda v, w=w: v.tensor_single_scalar(
                    dq4[:, 0:w], dq1[:, 0:w], 128, ALU.is_ge), deps=[])
                d4b = P.op("vector", lambda v, w=w: v.tensor_mul(
                    dq4[:, 0:w], dq4[:, 0:w], dq1[:, 0:w]), deps=[])
                d5 = P.op("vector", lambda v, w=w: v.tensor_add(
                    dq5[:, 0:w], dq3[:, 0:w], dq4[:, 0:w]), deps=[])
                last_bits = P.op("vector", lambda v, sg=sg, w=w: v.tensor_sub(
                    bitsB[:, sg:8 * w:8], dq5[:, 0:w], dq2[:, 0:w]), deps=[])
            nonlocal_bits = []
            mul_ops = []
            for g in range(ngr):
                mul_ops.append(P.op("vector", lambda v, g=g: v.tensor_scalar_mul(
                    denseB[:, g * 128:(g + 1) * 128],
                    bitsB[:, g * 128:(g + 1) * 128].bitcast(BF16),
                    scfB_sb[:, g:g + 1]),
                    deps=[last_bits, cast] + ([dense_user] if g == 0 else [])))
            last_tr = None
            for ic in range(ngr):
                sl2 = ic % 2
                ev = transpose_to(trB_ps, sl2, denseB, ic * 128,
                                  trbB, sl2 * 128, [mul_ops[ic], trb_user[sl2]])
                rows, cols = doff_fn(ic)
                dd = P.dmaop("scalar", lambda sc2, rows=rows, cols=cols, sl2=sl2: sc2.dma_start(
                    out=dst[rows[0]:rows[1], cols[0]:cols[1]],
                    in_=trbB[:, sl2 * 128:sl2 * 128 + 128]), deps=[ev])
                trb_user[sl2] = dd
                out_dmas.append(dd)
                last_tr = ev
            dense_user = last_tr
            bits_war = mul_ops[-1]
            return last_tr

        trb_user = [None, None]
        for oc in range(11):
            sl = oc % 2
            ld_pk = P.dmaop("sync", lambda s2, oc=oc, sl=sl: s2.dma_start(
                out=pk_sb[:, sl * 704:sl * 704 + 256],
                in_=gup_pk[oc * 128:(oc + 1) * 128, :]),
                deps=barA5 + [pk_user[sl]])
            ld_sc = P.dmaop("sync", lambda s2, oc=oc: s2.dma_start(
                out=scB_sb[:, 0:16], in_=gup_scT[oc * 128:(oc + 1) * 128, :]),
                deps=barA5 + [sc_user])
            last = dequant_chunk(
                ld_pk, ld_sc, sl, 256, 16, wgq_dmas, wg_gin,
                lambda ic, oc=oc: ((ic * 128, (ic + 1) * 128),
                                   (oc * 128, (oc + 1) * 128)))
            pk_user[sl] = last
        wdq_dmas = []
        for oc in range(2):
            sl = oc % 2
            ld_pk = P.dmaop("sync", lambda s2, oc=oc, sl=sl: s2.dma_start(
                out=pk_sb[:, sl * 704:sl * 704 + 704],
                in_=dn_pk[oc * 128:(oc + 1) * 128, :]),
                deps=[pk_user[sl]])
            ld_sc = P.dmaop("sync", lambda s2, oc=oc: s2.dma_start(
                out=scB_sb[:, 0:44], in_=dn_scT[oc * 128:(oc + 1) * 128, :]),
                deps=[sc_user])
            last = dequant_chunk(
                ld_pk, ld_sc, sl, 704, 44, wdq_dmas, wd_gin,
                lambda ic, oc=oc: ((ic * 128, (ic + 1) * 128),
                                   (oc * 128, (oc + 1) * 128)))
            pk_user[sl] = last

        ag3a = P.op("gpsimd", lambda g: g.collective_compute(
            "AllGather", ALU.bypass, replica_groups=[list(range(NCORE))],
            ins=[wg_gin.opt()], outs=[wg_gout.opt()]), deps=wgq_dmas)
        ag3b = P.op("gpsimd", lambda g: g.collective_compute(
            "AllGather", ALU.bypass, replica_groups=[list(range(NCORE))],
            ins=[wd_gin.opt()], outs=[wd_gout.opt()]), deps=wdq_dmas)

        barB1 = P.barrier()
        B1.close()

        B2 = ExitStack()

        tm_bfB = sb(B2, "tm_bfB", [128, D], BF16)
        n2_fm = sb(B2, "n2_fm", [128, 16 * TOK], BF16)
        h_fm = sb(B2, "h_fm", [128, 44 * 512], BF16)
        wt_bf = sb(B2, "wt_bf", [128, 3 * 32 * 128], BF16)
        wdt_bf = sb(B2, "wdt_bf", [128, 4 * 512], BF16)
        sil_bf = sb(B2, "sil_bf", [128, 512], BF16)
        ffn_bnc = sb(B2, "ffn_bnc", [128, 2 * 512], F32)
        ffn_t = sb(B2, "ffn_t", [128, D], F32)
        ln_tmpB = ffn_t
        sq4 = sb(B2, "sq4", [128, 16], F32)
        stB = alloc_stats(B2, "stB", NT)
        stB2 = alloc_stats(B2, "stB2", NT)
        B2t = ExitStack()
        trB2_ps = psb(B2t, "trB2_ps", 2)

        n2_stats = ln_stats(stB, xres, [(t, t * D) for t in range(NT)],
                            barB1, D)
        n2_done = []
        prev_applyB = None
        for t in range(NT):
            a = P.op("vector", lambda v, t=t, tm_bfB=tm_bfB: v.tensor_scalar(
                tm_bfB[:, 0:D], xres[:, t * D:(t + 1) * D],
                stB["mean"][:, t:t + 1], stB["rstd"][:, t:t + 1],
                ALU.subtract, ALU.mult), deps=n2_stats + [prev_applyB])
            last = None
            for ic in range(16):
                last = transpose_to(trB2_ps, ic % 2, tm_bfB, ic * 128,
                                    n2_fm, ic * TOK + t * 128, [a])
            prev_applyB = last
            n2_done.append(last)
        n2_all = n2_done[-1]
        B2t.close()
        ps_g = psf(B2, "ps_g", 2)
        ps_u = psf(B2, "ps_u", 2)
        ps_d = psf(B2, "ps_d", 4)

        wt_user = [None] * 3
        sil_user = None
        gps_user = [None, None]
        ups_user = [None, None]
        h_user = None
        bnc_user = [None, None]
        rd_user = [None] * 4
        prev_sq = None
        wdt_user = [None] * 4
        dps_user = [None] * 4
        res_doneB = []
        prev_applyB2 = None
        for tc in range(2):
            wr_dmas = [[] for _ in range(4)]
            h_ops = []
            for ocp in range(44):
                wsl = ocp % 3
                lds = []
                for half, ch in ((0, ocp), (1, 44 + ocp)):
                    ro, colr = ch // 11, (ch % 11) * 128
                    lds.append(P.dmaop("sync" if half == 0 else "scalar",
                                       lambda s2, ro=ro, colr=colr, half=half, wsl=wsl: s2.dma_start(
                        out=wt_bf[:, (wsl * 32 + half * 16) * 128:(wsl * 32 + half * 16 + 16) * 128]
                            .rearrange("p (a c) -> p a c", c=128),
                        in_=wg_gout[ro * D:(ro + 1) * D, colr:colr + 128]
                            .rearrange("(a p) c -> p a c", p=128)),
                        deps=[ag3a, wt_user[wsl]]))
                gsl = ocp % 2
                mmg = None
                mmu = None
                for ic in range(16):
                    depg = (lds + [n2_all, gps_user[gsl]]) if ic == 0 else []
                    mmg = P.op("tensor", lambda te, ic=ic, tc=tc, wsl=wsl, gsl=gsl,
                               first=(ic == 0): te.matmul(
                        ps_g[:, gsl * 512:gsl * 512 + 512],
                        wt_bf[:, (wsl * 32 + ic) * 128:(wsl * 32 + ic) * 128 + 128],
                        n2_fm[:, ic * TOK + tc * 512:ic * TOK + tc * 512 + 512],
                        start=first, stop=(ic == 15)), deps=depg)
                for ic in range(16):
                    depu = [ups_user[gsl]] if ic == 0 else []
                    mmu = P.op("tensor", lambda te, ic=ic, tc=tc, wsl=wsl, gsl=gsl,
                               first=(ic == 0): te.matmul(
                        ps_u[:, gsl * 512:gsl * 512 + 512],
                        wt_bf[:, (wsl * 32 + 16 + ic) * 128:(wsl * 32 + 16 + ic) * 128 + 128],
                        n2_fm[:, ic * TOK + tc * 512:ic * TOK + tc * 512 + 512],
                        start=first, stop=(ic == 15)), deps=depu)
                sil = P.op("scalar", lambda s2, gsl=gsl: s2.activation(
                    sil_bf[:, :], ps_g[:, gsl * 512:gsl * 512 + 512], AF.Silu),
                    deps=[mmg, sil_user])
                gps_user[gsl] = sil
                hm = P.op("vector", lambda v, ocp=ocp, gsl=gsl: v.tensor_mul(
                    h_fm[:, ocp * 512:(ocp + 1) * 512], sil_bf[:, :],
                    ps_u[:, gsl * 512:gsl * 512 + 512]), deps=[sil, mmu, h_user])
                ups_user[gsl] = hm
                sil_user = hm
                wt_user[wsl] = hm
                h_ops.append(hm)
            ev_d = []
            for od in range(4):
                last_mm = {}
                for ic in range(44):
                    dsl = ic % 4
                    ldd = []
                    for hh in range(2):
                        ro2 = od * 2 + hh
                        ldd.append(P.dmaop("sync" if hh == 0 else "scalar",
                                           lambda s2, ro2=ro2, ic=ic, dsl=dsl, hh=hh: s2.dma_start(
                            out=wdt_bf[:, dsl * 512 + hh * 256:dsl * 512 + hh * 256 + 256],
                            in_=wd_gout[ro2 * I0 + ic * 128:ro2 * I0 + (ic + 1) * 128, :]),
                            deps=[ag3b, wdt_user[dsl]]))
                    for t128 in range(4):
                        dep = (ldd + h_ops + ([dps_user[t128]] if ic == 0 else [])) if (ic == 0 or t128 == 0) else []
                        if ic > 0 and t128 == 0:
                            dep = ldd
                        last_mm[t128] = P.op("tensor", lambda te, ic=ic, t128=t128, od=od, dsl=dsl,
                                             first=(ic == 0): te.matmul(
                            ps_d[:, t128 * 512:t128 * 512 + 512],
                            h_fm[:, ic * 512 + t128 * 128:ic * 512 + t128 * 128 + 128],
                            wdt_bf[:, dsl * 512:dsl * 512 + 512],
                            start=first, stop=(ic == 43)), deps=dep)
                    wdt_user[dsl] = last_mm[3]
                for t128 in range(4):
                    bsl = (od * 4 + t128) % 2
                    ev = P.op("vector", lambda v, t128=t128, bsl=bsl: v.tensor_copy(
                        ffn_bnc[:, bsl * 512:bsl * 512 + 512],
                        ps_d[:, t128 * 512:t128 * 512 + 512]),
                        deps=[last_mm[t128], bnc_user[bsl]])
                    wd2 = P.dmaop("scalar", lambda s2, t128=t128, od=od, bsl=bsl: s2.dma_start(
                        out=ffn_dram[t128 * 128:(t128 + 1) * 128, od * 512:od * 512 + 512],
                        in_=ffn_bnc[:, bsl * 512:bsl * 512 + 512]),
                        deps=[ev, rd_user[t128]])
                    bnc_user[bsl] = wd2
                    dps_user[t128] = ev
                    wr_dmas[t128].append(wd2)
                    ev_d.append(wd2)
            h_user = ev_d[-1]
            for t128 in range(4):
                t = tc * 4 + t128
                rdb = P.dmaop("sync", lambda s2, t128=t128: s2.dma_start(
                    out=ffn_t[:, :], in_=ffn_dram[t128 * 128:(t128 + 1) * 128, :]),
                    deps=wr_dmas[t128] + [prev_applyB2, prev_sq])
                rd_user[t128] = rdb
                rd_done = rdb
                sq = P.op("scalar", lambda s2, t128=t128: s2.activation(
                    SC["sq_dump"][:, 0:D], ffn_t[:, :],
                    AF.Square, accum_out=sq4[:, t128:t128 + 1]), deps=[rdb])
                prev_sq = sq
                c1 = P.op("vector", lambda v, t128=t128: v.tensor_scalar_mul(
                    sq4[:, 4 + t128:5 + t128], sq4[:, t128:t128 + 1], 1.0 / D),
                    deps=[sq])
                c2 = P.op("scalar", lambda s2, t128=t128: s2.activation(
                    sq4[:, 8 + t128:9 + t128], sq4[:, 4 + t128:5 + t128],
                    AF.Ln, bias=1e-6), deps=[c1, const_dep[1e-6]])
                c3 = P.op("scalar", lambda s2, t128=t128: s2.activation(
                    sq4[:, 12 + t128:13 + t128], sq4[:, 8 + t128:9 + t128],
                    AF.Exp, scale=-0.5), deps=[c2])
                y2 = P.op("vector", lambda v, t128=t128, ln_tmpB=ln_tmpB: v.tensor_scalar_mul(
                    ln_tmpB[:, 0:D], ffn_t[:, :],
                    sq4[:, 12 + t128:13 + t128]), deps=[c3, prev_applyB2, rdb])
                stt = ln_stats(stB2, ln_tmpB, [(t, 0)], [y2], D,
                               war_deps=[prev_applyB2])
                sqd = SC["sq_dump"]
                ap1 = P.op("vector", lambda v, t=t, ln_tmpB=ln_tmpB, sqd=sqd: v.tensor_scalar(
                    sqd[:, 0:D], ln_tmpB[:, 0:D],
                    stB2["mean"][:, t:t + 1], stB2["rstd"][:, t:t + 1],
                    ALU.subtract, ALU.mult), deps=stt)
                ap2 = P.op("vector", lambda v, t=t, sqd=sqd: v.tensor_add(
                    xres[:, t * D:(t + 1) * D], xres[:, t * D:(t + 1) * D],
                    sqd[:, 0:D]), deps=[ap1])
                prev_applyB2 = ap2
                res_doneB.append(ap2)
        res_done = res_doneB
        if DEBUG:
            P.dmaop("gpsimd", lambda g: g.dma_start(out=dbg_wg, in_=wg_gout), deps=[ag3a])
            P.dmaop("gpsimd", lambda g: g.dma_start(out=dbg_wd, in_=wd_gout), deps=[ag3b])
            P.dmaop("gpsimd", lambda g: g.dma_start(out=dbg_ffn, in_=ffn_dram),
                    deps=res_doneB)
        barB = P.barrier()
        B2.close()
        barA5 = barB

    if PHASES >= 3:
        barB2 = res_done[-1]
        C0 = ExitStack()
        Q_fm = sb(C0, "Q_fm", [128, 16 * TOK], BF16)
        stC = alloc_stats(C0, "stC", NT)
        stC2 = alloc_stats(C0, "stC2", NT)

        C1 = ExitStack()
        tm_bfC = sb(C1, "tm_bfC", [128, D], BF16)
        n3_fm = sb(C1, "n3_fm", [128, 16 * TOK], BF16)
        wq_bf = sb(C1, "wq_bf", [128, 3 * 16 * 256], BF16)
        kbnc = sb(C1, "kbnc", [128, 2 * 512], BF16)
        vbnc = sb(C1, "vbnc", [128, 2 * 512], BF16)
        trC_ps = psb(C1, "trC_ps", 2)
        ps_qkv = psf(C1, "ps_qkv", 2)

        n3_stats = ln_stats(stC, xres, [(t, t * D) for t in range(NT)],
                            [barB2] + barA5, D)
        n3_done = []
        prev_applyC = None
        for t in range(NT):
            a = P.op("vector", lambda v, t=t, tm_bfC=tm_bfC: v.tensor_scalar(
                tm_bfC[:, 0:D], xres[:, t * D:(t + 1) * D],
                stC["mean"][:, t:t + 1], stC["rstd"][:, t:t + 1],
                ALU.subtract, ALU.mult), deps=n3_stats + [prev_applyC])
            last = None
            for ic in range(16):
                last = transpose_to(trC_ps, ic % 2, tm_bfC, ic * 128,
                                    n3_fm, ic * TOK + t * 128, [a])
            prev_applyC = last
            n3_done.append(last)
        n3_all = n3_done[-1]

        kin_dmas = []
        vdram_dmas = []
        vbnc_user = [None, None]
        wqsl_user = [None, None, None]
        qkv_ps_user = [None, None]
        kbnc_user = [None, None]
        for pi, part in enumerate((1, 0, 2)):
            for oc2 in range(8):
                wsl = (pi * 8 + oc2) % 3
                ldw = P.dmaop("gpsimd", lambda g, part=part, oc2=oc2, wsl=wsl: g.dma_start(
                    out=wq_bf[:, wsl * 4096:(wsl + 1) * 4096]
                        .rearrange("p (a c) -> p a c", c=256),
                    in_=w_qkv[:, part * D + oc2 * 256:part * D + oc2 * 256 + 256]
                        .rearrange("(a p) c -> p a c", p=128)),
                    deps=[wqsl_user[wsl]] + barA5)
                for sub in range(2):
                  oc = oc2 * 2 + sub
                  for tc in range(2):
                    psl = (oc * 2 + tc) % 2
                    mm = None
                    for ic in range(16):
                        dep = [ldw, n3_all, qkv_ps_user[psl]] if ic == 0 else []
                        mm = P.op("tensor", lambda te, ic=ic, tc=tc, wsl=wsl, psl=psl,
                                  sub=sub, first=(ic == 0): te.matmul(
                            ps_qkv[:, psl * 512:psl * 512 + 512],
                            wq_bf[:, wsl * 4096 + ic * 256 + sub * 128:wsl * 4096 + ic * 256 + sub * 128 + 128],
                            n3_fm[:, ic * TOK + tc * 512:ic * TOK + tc * 512 + 512],
                            start=first, stop=(ic == 15)), deps=dep)
                    if part == 1:
                        ksl = (oc * 2 + tc) % 2
                        ev = P.op("vector", lambda v, psl=psl, ksl=ksl: v.tensor_copy(
                            kbnc[:, ksl * 512:ksl * 512 + 512],
                            ps_qkv[:, psl * 512:psl * 512 + 512]),
                            deps=[mm, kbnc_user[ksl]])
                        kd = P.dmaop("scalar", lambda s2, oc=oc, tc=tc, ksl=ksl: s2.dma_start(
                            out=kin[oc * 128:(oc + 1) * 128, tc * 512:tc * 512 + 512],
                            in_=kbnc[:, ksl * 512:ksl * 512 + 512]), deps=[ev])
                        kbnc_user[ksl] = kd
                        kin_dmas.append(kd)
                        qkv_ps_user[psl] = ev
                    elif part == 0:
                        ev = P.op("vector", lambda v, oc=oc, tc=tc, psl=psl: v.tensor_copy(
                            Q_fm[:, oc * TOK + tc * 512:oc * TOK + tc * 512 + 512],
                            ps_qkv[:, psl * 512:psl * 512 + 512]), deps=[mm])
                        qkv_ps_user[psl] = ev
                    else:
                        vsl = (oc * 2 + tc) % 2
                        ev = P.op("vector", lambda v, psl=psl, vsl=vsl: v.tensor_copy(
                            vbnc[:, vsl * 512:vsl * 512 + 512],
                            ps_qkv[:, psl * 512:psl * 512 + 512]),
                            deps=[mm, vbnc_user[vsl]])
                        vd = P.dmaop("scalar", lambda s2, oc=oc, tc=tc, vsl=vsl: s2.dma_start(
                            out=vdram[oc * 128:(oc + 1) * 128, tc * 512:tc * 512 + 512],
                            in_=vbnc[:, vsl * 512:vsl * 512 + 512]), deps=[ev])
                        vbnc_user[vsl] = vd
                        vdram_dmas.append(vd)
                        qkv_ps_user[psl] = ev
                    wqsl_user[wsl] = mm

        ag4 = P.op("gpsimd", lambda g: g.collective_compute(
            "AllGather", ALU.bypass, replica_groups=[list(range(NCORE))],
            ins=[kin.opt()], outs=[kfull.opt()]), deps=kin_dmas)

        barC1 = P.barrier()
        C1.close()

        C2 = ExitStack()
        K_h = sb(C2, "K_h", [128, 2 * T], BF16)
        P_bf = sb(C2, "P_bf", [128, 2 * T], BF16)
        P_acc = sb(C2, "P_acc", [128, T], BF16)
        dacc = sb(C2, "dacc", [128, 16], F32)
        dnm = sb(C2, "dnm", [128, 4], F32)
        cs_bnc = sb(C2, "cs_bnc", [1, 2 * 512], F32)
        ps_sC = psf(C2, "ps_sC", 4)
        ps_cs = C2.enter_context(nc.psum_tensor("ps_cs", [1, 1024], F32))

        rs_dmas = []
        last_csdma = None
        kh_user = [None, None]
        sC_user = [None, None]
        p_user = [None, None]
        stt_user = None
        csps_user = [None, None]
        cs_dma_user = [None, None]
        for h in range(H0):
            khsl = h % 2
            kh_lds = []
            for ro in range(NCORE):
                kh_lds.append(P.dmaop("sync", lambda s2, ro=ro, h=h, khsl=khsl: s2.dma_start(
                    out=K_h[:, khsl * T + ro * TOK:khsl * T + (ro + 1) * TOK],
                    in_=kfull[ro * D + h * 128:ro * D + (h + 1) * 128, :]),
                    deps=[ag4, kh_user[khsl]]))
            last_stt = None
            for qt in range(NT):
                psl = qt % 2
                dsl = (qt % 2) * 8
                ex_user = None
                for kc2 in range(8):
                    sl = kc2 % 2
                    dep = (kh_lds + [sC_user[sl]]) if kc2 <= 1 else [sC_user[sl]]
                    mma = P.op("tensor", lambda te, h=h, qt=qt, kc2=kc2, sl=sl, khsl=khsl: te.matmul(
                        ps_sC[:, sl * 1024:sl * 1024 + 512],
                        Q_fm[:, h * TOK + qt * 128:h * TOK + qt * 128 + 128],
                        K_h[:, khsl * T + kc2 * 1024:khsl * T + kc2 * 1024 + 512],
                        start=True, stop=True), deps=dep)
                    mmb = P.op("tensor", lambda te, h=h, qt=qt, kc2=kc2, sl=sl, khsl=khsl: te.matmul(
                        ps_sC[:, sl * 1024 + 512:sl * 1024 + 1024],
                        Q_fm[:, h * TOK + qt * 128:h * TOK + qt * 128 + 128],
                        K_h[:, khsl * T + kc2 * 1024 + 512:khsl * T + kc2 * 1024 + 1024],
                        start=True, stop=True), deps=[])
                    ex = P.op("scalar", lambda s2, kc2=kc2, sl=sl, psl=psl, dsl=dsl: s2.activation(
                        P_bf[:, psl * T + kc2 * 1024:psl * T + kc2 * 1024 + 1024],
                        ps_sC[:, sl * 1024:sl * 1024 + 1024], AF.Exp,
                        scale=1.0 / math.sqrt(128.0),
                        accum_out=dacc[:, dsl + kc2:dsl + kc2 + 1]),
                        deps=[mmb, const_dep[1.0 / math.sqrt(128.0)]]
                             + ([p_user[psl]] if kc2 <= 1 else []))
                    sC_user[sl] = ex
                    ex_user = ex
                rd = P.op("vector", lambda v, dsl=dsl: v.reduce_sum(
                    dnm[:, 0:1], dacc[:, dsl:dsl + 8], axis=AX.X), deps=[ex_user])
                rc = P.op("vector", lambda v: v.reciprocal(
                    dnm[:, 1:2], dnm[:, 0:1]), deps=[rd])
                if qt == 0:
                    stt = P.op("vector", lambda v, psl=psl: v.tensor_scalar_mul(
                        P_acc[:, :], P_bf[:, psl * T:psl * T + T], dnm[:, 1:2]),
                        deps=[rc, last_csdma])
                else:
                    stt = P.op("vector", lambda v, psl=psl: v.scalar_tensor_tensor(
                        P_acc[:, :], P_bf[:, psl * T:psl * T + T], dnm[:, 1:2],
                        P_acc[:, :], ALU.mult, ALU.add), deps=[rc, stt_user])
                p_user[psl] = stt
                stt_user = stt
                last_stt = stt
            for kc in range(16):
                csl = kc % 2
                csm = P.op("tensor", lambda te, kc=kc, csl=csl: te.matmul(
                    ps_cs[:, csl * 512:csl * 512 + 512], ones_col[:, :],
                    P_acc[:, kc * 512:kc * 512 + 512],
                    start=True, stop=True),
                    deps=[last_stt, op_ones_col, csps_user[csl]])
                cse = P.op("vector", lambda v2, csl=csl: v2.tensor_copy(
                    cs_bnc[:, csl * 512:csl * 512 + 512],
                    ps_cs[:, csl * 512:csl * 512 + 512]),
                    deps=[csm, cs_dma_user[csl]])
                csps_user[csl] = cse
                csd = P.dmaop("scalar", lambda s2, h=h, kc=kc, csl=csl: s2.dma_start(
                    out=rs_in[(kc // 2) * 16 + h:(kc // 2) * 16 + h + 1,
                              (kc % 2) * 512:(kc % 2) * 512 + 512],
                    in_=cs_bnc[:, csl * 512:csl * 512 + 512]), deps=[cse])
                rs_dmas.append(csd)
                cs_dma_user[csl] = csd
                last_csdma = csd
            kh_user[khsl] = ex_user

        rs = P.op("gpsimd", lambda g: g.collective_compute(
            "ReduceScatter", ALU.add, replica_groups=[list(range(NCORE))],
            ins=[rs_in.opt()], outs=[rs_out.opt()]), deps=rs_dmas)

        barC2 = P.barrier()
        C2.close()

        C3 = ExitStack()
        V_rd = sb(C3, "V_rd", [128, 16 * TOK], BF16)
        cs1 = sb(C3, "cs1", [1, TOK], F32)
        cs1b = sb(C3, "cs1b", [1, TOK], BF16)
        csb = sb(C3, "csb", [128, TOK], BF16)
        wo_bf = sb(C3, "wo_bf", [128, 3 * 16 * 128], BF16)
        attn1_fm = sb(C3, "attn1_fm", [128, 16 * TOK], BF16)
        attn_tmC = sb(C3, "attn_tmC", [128, D], BF16)
        trC2_ps = psb(C3, "trC2_ps", 2)
        ps_bc = psf(C3, "ps_bc", 1)
        ps_wo = psf(C3, "ps_wo", 2)

        ld_v = P.dmaop("sync", lambda s2: s2.dma_start(
            out=V_rd[:, :].rearrange("p (a c) -> p a c", c=TOK),
            in_=vdram.rearrange("(a p) c -> p a c", p=128)),
            deps=vdram_dmas + barC2)
        cs1_user = None
        bc_userC = None
        vs_done = []
        for h in range(H0):
            ld1 = P.dmaop("sync", lambda s2, h=h: s2.dma_start(
                out=cs1[:, :], in_=rs_out[h:h + 1, :]), deps=[rs, cs1_user])
            cp1 = P.op("vector", lambda v: v.tensor_copy(cs1b[:, :], cs1[:, :]),
                       deps=[ld1, cs1_user])
            bch = []
            for half in range(2):
                bcm = P.op("tensor", lambda te, half=half: te.matmul(
                    ps_bc[:, 0:512], ones_row[:, :],
                    cs1b[:, half * 512:half * 512 + 512],
                    start=True, stop=True), deps=[cp1, op_ones, bc_userC] + barC2)
                bce = P.op("vector", lambda v, half=half: v.tensor_copy(
                    csb[:, half * 512:half * 512 + 512], ps_bc[:, 0:512]),
                    deps=[bcm])
                bc_userC = bce
                bch.append(bce)
            vsc = P.op("vector", lambda v, h=h: v.tensor_mul(
                V_rd[:, h * TOK:(h + 1) * TOK], V_rd[:, h * TOK:(h + 1) * TOK],
                csb[:, :]), deps=bch + [ld_v])
            cs1_user = vsc
            vs_done.append(vsc)

        woldu = [None, None, None]
        wops_user = [None, None]
        a1fm_done = []
        for oc in range(16):
            wsl = oc % 3
            ldwo = P.dmaop("gpsimd", lambda g, oc=oc, wsl=wsl: g.dma_start(
                out=wo_bf[:, wsl * 2048:(wsl + 1) * 2048]
                    .rearrange("p (a c) -> p a c", c=128),
                in_=w_o[:, oc * 128:(oc + 1) * 128]
                    .rearrange("(a p) c -> p a c", p=128)),
                deps=[woldu[wsl]] + barC2)
            for tc in range(2):
                psl = (oc * 2 + tc) % 2
                mm = None
                for hh in range(16):
                    dep = ([ldwo, wops_user[psl]] + vs_done) if hh == 0 else []
                    mm = P.op("tensor", lambda te, hh=hh, tc=tc, wsl=wsl, psl=psl,
                              first=(hh == 0): te.matmul(
                        ps_wo[:, psl * 512:psl * 512 + 512],
                        wo_bf[:, wsl * 2048 + hh * 128:wsl * 2048 + hh * 128 + 128],
                        V_rd[:, hh * TOK + tc * 512:hh * TOK + tc * 512 + 512],
                        start=first, stop=(hh == 15)), deps=dep)
                ev = P.op("vector", lambda v, oc=oc, tc=tc, psl=psl: v.tensor_copy(
                    attn1_fm[:, oc * TOK + tc * 512:oc * TOK + tc * 512 + 512],
                    ps_wo[:, psl * 512:psl * 512 + 512]), deps=[mm])
                wops_user[psl] = ev
                woldu[wsl] = mm
                a1fm_done.append(ev)

        res_doneC = []
        prev_applyC2 = None
        for t in range(NT):
            trs = []
            for ic in range(16):
                last = transpose_to(trC2_ps, ic % 2, attn1_fm, ic * TOK + t * 128,
                                    attn_tmC, ic * 128,
                                    [a1fm_done[-1], prev_applyC2])
                trs.append(last)
            stt2 = ln_stats(stC2, attn_tmC, [(t, 0)], trs, D,
                            war_deps=[prev_applyC2])
            sqd = SC["sq_dump"]
            ap1 = P.op("vector", lambda v, t=t, sqd=sqd: v.tensor_scalar(
                sqd[:, 0:D], attn_tmC[:, 0:D],
                stC2["mean"][:, t:t + 1], stC2["rstd"][:, t:t + 1],
                ALU.subtract, ALU.mult), deps=stt2)
            ap2 = P.op("vector", lambda v, t=t, sqd=sqd: v.tensor_add(
                xres[:, t * D:(t + 1) * D], xres[:, t * D:(t + 1) * D],
                sqd[:, 0:D]), deps=[ap1])
            prev_applyC2 = ap2
            res_doneC.append(ap2)
        res_done = res_doneC
        barC = P.barrier()
        C3.close()
        C0.close()
        barA5 = barC

    if PHASES >= 4:
        barC_ = res_done[-1]
        DD = ExitStack()
        n4_fm = sb(DD, "n4_fm", [128, 16 * 512], BF16)
        h1_fm = sb(DD, "h1_fm", [128, 32 * 512], BF16)
        h2_fm = sb(DD, "h2_fm", [128, 32 * 512], BF16)
        w1_bf = sb(DD, "w1_bf", [128, 2 * 16 * 128], BF16)
        w2_bf = sb(DD, "w2_bf", [128, 2 * 32 * 128], BF16)
        w3_bf = sb(DD, "w3_bf", [128, 4 * 512], BF16)
        b1_sb = sb(DD, "b1_sb", [128, 32], F32)
        b2_sb = sb(DD, "b2_sb", [128, 32], F32)
        stD = alloc_stats(DD, "stD", NT)
        stD2 = alloc_stats(DD, "stD2", NT)
        Dt = ExitStack()
        tm_bfD = sb(Dt, "tm_bfD", [128, D], BF16)
        trD_ps = psb(Dt, "trD_ps", 2)

        ld_b1 = P.dmaop("sync", lambda s2: s2.dma_start(out=b1_sb[:, :], in_=b1_cols),
                        deps=barA5)
        ld_b2D = P.dmaop("sync", lambda s2: s2.dma_start(out=b2_sb[:, :], in_=b2_cols),
                         deps=barA5)
        ld_b3 = P.dmaop("gpsimd", lambda g: g.dma_start(out=brow_bf[:, :], in_=b3_row),
                        deps=barA5)

        n4b = sb(Dt, "n4b", [128, 2 * 128], BF16)
        n4_stats = ln_stats(stD, xres, [(t, t * D) for t in range(NT)],
                            [barC_] + barA5, D)
        n4_dmas = []
        n4b_user = [None, None]
        prev_applyD = None
        for t in range(NT):
            a = P.op("vector", lambda v, t=t, tm_bfD=tm_bfD: v.tensor_scalar(
                tm_bfD[:, 0:D], xres[:, t * D:(t + 1) * D],
                stD["mean"][:, t:t + 1], stD["rstd"][:, t:t + 1],
                ALU.subtract, ALU.mult), deps=n4_stats + [prev_applyD])
            last = None
            for ic in range(16):
                sl9 = ic % 2
                ev = transpose_to(trD_ps, sl9, tm_bfD, ic * 128,
                                  n4b, sl9 * 128, [a, n4b_user[sl9]])
                dd = P.dmaop("scalar", lambda s2, ic=ic, t=t, sl9=sl9: s2.dma_start(
                    out=n4_dram[ic * 128:(ic + 1) * 128, t * 128:(t + 1) * 128],
                    in_=n4b[:, sl9 * 128:sl9 * 128 + 128]), deps=[ev])
                n4b_user[sl9] = dd
                n4_dmas.append(dd)
                last = ev
            prev_applyD = last
        Dt.close()
        ffn_tD = sb(DD, "ffn_tD", [128, D], F32)
        ffn_bncD = sb(DD, "ffn_bncD", [128, 2 * 512], F32)
        ps_h = psf(DD, "ps_h", 2)
        ps_d4 = psf(DD, "ps_d4", 4)

        w1_user = [None] * 2
        w2_user = [None] * 2
        w3_user = [None] * 4
        hps_user = [None, None]
        d4ps_user = [None] * 4
        h1_user = None
        h2_user = None
        bncD_user = [None, None]
        rdD_user = None
        prev_sqD = None
        prev_applyD2 = None
        res_doneD = []
        n4ld_user = None
        for tc in range(2):
            wrD = [[] for _ in range(4)]
            ldn4 = P.dmaop("sync", lambda s2, tc=tc: s2.dma_start(
                out=n4_fm[:, :].rearrange("p (a c) -> p a c", c=512),
                in_=n4_dram[:, tc * 512:(tc + 1) * 512]
                    .rearrange("(a p) c -> p a c", p=128)),
                deps=n4_dmas + [n4ld_user])
            h1_ops = []
            for oc in range(32):
                wsl1 = oc % 2
                ldw = P.dmaop("gpsimd", lambda g, oc=oc, wsl1=wsl1: g.dma_start(
                    out=w1_bf[:, wsl1 * 2048:(wsl1 + 1) * 2048]
                        .rearrange("p (a c) -> p a c", c=128),
                    in_=w1T[:, oc * 128:(oc + 1) * 128]
                        .rearrange("(a p) c -> p a c", p=128)),
                    deps=[w1_user[wsl1]] + barA5)
                psl = oc % 2
                mm = None
                for ic in range(16):
                    dep = [ldw, ldn4, hps_user[psl]] if ic == 0 else []
                    mm = P.op("tensor", lambda te, ic=ic, tc=tc, psl=psl, wsl1=wsl1,
                              first=(ic == 0): te.matmul(
                        ps_h[:, psl * 512:psl * 512 + 512],
                        w1_bf[:, wsl1 * 2048 + ic * 128:wsl1 * 2048 + ic * 128 + 128],
                        n4_fm[:, ic * 512:ic * 512 + 512],
                        start=first, stop=(ic == 15)), deps=dep)
                ev = P.op("scalar", lambda s2, oc=oc, psl=psl: s2.activation(
                    h1_fm[:, oc * 512:(oc + 1) * 512],
                    ps_h[:, psl * 512:psl * 512 + 512], AF.Relu,
                    bias=b1_sb[:, oc:oc + 1]),
                    deps=[mm, ld_b1, h1_user])
                hps_user[psl] = ev
                w1_user[wsl1] = mm
                h1_ops.append(ev)
            h2_ops = []
            for oc in range(32):
                wsl2 = oc % 2
                ldw = P.dmaop("gpsimd", lambda g, oc=oc, wsl2=wsl2: g.dma_start(
                    out=w2_bf[:, wsl2 * 4096:(wsl2 + 1) * 4096]
                        .rearrange("p (a c) -> p a c", c=128),
                    in_=w2T[:, oc * 128:(oc + 1) * 128]
                        .rearrange("(a p) c -> p a c", p=128)),
                    deps=[w2_user[wsl2]] + barA5)
                psl = oc % 2
                mm = None
                for ic in range(32):
                    dep = ([ldw, hps_user[psl]] + h1_ops) if ic == 0 else []
                    mm = P.op("tensor", lambda te, ic=ic, psl=psl, wsl2=wsl2,
                              first=(ic == 0): te.matmul(
                        ps_h[:, psl * 512:psl * 512 + 512],
                        w2_bf[:, wsl2 * 4096 + ic * 128:wsl2 * 4096 + ic * 128 + 128],
                        h1_fm[:, ic * 512:(ic + 1) * 512],
                        start=first, stop=(ic == 31)), deps=dep)
                ev = P.op("scalar", lambda s2, oc=oc, psl=psl: s2.activation(
                    h2_fm[:, oc * 512:(oc + 1) * 512],
                    ps_h[:, psl * 512:psl * 512 + 512], AF.Relu,
                    bias=b2_sb[:, oc:oc + 1]),
                    deps=[mm, ld_b2D, h2_user])
                hps_user[psl] = ev
                w2_user[wsl2] = mm
                h2_ops.append(ev)
            h1_user = h2_ops[-1]
            n4ld_user = [w1_user[0], w1_user[1]]
            evD = []
            for od in range(4):
                last_mm = {}
                for ic in range(32):
                    dsl = ic % 4
                    ldw3 = P.dmaop("gpsimd", lambda s2, od=od, ic=ic, dsl=dsl: s2.dma_start(
                        out=w3_bf[:, dsl * 512:(dsl + 1) * 512],
                        in_=w3T[ic * 128:(ic + 1) * 128, od * 512:od * 512 + 512]),
                        deps=[w3_user[dsl]] + barA5)
                    for t128 in range(4):
                        if ic == 0:
                            dep = [ldw3, d4ps_user[t128]] + h2_ops
                        elif t128 == 0:
                            dep = [ldw3]
                        else:
                            dep = []
                        last_mm[t128] = P.op("tensor", lambda te, ic=ic, t128=t128, dsl=dsl,
                                             first=(ic == 0): te.matmul(
                            ps_d4[:, t128 * 512:t128 * 512 + 512],
                            h2_fm[:, ic * 512 + t128 * 128:ic * 512 + t128 * 128 + 128],
                            w3_bf[:, dsl * 512:(dsl + 1) * 512],
                            start=first, stop=False), deps=dep)
                    w3_user[dsl] = last_mm[3]
                for t128 in range(4):
                    mmb = P.op("tensor", lambda te, t128=t128, od=od: te.matmul(
                        ps_d4[:, t128 * 512:t128 * 512 + 512], ones_row[:, :],
                        brow_bf[:, od * 512:od * 512 + 512],
                        start=False, stop=True),
                        deps=[last_mm[t128], ld_b3, op_ones])
                    bsl = (od * 4 + t128) % 2
                    ev = P.op("vector", lambda v, t128=t128, bsl=bsl: v.tensor_copy(
                        ffn_bncD[:, bsl * 512:bsl * 512 + 512],
                        ps_d4[:, t128 * 512:t128 * 512 + 512]),
                        deps=[mmb, bncD_user[bsl]])
                    wd2 = P.dmaop("scalar", lambda s2, t128=t128, od=od, bsl=bsl: s2.dma_start(
                        out=ffnD_dram[t128 * 128:(t128 + 1) * 128, od * 512:od * 512 + 512],
                        in_=ffn_bncD[:, bsl * 512:bsl * 512 + 512]),
                        deps=[ev, rdD_user])
                    bncD_user[bsl] = wd2
                    d4ps_user[t128] = ev
                    wrD[t128].append(wd2)
            h2_user = d4ps_user[3]
            for t128 in range(4):
                t = tc * 4 + t128
                rdb = P.dmaop("sync", lambda s2, t128=t128: s2.dma_start(
                    out=ffn_tD[:, :], in_=ffnD_dram[t128 * 128:(t128 + 1) * 128, :]),
                    deps=wrD[t128] + [prev_applyD2, prev_sqD])
                rdD_user = rdb
                stt3 = ln_stats(stD2, ffn_tD, [(t, 0)], [rdb], D,
                                war_deps=[prev_applyD2])
                prev_sqD = stt3[0]
                sqd = SC["sq_dump"]
                ap1 = P.op("vector", lambda v, t=t, sqd=sqd: v.tensor_scalar(
                    sqd[:, 0:D], ffn_tD[:, 0:D],
                    stD2["mean"][:, t:t + 1], stD2["rstd"][:, t:t + 1],
                    ALU.subtract, ALU.mult), deps=stt3)
                ap2 = P.op("vector", lambda v, t=t, sqd=sqd: v.tensor_add(
                    xres[:, t * D:(t + 1) * D], xres[:, t * D:(t + 1) * D],
                    sqd[:, 0:D]), deps=[ap1])
                prev_applyD2 = ap2
                res_doneD.append(ap2)
        res_done = res_doneD
        barD = P.barrier()
        DD.close()
        barA5 = barD

    st_dmas = []
    for t in range(NT):
        st_dmas.append(P.dmaop("sync", lambda s, t=t: s.dma_start(
            out=out[t * 128:(t + 1) * 128, :],
            in_=xres[:, t * D:(t + 1) * D]), deps=[res_done[t]] + barA5))
    P.op("sync", "nop", deps=st_dmas)

    P.emit()
    top.close()
    return nc



def _bf(x):
    return np.ascontiguousarray(np.asarray(x).astype(ml_dtypes.bfloat16))


def _f32(x):
    return np.ascontiguousarray(np.asarray(x).astype(np.float32))


def _prep_inputs(inputs):
    x = np.asarray(inputs["L0_attn_x"]).reshape(T, D)
    wq_a = np.asarray(inputs["L0_attn_wq_a"])
    bq_a = np.asarray(inputs["L0_attn_bq_a"])
    g_q = np.asarray(inputs["L0_attn_g_q"])
    b_q = np.asarray(inputs["L0_attn_b_q"])
    wq_b = np.asarray(inputs["L0_attn_wq_b"])
    bq_b = np.asarray(inputs["L0_attn_bq_b"])
    wkv_a = np.asarray(inputs["L0_attn_wkv_a"])
    bkv_a = np.asarray(inputs["L0_attn_bkv_a"])
    wkv_b = np.asarray(inputs["L0_attn_wkv_b"])
    bkv_b = np.asarray(inputs["L0_attn_bkv_b"])
    wout = np.asarray(inputs["L0_attn_wout"])
    bout = np.asarray(inputs["L0_attn_bout"])

    j = np.arange(32, dtype=np.float64)
    inv = 1.0 / (10000.0 ** (2.0 * j / 64.0))
    tpos = np.arange(T, dtype=np.float64)
    ang = inv[:, None] * tpos[None, :]
    cos_t = np.cos(ang).astype(np.float32)
    sin_t = np.sin(ang).astype(np.float32)
    csA = _f32(np.vstack([cos_t, sin_t]))
    csB = _f32(np.vstack([sin_t, cos_t]))

    mask = np.full((128, 1920), -400.0, np.float32)
    for p in range(128):
        mask[p, 896 + p:] = 0.0

    shared = {
        "wqaT": _f32(wq_a.T),
        "bqa_row": _f32(bq_a.reshape(1, QLR)),
        "gq_bc": _f32(np.broadcast_to(g_q.reshape(1, QLR), (128, QLR))),
        "bq_bc": _f32(np.broadcast_to(b_q.reshape(1, QLR), (128, QLR))),
        "wkvaT": _f32(wkv_a.T),
        "bkva_cols": _f32(bkv_a[:KVLR].reshape(4, 128).T),
        "bkva_rope": _f32(bkv_a[KVLR:].reshape(1, 64).T),
        "woutT": _f32(wout.T),
        "bout_row": _f32(bout.reshape(1, D)),
        "csA_t": csA,
        "csB_t": csB,
        "mask_h": _bf(mask),
        "eye_h": _bf(np.eye(128, dtype=np.float32)),
    }
    if PHASES >= 3:
        shared["w_qkv"] = _f32(inputs["L1_attn_w_qkv"])
        shared["w_o"] = _f32(inputs["L1_attn_w_o"])
    if PHASES >= 4:
        shared["w1T"] = _f32(np.asarray(inputs["L1_ffn_W1"]).T)
        shared["b1_cols"] = _f32(np.asarray(inputs["L1_ffn_b1"]).reshape(32, 128).T)
        shared["w2T"] = _f32(np.asarray(inputs["L1_ffn_W2"]).T)
        shared["b2_cols"] = _f32(np.asarray(inputs["L1_ffn_b2"]).reshape(32, 128).T)
        shared["w3T"] = _f32(np.asarray(inputs["L1_ffn_W3"]).T)
        shared["b3_row"] = _f32(np.asarray(inputs["L1_ffn_b3"]).reshape(1, D))

    in_maps = []
    for r in range(NCORE):
        m = dict(shared)
        m["x_own"] = _f32(x[r * TOK:(r + 1) * TOK])
        m["csA_own"] = _f32(csA[:, r * TOK:(r + 1) * TOK])
        m["csB_own"] = _f32(csB[:, r * TOK:(r + 1) * TOK])
        h0 = 2 * r
        m["wqbT"] = _f32(wq_b[h0 * DQK:(h0 + 2) * DQK, :].T)
        bslice = bq_b[h0 * DQK:(h0 + 2) * DQK].reshape(2, DQK)
        m["bqb_rope"] = _f32(bslice[:, :DR].T)
        m["bqb_nope"] = _f32(bslice[:, DR:].T)
        m["wkvbT"] = _f32(wkv_b[h0 * (DN + DV):(h0 + 2) * (DN + DV), :].T)
        if PHASES >= 2:
            gup = np.asarray(inputs["L0_ffn_gate_up_packed"])
            gsc = np.asarray(inputs["L0_ffn_gate_up_scales"])
            dpk = np.asarray(inputs["L0_ffn_down_packed"])
            dsc = np.asarray(inputs["L0_ffn_down_scales"])
            m["gup_pk"] = np.ascontiguousarray(gup[r * (GU // 8):(r + 1) * (GU // 8)])
            m["gup_scT"] = np.ascontiguousarray(gsc.T[r * (GU // 8):(r + 1) * (GU // 8)])
            m["dn_pk"] = np.ascontiguousarray(dpk[r * (D // 8):(r + 1) * (D // 8)])
            m["dn_scT"] = np.ascontiguousarray(dsc.T[r * (D // 8):(r + 1) * (D // 8)])
        kb = bkv_b[h0 * (DN + DV):(h0 + 2) * (DN + DV)].reshape(2, DN + DV)
        m["bkvb_nope"] = _f32(kb[:, :DN].T)
        m["bkvb_v"] = _f32(kb[:, DN:].reshape(1, 2 * DV))
        in_maps.append(m)
    return in_maps


def kernel(**inputs):
    in_maps = _prep_inputs(inputs)
    nc = build_program()
    res = run_bass_kernel_spmd(nc, in_maps, core_ids=list(range(NCORE)))
    rows = [res.results[r]["out"] for r in range(NCORE)]
    return np.concatenate(rows, axis=0).reshape(1, T, D).astype(np.float32)


# version 34
# speedup vs baseline: 1.1436x; 1.1200x over previous
import math
import numpy as np
import ml_dtypes
import concourse.bass as bass
import concourse.mybir as mybir
from concourse.bass_utils import run_bass_kernel_spmd

F32 = mybir.dt.float32
BF16 = mybir.dt.bfloat16
F16 = mybir.dt.float16
I32 = mybir.dt.int32
I16 = mybir.dt.int16
AF = mybir.ActivationFunctionType
ALU = mybir.AluOpType
AX = mybir.AxisListType

NCORE = 8
T = 8192
D = 2048
TOK = T // NCORE
NT = TOK // 128
H0 = 16
QLR, KVLR = 1024, 512
DN, DR, DV = 128, 64, 128
DQK = DN + DR
I0 = 5632
GU = 2 * I0
EPS = 1e-5

PHASES = 4
DEBUG = False
ENGINES = ("sync", "vector", "scalar", "tensor", "gpsimd")


class Plan:

    NLANES = 8

    class Op:
        __slots__ = ("fn", "eng", "is_dma", "marked", "waits", "count",
                     "lane", "lane_k")

        def __init__(self, fn, eng, is_dma):
            self.fn, self.eng, self.is_dma = fn, eng, is_dma
            self.marked = False
            self.waits = []
            self.count = None
            self.lane = None
            self.lane_k = None

    def __init__(self, nc):
        self.nc = nc
        self.ops = {e: [] for e in ENGINES}
        self.ord = {e: nc.alloc_semaphore(name=f"ord_{e}") for e in ENGINES}
        self.dma_lanes = {e: [nc.alloc_semaphore(name=f"dma_{e}_{i}")
                              for i in range(Plan.NLANES)]
                          for e in ENGINES}
        self._dma_cnt = {e: 0 for e in ENGINES}
        self._lane_cnt = {e: [0] * Plan.NLANES for e in ENGINES}

    def op(self, eng, fn, deps=(), dma=False):
        o = Plan.Op(fn, eng, dma)
        if dma:
            i = self._dma_cnt[eng]
            self._dma_cnt[eng] = i + 1
            o.lane = i % Plan.NLANES
            self._lane_cnt[eng][o.lane] += 1
            o.lane_k = self._lane_cnt[eng][o.lane]
        for d in deps:
            if d is None:
                continue
            if isinstance(d, (list, tuple)):
                for dd in d:
                    if dd is not None:
                        dd.marked = True
                        o.waits.append(dd)
            else:
                d.marked = True
                o.waits.append(d)
        self.ops[eng].append(o)
        return o

    def dmaop(self, eng, fn, deps=()):
        o = self.op(eng, fn, deps, dma=True)
        o.marked = True
        return o

    def barrier(self):
        lasts = [self.ops[e][-1] for e in ENGINES if self.ops[e]]
        outs = []
        for e in ENGINES:
            outs.append(self.op(e, "nop", deps=lasts))
        return outs

    def emit(self):
        for e in ENGINES:
            c_ord, c_dma = 0, 0
            for o in self.ops[e]:
                if o.marked:
                    if o.is_dma:
                        c_dma += 16
                        o.count = c_dma
                    else:
                        c_ord += 1
                        o.count = c_ord

        def emit_engine(engine_obj, eng_name):
            for o in self.ops[eng_name]:
                need = {}
                for p in o.waits:
                    if p.is_dma:
                        sem = self.dma_lanes[p.eng][p.lane]
                        val = 16 * p.lane_k
                    else:
                        sem = self.ord[p.eng]
                        val = p.count
                    key = id(sem)
                    if key not in need or need[key][1] < val:
                        need[key] = (sem, val)
                if o.is_dma and o.lane_k > 1:
                    key = id(self.dma_lanes[eng_name][o.lane])
                    v = 16 * (o.lane_k - 1)
                    if key not in need or need[key][1] < v:
                        need[key] = (self.dma_lanes[eng_name][o.lane], v)
                for sem, val in need.values():
                    engine_obj.wait_ge(sem, val)
                inst = engine_obj.nop() if o.fn == "nop" else o.fn(engine_obj)
                if o.is_dma:
                    inst.then_inc(self.dma_lanes[eng_name][o.lane], 16)
                elif o.marked:
                    inst.then_inc(self.ord[eng_name], 1)

        with self.nc.Block() as block:
            @block.sync
            def _(sync):
                emit_engine(sync, "sync")

            @block.vector
            def _(vector):
                emit_engine(vector, "vector")

            @block.scalar
            def _(scalar):
                emit_engine(scalar, "scalar")

            @block.tensor
            def _(tensor):
                emit_engine(tensor, "tensor")

            @block.gpsimd
            def _(gpsimd):
                emit_engine(gpsimd, "gpsimd")


def build_program():
    from contextlib import ExitStack
    nc = bass.Bass("TRN2", target_bir_lowering=False)
    P = Plan(nc)
    top = ExitStack()

    def sb(stack, name, shape, dt):
        return stack.enter_context(nc.sbuf_tensor(name, shape, dt))

    def psf(stack, name, banks):
        return stack.enter_context(nc.psum_tensor(name, [128, 512 * banks], F32))

    def psb(stack, name, banks):
        return stack.enter_context(nc.psum_tensor(name, [128, 1024 * banks], BF16))

    def din(name, shape, dt):
        return nc.dram_tensor(name, shape, dt, kind="ExternalInput").ap()

    def dint(name, shape, dt):
        return nc.dram_tensor(name, shape, dt).ap()

    x_in = din("x_own", [TOK, D], F32)
    wqaT = din("wqaT", [D, QLR], F32)
    bqa_row = din("bqa_row", [1, QLR], F32)
    gq_bc = din("gq_bc", [128, QLR], F32)
    bq_bc = din("bq_bc", [128, QLR], F32)
    wqbT = din("wqbT", [QLR, 2 * DQK], F32)
    bqb_nope = din("bqb_nope", [128, 2], F32)
    bqb_rope = din("bqb_rope", [64, 2], F32)
    wkvaT = din("wkvaT", [D, KVLR + DR], F32)
    bkva_cols = din("bkva_cols", [128, 4], F32)
    bkva_rope = din("bkva_rope", [64, 1], F32)
    wkvbT = din("wkvbT", [KVLR, 2 * (DN + DV)], F32)
    bkvb_nope = din("bkvb_nope", [128, 2], F32)
    bkvb_v = din("bkvb_v", [1, 2 * DV], F32)
    woutT = din("woutT", [D, D], F32)
    bout_row = din("bout_row", [1, D], F32)
    csA_t = din("csA_t", [64, T], F32)
    csB_t = din("csB_t", [64, T], F32)
    csA_own = din("csA_own", [64, TOK], F32)
    csB_own = din("csB_own", [64, TOK], F32)
    mask_h = din("mask_h", [128, 1920], BF16)
    eye_h = din("eye_h", [128, 128], BF16)
    if PHASES >= 2:
        gup_pk = din("gup_pk", [GU // 8, 256], I32)
        gup_scT = din("gup_scT", [GU // 8, 16], F16)
        dn_pk = din("dn_pk", [D // 8, 704], I32)
        dn_scT = din("dn_scT", [D // 8, 44], F16)
    if PHASES >= 3:
        w_qkv = din("w_qkv", [D, 3 * D], F32)
        w_o = din("w_o", [D, D], F32)
    if PHASES >= 4:
        w1T = din("w1T", [D, 4096], F32)
        b1_cols = din("b1_cols", [128, 32], F32)
        w2T = din("w2T", [4096, 4096], F32)
        b2_cols = din("b2_cols", [128, 32], F32)
        w3T = din("w3T", [4096, D], F32)
        b3_row = din("b3_row", [1, D], F32)

    out = nc.dram_tensor("out", [TOK, D], F32, kind="ExternalOutput").ap()
    if DEBUG:
        dbg_gout = nc.dram_tensor("dbg_gout", [NCORE * 1600, TOK], BF16,
                                  kind="ExternalOutput").ap()
        dbg_y = nc.dram_tensor("dbg_y", [2 * DV, T], BF16,
                               kind="ExternalOutput").ap()
        dbg_attn = nc.dram_tensor("dbg_attn", [TOK, D], F32,
                                  kind="ExternalOutput").ap()
        dbg_yf = nc.dram_tensor("dbg_yf", [NCORE * 2 * DV, T], BF16,
                                kind="ExternalOutput").ap()
        if PHASES >= 2:
            dbg_wg = nc.dram_tensor("dbg_wg", [NCORE * D, GU // 8], BF16,
                                    kind="ExternalOutput").ap()
            dbg_wd = nc.dram_tensor("dbg_wd", [NCORE * I0, D // 8], BF16,
                                    kind="ExternalOutput").ap()
            dbg_ffn = nc.dram_tensor("dbg_ffn", [512, D], F32,
                                     kind="ExternalOutput").ap()

    gin_q = dint("gin_q", [QLR, TOK], BF16)
    gout_q = dint("gout_q", [NCORE * QLR, TOK], BF16)
    gin_k = dint("gin_k", [KVLR + DR, TOK], BF16)
    gout_k = dint("gout_k", [NCORE * (KVLR + DR), TOK], BF16)
    ygath_a = dint("ygath_a", [DV, T], BF16)
    yfull_a = dint("yfull_a", [NCORE * DV, T], BF16)
    ygath_b = dint("ygath_b", [DV, T], BF16)
    yfull_b = dint("yfull_b", [NCORE * DV, T], BF16)
    if PHASES >= 2:
        ffn_dram = dint("ffn_dram", [512, D], F32)
        wg_gin = dint("wg_gin", [D, GU // 8], BF16)
        wg_gout = dint("wg_gout", [NCORE * D, GU // 8], BF16)
        wd_gin = dint("wd_gin", [I0, D // 8], BF16)
        wd_gout = dint("wd_gout", [NCORE * I0, D // 8], BF16)
    if PHASES >= 3:
        kin = dint("kin", [D, TOK], BF16)
        kfull = dint("kfull", [NCORE * D, TOK], BF16)
        rs_in = dint("rs_in", [NCORE * H0, TOK], F32)
        rs_out = dint("rs_out", [H0, TOK], F32)
        vdram = dint("vdram", [D, TOK], BF16)
    if PHASES >= 4:
        ffnD_dram = dint("ffnD_dram", [512, D], F32)
        n4_dram = dint("n4_dram", [D, TOK], BF16)


    xres = sb(top, "xres", [128, NT * D], F32)
    rt_sh1 = sb(top, "rt_sh1", [32, 512], F32)
    rt_sh2 = sb(top, "rt_sh2", [32, 512], F32)
    ones_row = sb(top, "ones_row", [1, 128], BF16)
    ones_col = sb(top, "ones_col", [128, 1], BF16)
    eye_bf = sb(top, "eye_bf", [128, 128], BF16)
    SC = {}

    CONST_VALS = [EPS, 1e-6, 1.0 / math.sqrt(float(DQK)), 1.0 / math.sqrt(128.0)]
    cst_sb = sb(top, "cst_sb", [128, len(CONST_VALS)], F32)
    const_dep = {}
    for i, vv in enumerate(CONST_VALS):
        o = P.op("vector", lambda v, i=i, vv=vv: v.memset(cst_sb[:, i:i + 1], vv))
        nc.const_aps.register(F32, vv, cst_sb[:, i:i + 1])
        const_dep[vv] = o

    P.op("sync", lambda s: (s.partition_id(), s.nop())[1])
    op_ones = P.op("vector", lambda v: v.memset(ones_row[:, :], 1.0))
    op_ones_col = P.op("vector", lambda v: v.memset(ones_col[:, :], 1.0))
    ld_eye = P.dmaop("sync", lambda s: s.dma_start(out=eye_bf[:, :], in_=eye_h))

    def alloc_stats(stack, tag, n):
        return {k: sb(stack, f"{tag}_{k}", [128, n], F32)
                for k in ("s", "sq", "mean", "msq", "var", "rstd")}

    def ln_stats(st, src, cols, deps, d, eps=EPS, war_deps=(), sqd=None):
        sq_dump = SC["sq_dump"] if sqd is None else sqd
        """cols: list of (idx, col_offset). Fills st mean/rstd at [:, idx]."""
        col_ops = []
        for idx, off in cols:
            col_ops.append(P.op("vector", lambda v, idx=idx, off=off: v.reduce_sum(
                st["s"][:, idx:idx + 1], src[:, off:off + d], axis=AX.X),
                deps=list(deps) + list(war_deps)))
            col_ops.append(P.op("scalar", lambda s, idx=idx, off=off, sq_dump=sq_dump: s.activation(
                sq_dump[:, 0:d], src[:, off:off + d], AF.Square,
                accum_out=st["sq"][:, idx:idx + 1]),
                deps=list(deps) + list(war_deps)))
        lo = min(i for i, _ in cols)
        hi = max(i for i, _ in cols) + 1
        m1 = P.op("vector", lambda v: v.tensor_scalar_mul(
            st["mean"][:, lo:hi], st["s"][:, lo:hi], 1.0 / d), deps=col_ops)
        m2 = P.op("vector", lambda v: v.tensor_scalar_mul(
            st["msq"][:, lo:hi], st["sq"][:, lo:hi], 1.0 / d), deps=col_ops)
        m3 = P.op("vector", lambda v: v.tensor_mul(
            st["var"][:, lo:hi], st["mean"][:, lo:hi], st["mean"][:, lo:hi]),
            deps=[m1])
        m4 = P.op("vector", lambda v: v.tensor_sub(
            st["var"][:, lo:hi], st["msq"][:, lo:hi], st["var"][:, lo:hi]),
            deps=[m2, m3])
        m5 = P.op("scalar", lambda s: s.activation(
            st["rstd"][:, lo:hi], st["var"][:, lo:hi], AF.Ln, bias=eps),
            deps=[m4, const_dep[eps]] + list(war_deps))
        m6 = P.op("scalar", lambda s: s.activation(
            st["rstd"][:, lo:hi], st["rstd"][:, lo:hi], AF.Exp, scale=-0.5),
            deps=[m5])
        return [m6]

    tr_prev = {}

    def transpose_to(ps_t, slot, src_bf, src_off, dst_bf, dst_off, deps):
        tr = P.op("tensor", lambda te: te.transpose(
            ps_t[:, slot * 1024:slot * 1024 + 128],
            src_bf[:, src_off:src_off + 128], eye_bf[:, :]),
            deps=list(deps) + [ld_eye, tr_prev.get((id(ps_t), slot))])
        ev = P.op("scalar", lambda s: s.copy(
            dst_bf[:, dst_off:dst_off + 128],
            ps_t[:, slot * 1024:slot * 1024 + 128]), deps=[tr])
        tr_prev[(id(ps_t), slot)] = ev
        return ev

    def rope_ops(dst_bf, dst_off, src_f32, src_off, csA, csB, cs_off, w,
                 deps, t1, t2):
        o1 = P.op("vector", lambda v: v.tensor_mul(
            t1[:, 0:w], src_f32[0:32, src_off:src_off + w],
            csA[0:32, cs_off:cs_off + w]), deps=deps)
        o2 = P.op("vector", lambda v: v.tensor_mul(
            t2[:, 0:w], src_f32[32:64, src_off:src_off + w],
            csA[32:64, cs_off:cs_off + w]), deps=deps)
        o3 = P.op("vector", lambda v: v.tensor_sub(
            dst_bf[0:32, dst_off:dst_off + w], t1[:, 0:w], t2[:, 0:w]),
            deps=[o1, o2])
        o4 = P.op("vector", lambda v: v.tensor_mul(
            t1[:, 0:w], src_f32[32:64, src_off:src_off + w],
            csB[32:64, cs_off:cs_off + w]), deps=[o3])
        o5 = P.op("vector", lambda v: v.tensor_mul(
            t2[:, 0:w], src_f32[0:32, src_off:src_off + w],
            csB[0:32, cs_off:cs_off + w]), deps=[o3])
        o6 = P.op("vector", lambda v: v.tensor_add(
            dst_bf[32:64, dst_off:dst_off + w], t1[:, 0:w], t2[:, 0:w]),
            deps=[o4, o5])
        return o6

    A1 = ExitStack()
    SC["sq_dump"] = sb(A1, "sq_dumpA", [128, D], F32)
    ln_tmp = SC["ln_tmp"] = sb(A1, "ln_tmpA", [128, QLR], F32)
    tm_bf = SC["tm_bf"] = sb(A1, "tm_bfA", [128, D], BF16)
    n1_fm = sb(A1, "n1_fm", [128, 16 * TOK], BF16)
    tr_ps = psb(A1, "tr_ps", 2)

    x_loads = [P.dmaop("sync", lambda s, t=t: s.dma_start(
        out=xres[:, t * D:(t + 1) * D], in_=x_in[t * 128:(t + 1) * 128, :]))
        for t in range(NT)]

    stA = alloc_stats(A1, "stA", NT)
    stQL = alloc_stats(A1, "stQL", NT)
    n1_stats = ln_stats(stA, xres, [(t, t * D) for t in range(NT)], x_loads, D)
    n1_done = []
    prev_apply = None
    for t in range(NT):
        a = P.op("vector", lambda v, t=t, tm_bf=tm_bf: v.tensor_scalar(
            tm_bf[:, 0:D], xres[:, t * D:(t + 1) * D],
            stA["mean"][:, t:t + 1], stA["rstd"][:, t:t + 1],
            ALU.subtract, ALU.mult), deps=n1_stats + [prev_apply])
        last = None
        for ic in range(16):
            last = transpose_to(tr_ps, ic % 2, tm_bf, ic * 128,
                                n1_fm, ic * TOK + t * 128, [a])
        prev_apply = last
        n1_done.append(last)
    n1_all = n1_done[-1]

    ql_bf = sb(A1, "ql_bf", [128, NT * QLR], BF16)
    qlb_bf = sb(A1, "qlb_bf", [128, 2 * 128], BF16)
    bqa_bf = sb(A1, "bqa_bf", [1, QLR], BF16)
    gq_sb = sb(A1, "gq_sb", [128, QLR], BF16)
    bq_sb = sb(A1, "bq_sb", [128, QLR], BF16)
    bkva_sb = sb(A1, "bkva_sb", [128, 4], F32)
    bkvar_sb = sb(A1, "bkvar_sb", [64, 1], F32)
    kv_fm = sb(A1, "kv_fm", [128, 4 * TOK], BF16)
    kvr_f32 = sb(A1, "kvr_f32", [64, 512], F32)
    kvr_bf = sb(A1, "kvr_bf", [64, TOK], BF16)
    cskA_sb = sb(A1, "cskA_sb", [64, 512], F32)
    cskB_sb = sb(A1, "cskB_sb", [64, 512], F32)
    rt1 = rt_sh1
    rt2 = rt_sh2
    ql_ps = psf(A1, "ql_ps", 2)
    kv_ps = psf(A1, "kv_ps", 2)
    wqa_bf = sb(A1, "wqa_bf", [128, 16 * QLR], BF16)
    wkva_bf = sb(A1, "wkva_bf", [128, 16 * (KVLR + DR)], BF16)

    ld_wqa = P.dmaop("gpsimd", lambda g: g.dma_start(
        out=wqa_bf[:, :].rearrange("p (a o) -> p a o", o=QLR), in_=wqaT.rearrange("(a p) o -> p a o", p=128)))
    ld_wkva = P.dmaop("gpsimd", lambda g: g.dma_start(
        out=wkva_bf[:, :].rearrange("p (a o) -> p a o", o=KVLR + DR), in_=wkvaT.rearrange("(a p) o -> p a o", p=128)))
    ld_bqa = P.dmaop("gpsimd", lambda g: g.dma_start(out=bqa_bf[:, :], in_=bqa_row))
    pc_wout = pc_wqkv = pc_wo = pc_w1 = pc_w2 = pc_w3 = None
    ld_gq = P.dmaop("gpsimd", lambda g: g.dma_start(out=gq_sb[:, :], in_=gq_bc))
    ld_bq = P.dmaop("gpsimd", lambda g: g.dma_start(out=bq_sb[:, :], in_=bq_bc))
    ld_bkva = P.dmaop("sync", lambda s: s.dma_start(out=bkva_sb[:, :], in_=bkva_cols))
    ld_bkvar = P.dmaop("sync", lambda s: s.dma_start(out=bkvar_sb[:, :], in_=bkva_rope))

    ql_evacs = []
    ps_user = [None, None]
    for t in range(NT):
        for oc in range(2):
            slot = (t * 2 + oc) % 2
            mm = None
            for ic in range(16):
                deps = [ld_wqa, n1_done[t], ps_user[slot]] if ic == 0 else []
                mm = P.op("tensor", lambda te, t=t, oc=oc, ic=ic, slot=slot,
                          first=(ic == 0): te.matmul(
                    ql_ps[:, slot * 512:slot * 512 + 512],
                    n1_fm[:, ic * TOK + t * 128:ic * TOK + t * 128 + 128],
                    wqa_bf[:, ic * QLR + oc * 512:ic * QLR + oc * 512 + 512],
                    start=first, stop=False), deps=deps)
            mmb = P.op("tensor", lambda te, oc=oc, slot=slot: te.matmul(
                ql_ps[:, slot * 512:slot * 512 + 512], ones_row[:, :],
                bqa_bf[:, oc * 512:oc * 512 + 512],
                start=False, stop=True), deps=[mm, ld_bqa, op_ones])
            ev = P.op("vector", lambda v, t=t, oc=oc, slot=slot: v.tensor_copy(
                ql_bf[:, t * QLR + oc * 512:t * QLR + oc * 512 + 512],
                ql_ps[:, slot * 512:slot * 512 + 512]), deps=[mmb])
            ps_user[slot] = ev
            ql_evacs.append(ev)

    ql_stats = ln_stats(stQL, ql_bf, [(t, t * QLR) for t in range(NT)],
                        [ql_evacs[-1]], QLR)
    prev_tr = None
    ql_done = []
    qlb_user = [None, None]
    gin_ql_dmas = []
    for t in range(NT):
        a1 = P.op("vector", lambda v, t=t, ln_tmp=ln_tmp: v.scalar_tensor_tensor(
            ln_tmp[:, 0:QLR], ql_bf[:, t * QLR:(t + 1) * QLR],
            stQL["mean"][:, t:t + 1], gq_sb[:, :],
            ALU.subtract, ALU.mult), deps=ql_stats + [ld_gq])
        a3 = P.op("vector", lambda v, t=t, ln_tmp=ln_tmp, tm_bf=tm_bf: v.scalar_tensor_tensor(
            tm_bf[:, 0:QLR], ln_tmp[:, 0:QLR],
            stQL["rstd"][:, t:t + 1], bq_sb[:, :],
            ALU.mult, ALU.add), deps=[a1, ld_bq, prev_tr])
        last = None
        for ic in range(8):
            sl8 = ic % 2
            ev = transpose_to(tr_ps, sl8, tm_bf, ic * 128,
                              qlb_bf, sl8 * 128,
                              [a3, qlb_user[sl8]])
            dd = P.dmaop("scalar", lambda sc, ic=ic, t=t, sl8=sl8: sc.dma_start(
                out=gin_q[ic * 128:(ic + 1) * 128, t * 128:(t + 1) * 128],
                in_=qlb_bf[:, sl8 * 128:sl8 * 128 + 128]), deps=[ev])
            qlb_user[sl8] = dd
            gin_ql_dmas.append(dd)
            last = ev
        prev_tr = last
        ql_done.append(last)

    kv_evacs = []
    ps_user = [None, None]
    rope_ev = []
    rope_k = []
    for cc in range(5):
        width = 64 if cc == 4 else 128
        for c in range(2):
            slot = (cc * 2 + c) % 2
            mm = None
            for ic in range(16):
                deps = [ld_wkva, n1_all, ps_user[slot]] if ic == 0 else []
                mm = P.op("tensor", lambda te, cc=cc, width=width, c=c, ic=ic,
                          slot=slot, first=(ic == 0): te.matmul(
                    kv_ps[0:width, slot * 512:slot * 512 + 512],
                    wkva_bf[:, ic * 576 + cc * 128:ic * 576 + cc * 128 + width],
                    n1_fm[:, ic * TOK + c * 512:ic * TOK + c * 512 + 512],
                    start=first, stop=(ic == 15)), deps=deps)
            if width == 128:
                ev = P.op("scalar", lambda s, cc=cc, c=c, slot=slot: s.activation(
                    kv_fm[:, cc * TOK + c * 512:cc * TOK + c * 512 + 512],
                    kv_ps[:, slot * 512:slot * 512 + 512], AF.Identity,
                    bias=bkva_sb[:, cc:cc + 1]), deps=[mm, ld_bkva])
            else:
                ev = P.op("scalar", lambda s, c=c, slot=slot: s.activation(
                    kvr_f32[:, 0:512],
                    kv_ps[0:64, slot * 512:slot * 512 + 512], AF.Identity,
                    bias=bkvar_sb[:, 0:1]), deps=[mm, ld_bkvar] + rope_k)
                rope_ev.append(ev)
                ld_ck = P.dmaop("sync", lambda s2, c=c: s2.dma_start(
                    out=cskA_sb[:, :], in_=csA_own[:, c * 512:(c + 1) * 512]),
                    deps=rope_k)
                ld_sk = P.dmaop("sync", lambda s2, c=c: s2.dma_start(
                    out=cskB_sb[:, :], in_=csB_own[:, c * 512:(c + 1) * 512]),
                    deps=rope_k)
                rope_k.append(rope_ops(kvr_bf, c * 512, kvr_f32, 0,
                                       cskA_sb, cskB_sb, 0, 512,
                                       [ev, ld_ck, ld_sk], rt1, rt2))
            ps_user[slot] = ev
            kv_evacs.append(ev)


    ag1q = P.op("gpsimd", lambda g: g.collective_compute(
        "AllGather", ALU.bypass, replica_groups=[list(range(NCORE))],
        ins=[gin_q.opt()], outs=[gout_q.opt()]), deps=gin_ql_dmas)
    gin_dmas = []
    for cc in range(4):
        gin_dmas.append(P.dmaop("scalar", lambda g, cc=cc: g.dma_start(
            out=gin_k[cc * 128:(cc + 1) * 128, :],
            in_=kv_fm[:, cc * TOK:(cc + 1) * TOK]), deps=[kv_evacs[-1]]))
    gin_dmas.append(P.dmaop("scalar", lambda g: g.dma_start(
        out=gin_k[KVLR:KVLR + DR, :], in_=kvr_bf[:, :]),
        deps=rope_k))
    ag1k = P.op("gpsimd", lambda g: g.collective_compute(
        "AllGather", ALU.bypass, replica_groups=[list(range(NCORE))],
        ins=[gin_k.opt()], outs=[gout_k.opt()]), deps=gin_dmas)

    barA1 = P.barrier()
    A1.close()
    brow_bf = sb(top, "brow_bf", [1, D], BF16)

    A3 = ExitStack()
    wqb_bf = sb(A3, "wqb_bf", [128, 8 * 192], BF16)
    wkvb_bf = sb(A3, "wkvb_bf", [128, 4 * 256], BF16)
    bqbn_sb = sb(A3, "bqbn_sb", [128, 2], F32)
    bqbr_sb = sb(A3, "bqbr_sb", [64, 2], F32)
    bkvbn_sb = sb(A3, "bkvbn_sb", [128, 2], F32)
    bkvbv_bf = sb(A3, "bkvbv_bf", [1, 2 * DV], BF16)
    csqA_sb = sb(A3, "csqA_sb", [64, 512], F32)
    csqB_sb = sb(A3, "csqB_sb", [64, 512], F32)
    gq_buf = sb(A3, "gq_buf", [128, 2 * 8 * 512], BF16)
    gkv_buf = sb(A3, "gkv_buf", [128, 2 * 4 * 512], BF16)
    Qn = sb(A3, "Qn", [128, T], BF16)
    Qr = sb(A3, "Qr", [64, T], BF16)
    Kn = sb(A3, "Kn", [128, T], BF16)
    Kr = sb(A3, "Kr", [64, T], BF16)
    Vt = sb(A3, "Vt", [128, (T // 128) * DV], BF16)
    qr_f32 = sb(A3, "qr_f32", [64, 512], F32)
    pt_bf = sb(A3, "pt_bf", [128, 2 * 1024], BF16)
    invd_f = sb(A3, "invd_f", [128, 1024], F32)
    yb_bf = sb(A3, "yb_bf", [128, 2 * 1024], BF16)
    rt1b = rt_sh1
    rt2b = rt_sh2
    mask_bf = sb(A3, "mask_bf", [128, 1920], BF16)
    ones128_bf = sb(A3, "ones128_bf", [128, 128], BF16)

    ps_all = psf(A3, "ps_all", 8)
    ld_mask = P.dmaop("sync", lambda s: s.dma_start(out=mask_bf[:, :], in_=mask_h),
                      deps=barA1)
    op_ones128 = P.op("vector", lambda v: v.memset(ones128_bf[:, :], 1.0))

    ld_b1 = P.dmaop("sync", lambda s: s.dma_start(out=bqbn_sb[:, :], in_=bqb_nope), deps=barA1)
    ld_b2 = P.dmaop("sync", lambda s: s.dma_start(out=bqbr_sb[:, :], in_=bqb_rope), deps=barA1)
    ld_b3 = P.dmaop("sync", lambda s: s.dma_start(out=bkvbn_sb[:, :], in_=bkvb_nope), deps=barA1)
    ld_b4 = P.dmaop("gpsimd", lambda g: g.dma_start(out=bkvbv_bf[:, :], in_=bkvb_v), deps=barA1)

    a2a_dmas = []
    yd_byhead = [[], []]
    a2a = [None, None]
    head_bar = barA1
    for hloc in range(2):
        ld_wqb = P.dmaop("gpsimd", lambda g, hloc=hloc: g.dma_start(
            out=wqb_bf[:, :].rearrange("p (a o) -> p a o", o=192),
            in_=wqbT[:, hloc * 192:(hloc + 1) * 192]
                .rearrange("(a p) o -> p a o", p=128)),
            deps=head_bar)
        ld_wkvb = P.dmaop("gpsimd", lambda g, hloc=hloc: g.dma_start(
            out=wkvb_bf[:, :].rearrange("p (a o) -> p a o", o=256),
            in_=wkvbT[:, hloc * 256:(hloc + 1) * 256]
                .rearrange("(a p) o -> p a o", p=128)),
            deps=head_bar)
        if hloc == 1:
            a2a[0] = P.op("gpsimd", lambda g: g.collective_compute(
                "AllGather", ALU.bypass, replica_groups=[list(range(NCORE))],
                ins=[ygath_a.opt()], outs=[yfull_a.opt()]), deps=yd_byhead[0])
        slot_user = [None, None]
        proj_last = []
        prev_evr = None
        prev_evn = None
        kn_prev_ev = None
        v_prev_ev = None
        prev_rope = None
        for c in range(16):
            sl = c % 2
            rk, co = c // 2, (c % 2) * 512
            deps_ld = [slot_user[sl]] + head_bar
            lds = []
            for ic in range(8):
                lds.append(P.dmaop("sync", lambda s, ic=ic, rk=rk, co=co, sl=sl: s.dma_start(
                    out=gq_buf[:, (sl * 8 + ic) * 512:(sl * 8 + ic) * 512 + 512],
                    in_=gout_q[rk * QLR + ic * 128:rk * QLR + (ic + 1) * 128, co:co + 512]),
                    deps=deps_ld + [ag1q]))
            for ic in range(4):
                lds.append(P.dmaop("sync", lambda s, ic=ic, rk=rk, co=co, sl=sl: s.dma_start(
                    out=gkv_buf[:, (sl * 4 + ic) * 512:(sl * 4 + ic) * 512 + 512],
                    in_=gout_k[rk * 576 + ic * 128:rk * 576 + (ic + 1) * 128, co:co + 512]),
                    deps=deps_ld + [ag1k]))
            ld_kr = P.dmaop("sync", lambda s, c=c, rk=rk, co=co: s.dma_start(
                out=Kr[:, c * 512:c * 512 + 512],
                in_=gout_k[rk * 576 + KVLR:rk * 576 + KVLR + DR, co:co + 512]),
                deps=deps_ld + [ag1k])
            lds.append(ld_kr)
            ld_cq = P.dmaop("sync", lambda s, c=c: s.dma_start(
                out=csqA_sb[:, :], in_=csA_t[:, c * 512:(c + 1) * 512]),
                deps=[prev_rope] + head_bar)
            ld_sq = P.dmaop("sync", lambda s, c=c: s.dma_start(
                out=csqB_sb[:, :], in_=csB_t[:, c * 512:(c + 1) * 512]),
                deps=[prev_rope] + head_bar)

            mm = None
            for ic in range(8):
                deps = (lds + [ld_wqb, prev_evr] + head_bar) if ic == 0 else []
                mm = P.op("tensor", lambda te, ic=ic, sl=sl, hloc=hloc,
                          first=(ic == 0): te.matmul(
                    ps_all[0:64, 0:512],
                    wqb_bf[:, ic * 192:ic * 192 + 64],
                    gq_buf[:, (sl * 8 + ic) * 512:(sl * 8 + ic) * 512 + 512],
                    start=first, stop=(ic == 7)), deps=deps)
            evr = P.op("scalar", lambda s, hloc=hloc: s.activation(
                qr_f32[:, 0:512], ps_all[0:64, 0:512], AF.Identity,
                bias=bqbr_sb[:, hloc:hloc + 1]), deps=[mm, ld_b2, prev_rope])
            prev_evr = evr
            rp = rope_ops(Qr, c * 512, qr_f32, 0, csqA_sb, csqB_sb, 0,
                          512, [evr, ld_cq, ld_sq], rt1b, rt2b)
            prev_rope = rp
            mm2 = None
            for ic in range(8):
                deps = ([prev_evn] if prev_evn is not None else (lds + [ld_wqb])) if ic == 0 else []
                mm2 = P.op("tensor", lambda te, ic=ic, sl=sl, hloc=hloc,
                           first=(ic == 0): te.matmul(
                    ps_all[:, 512:1024],
                    wqb_bf[:, ic * 192 + 64:ic * 192 + 192],
                    gq_buf[:, (sl * 8 + ic) * 512:(sl * 8 + ic) * 512 + 512],
                    start=first, stop=(ic == 7)), deps=deps)
            evn = P.op("scalar", lambda s, c=c, hloc=hloc: s.activation(
                Qn[:, c * 512:c * 512 + 512], ps_all[:, 512:1024], AF.Identity,
                bias=bqbn_sb[:, hloc:hloc + 1]), deps=[mm2, ld_b1])
            prev_evn = evn
            mm3 = None
            for ic in range(4):
                deps = (lds + [ld_wkvb, kn_prev_ev] + head_bar) if ic == 0 else []
                mm3 = P.op("tensor", lambda te, ic=ic, sl=sl, hloc=hloc,
                           first=(ic == 0): te.matmul(
                    ps_all[:, 1024:1536],
                    wkvb_bf[:, ic * 256:ic * 256 + 128],
                    gkv_buf[:, (sl * 4 + ic) * 512:(sl * 4 + ic) * 512 + 512],
                    start=first, stop=(ic == 3)), deps=deps)
            kn_ev = P.op("scalar", lambda s, c=c, hloc=hloc: s.activation(
                Kn[:, c * 512:c * 512 + 512], ps_all[:, 1024:1536], AF.Identity,
                bias=bkvbn_sb[:, hloc:hloc + 1]), deps=[mm3, ld_b3])
            kn_prev_ev = kn_ev
            v_evs = []
            last_v_mm = None
            for s4 in range(4):
                tt = c * 4 + s4
                mm4 = None
                for ic in range(4):
                    deps = ([v_prev_ev, ld_b4, op_ones] + head_bar) if ic == 0 else []
                    mm4 = P.op("tensor", lambda te, ic=ic, sl=sl, s4=s4, hloc=hloc,
                               first=(ic == 0): te.matmul(
                        ps_all[:, 1536:1664],
                        gkv_buf[:, (sl * 4 + ic) * 512 + s4 * 128:(sl * 4 + ic) * 512 + s4 * 128 + 128],
                        wkvb_bf[:, ic * 256 + 128:ic * 256 + 256],
                        start=first, stop=False), deps=deps)
                mm4b = P.op("tensor", lambda te, hloc=hloc: te.matmul(
                    ps_all[:, 1536:1664], ones_row[:, :],
                    bkvbv_bf[:, hloc * DV:hloc * DV + DV],
                    start=False, stop=True), deps=[mm4])
                last_v_mm = mm4b
                v_ev = P.op("vector", lambda v, tt=tt: v.tensor_copy(
                    Vt[:, tt * DV:tt * DV + DV], ps_all[:, 1536:1664]), deps=[mm4b])
                v_prev_ev = v_ev
                v_evs.append(v_ev)
            slot_user[sl] = [last_v_mm, evn]
            proj_last += [rp, evn, kn_ev, ld_kr] + v_evs

        ps_s_user = [None, None]
        pt_user = [None, None]
        y_user = None
        dn_user = None
        yb_user = [None, None]
        for qc in range(8):
            nkt = min((qc + 1) * 8, 64)
            av_last = None
            dn_last = None
            pend = None
            for kt in range(nkt):
                sl = kt % 2
                bdry = (kt * 128 >= qc * 1024)
                mm_last = []
                for h2 in range(2):
                    co = h2 * 512
                    deps = ([ps_s_user[sl]] + (proj_last if (qc == 0 and kt == 0)
                                               else [])) if h2 == 0 else []
                    smm1 = P.op("tensor", lambda te, kt=kt, qc=qc, sl=sl, co=co: te.matmul(
                        ps_all[:, sl * 1024 + co:sl * 1024 + co + 512],
                        Kn[:, kt * 128:kt * 128 + 128],
                        Qn[:, qc * 1024 + co:qc * 1024 + co + 512],
                        start=True, stop=False), deps=deps)
                    smm2 = P.op("tensor", lambda te, kt=kt, qc=qc, sl=sl, co=co,
                                bdry=bdry: te.matmul(
                        ps_all[:, sl * 1024 + co:sl * 1024 + co + 512],
                        Kr[:, kt * 128:kt * 128 + 128],
                        Qr[:, qc * 1024 + co:qc * 1024 + co + 512],
                        start=False, stop=not bdry), deps=[smm1])
                    mm = smm2
                    if bdry:
                        r0 = 896 - (kt * 128 - qc * 1024) + co
                        mm = P.op("tensor", lambda te, sl=sl, r0=r0, co=co: te.matmul(
                            ps_all[:, sl * 1024 + co:sl * 1024 + co + 512],
                            eye_bf[:, :], mask_bf[:, r0:r0 + 512],
                            start=False, stop=True), deps=[smm2, ld_mask, ld_eye])
                    mm_last.append(mm)
                if pend is not None:
                    dn_last, av_last = pend()
                    pend = None
                ex = P.op("scalar", lambda s, sl=sl: s.activation(
                    pt_bf[:, sl * 1024:sl * 1024 + 1024],
                    ps_all[:, sl * 1024:sl * 1024 + 1024], AF.Exp,
                    scale=1.0 / math.sqrt(float(DQK))),
                    deps=mm_last + [pt_user[sl], const_dep[1.0 / math.sqrt(float(DQK))]])
                ps_s_user[sl] = ex

                def mk_pend(kt=kt, sl=sl, ex=ex, qc=qc, nkt=nkt):
                    def emit():
                        dnl, avl = [], []
                        for h2 in range(2):
                            co = h2 * 512
                            dnl.append(P.op("tensor", lambda te, co=co: te.matmul(
                                ps_all[0:128, 3072 + co:3072 + co + 512],
                                ones128_bf[:, :],
                                pt_bf[:, sl * 1024 + co:sl * 1024 + co + 512],
                                start=(kt == 0), stop=(kt == nkt - 1)),
                                deps=[ex, op_ones128] + ([dn_user] if kt == 0 and h2 == 0 else [])))
                            avl.append(P.op("tensor", lambda te, co=co: te.matmul(
                                ps_all[:, 2048 + co:2048 + co + 512],
                                Vt[:, kt * DV:kt * DV + DV],
                                pt_bf[:, sl * 1024 + co:sl * 1024 + co + 512],
                                start=(kt == 0), stop=(kt == nkt - 1)),
                                deps=[ex] + ([y_user] if kt == 0 and h2 == 0 else [])))
                        pt_user[sl] = dnl + avl
                        return dnl, avl
                    return emit
                pend = mk_pend()
            dn_last, av_last = pend()
            rc = P.op("vector", lambda v: v.reciprocal(
                invd_f[:, :], ps_all[:, 3072:4096]), deps=dn_last)
            dn_user = rc
            ysl = qc % 2
            ymul = P.op("vector", lambda v, ysl=ysl: v.tensor_mul(
                yb_bf[:, ysl * 1024:ysl * 1024 + 1024], ps_all[:, 2048:3072],
                invd_f[:, :]), deps=[rc, yb_user[ysl]] + av_last)
            y_user = ymul
            yd = P.dmaop("scalar", lambda g, qc=qc, hloc=hloc, ysl=ysl: g.dma_start(
                out=(ygath_a if hloc == 0 else ygath_b)[:, qc * 1024:qc * 1024 + 1024],
                in_=yb_bf[:, ysl * 1024:ysl * 1024 + 1024]), deps=[ymul])
            yb_user[ysl] = yd
            yd_byhead[hloc].append(yd)
        head_bar = P.barrier()





    if DEBUG:
        P.dmaop("gpsimd", lambda g: g.dma_start(out=dbg_gout, in_=gout), deps=[ag1])
        P.dmaop("gpsimd", lambda g: g.dma_start(out=dbg_y, in_=ygath), deps=a2a_dmas)
        P.dmaop("gpsimd", lambda g: g.dma_start(out=dbg_yf, in_=yfull), deps=[a2a])
    barA3 = P.barrier()
    A3.close()
    SC["sq_dump"] = sqd_sh = sb(top, "sqd_sh", [128, D], F32)

    A5 = ExitStack()
    ln_tmp = sb(A5, "ln_tmpA5", [128, D], F32)
    wout_bf = sb(A5, "wout_bf", [128, 16 * D], BF16)
    ysl_bf = sb(A5, "ysl_bf", [128, 2 * D], BF16)
    attn_tm = sb(A5, "attn_tm", [128, D], F32)
    ps_w = psf(A5, "ps_w", 4)
    stW = alloc_stats(A5, "stW", NT)

    ld_wout = P.dmaop("gpsimd", lambda g: g.dma_start(
        out=wout_bf[:, :].rearrange("p (a o) -> p a o", o=D),
        in_=woutT.rearrange("(a p) o -> p a o", p=128)),
        deps=barA3)
    ld_bout = P.dmaop("gpsimd", lambda g: g.dma_start(out=brow_bf[:, :], in_=bout_row),
                      deps=barA3)
    a2a[1] = P.op("gpsimd", lambda g: g.collective_compute(
        "AllGather", ALU.bypass, replica_groups=[list(range(NCORE))],
        ins=[ygath_b.opt()], outs=[yfull_b.opt()]), deps=yd_byhead[1])
    ysl_user = [None, None]
    RT_OFF = {}
    wps_user = [None] * 4
    res_done = []
    prev_apply = None
    for t in range(NT):
        sl = t % 2
        def off_expr(s, t):
            if "base" not in RT_OFF:
                RT_OFF["base"] = s.partition_id() * TOK
            if t not in RT_OFF:
                RT_OFF[t] = RT_OFF["base"] + t * 128
            return RT_OFF[t]
        def _ldy_a(s, t=t, sl=sl):
            return s.dma_start(
                out=ysl_bf[:, sl * D:sl * D + 1024]
                    .rearrange("p (a c) -> p a c", c=128),
                in_=yfull_a.rearrange("(a p) c -> p a c", p=128)
                    [:, :, bass.ds(off_expr(s, t), 128)])
        def _ldy_b(s, t=t, sl=sl):
            return s.dma_start(
                out=ysl_bf[:, sl * D + 1024:(sl + 1) * D]
                    .rearrange("p (a c) -> p a c", c=128),
                in_=yfull_b.rearrange("(a p) c -> p a c", p=128)
                    [:, :, bass.ds(off_expr(s, t), 128)])
        ldy_a = P.dmaop("sync", _ldy_a, deps=a2a[:1] + [ysl_user[sl]] + barA3)
        ldy = P.dmaop("sync", _ldy_b, deps=a2a[1:] + [ldy_a])
        evs = []
        for oc in range(4):
            mm = None
            for ic in range(16):
                deps = [ldy, ld_wout, wps_user[oc]] if ic == 0 else []
                mm = P.op("tensor", lambda te, oc=oc, ic=ic, sl=sl,
                          first=(ic == 0): te.matmul(
                    ps_w[:, oc * 512:oc * 512 + 512],
                    ysl_bf[:, sl * D + ic * 128:sl * D + ic * 128 + 128],
                    wout_bf[:, ic * D + oc * 512:ic * D + oc * 512 + 512],
                    start=first, stop=False), deps=deps)
            mmb = P.op("tensor", lambda te, oc=oc: te.matmul(
                ps_w[:, oc * 512:oc * 512 + 512], ones_row[:, :],
                brow_bf[:, oc * 512:oc * 512 + 512],
                start=False, stop=True), deps=[mm, ld_bout, op_ones])
            ev = P.op("vector", lambda v, oc=oc: v.tensor_copy(
                attn_tm[:, oc * 512:oc * 512 + 512],
                ps_w[:, oc * 512:oc * 512 + 512]),
                deps=[mmb, prev_apply])
            wps_user[oc] = ev
            evs.append(ev)
        ysl_user[sl] = evs[-1]
        if DEBUG:
            dmp = P.dmaop("gpsimd", lambda g, t=t: g.dma_start(
                out=dbg_attn[t * 128:(t + 1) * 128, :], in_=attn_tm[:, :]),
                deps=evs)
            evs = evs + [dmp]
        stt = ln_stats(stW, attn_tm, [(t, 0)], evs, D,
                       war_deps=[prev_apply])
        ap1 = P.op("vector", lambda v, t=t, ln_tmp=ln_tmp: v.tensor_scalar(
            ln_tmp[:, 0:D], attn_tm[:, 0:D],
            stW["mean"][:, t:t + 1], stW["rstd"][:, t:t + 1],
            ALU.subtract, ALU.mult), deps=stt)
        ap2 = P.op("vector", lambda v, t=t, ln_tmp=ln_tmp: v.tensor_add(
            xres[:, t * D:(t + 1) * D], xres[:, t * D:(t + 1) * D],
            ln_tmp[:, 0:D]), deps=[ap1])
        prev_apply = ap2
        res_done.append(ap2)

    barA5 = P.barrier()
    A5.close()

    if PHASES >= 2:
        B1 = ExitStack()
        pk_sb = sb(B1, "pk_sb", [128, 2 * 704], I32)
        scB_sb = sb(B1, "scB_sb", [128, 44], F16)
        scfB_sb = sb(B1, "scfB_sb", [128, 44], F32)
        dq1 = sb(B1, "dq1", [128, 704], I32)
        dq2 = sb(B1, "dq2", [128, 704], I32)
        dq3 = sb(B1, "dq3", [128, 704], I32)
        dq4 = sb(B1, "dq4", [128, 704], I32)
        dq5 = sb(B1, "dq5", [128, 704], I32)
        bitsB = sb(B1, "bitsB", [128, I0], I16)
        denseB = sb(B1, "denseB", [128, I0], BF16)
        trbB = sb(B1, "trbB", [128, 2 * 128], BF16)
        trB_ps = psb(B1, "trB_ps", 2)

        wgq_dmas = []
        pk_user = [None, None]
        dense_user = None
        sc_user = None
        bits_war = None

        def dequant_chunk(ld_pk, ld_sc, pksl, wi32, ngr, out_dmas, dst, doff_fn):
            nonlocal dense_user, sc_user, bits_war
            w = wi32
            cast = P.op("vector", lambda v: v.tensor_copy(
                scfB_sb[:, 0:ngr], scB_sb[:, 0:ngr]), deps=[ld_sc])
            sc_user = cast
            last_bits = None
            for sg in range(8):
                sh1 = (6 - 4 * sg, ALU.logical_shift_left) if 4 * sg < 6 else                       (4 * sg - 6, ALU.logical_shift_right)
                sh2 = (12 - 4 * sg, ALU.logical_shift_left) if sg <= 3 else (4 * sg - 12, ALU.logical_shift_right)
                d1 = P.op("vector", lambda v, sh1=sh1, w=w, pksl=pksl: v.tensor_scalar(
                    dq1[:, 0:w], pk_sb[:, pksl * 704:pksl * 704 + w],
                    sh1[0], 448, sh1[1], ALU.bitwise_and),
                    deps=[ld_pk] if sg == 0 else [])
                d2 = P.op("vector", lambda v, sh2=sh2, w=w, pksl=pksl: v.tensor_scalar(
                    dq2[:, 0:w], pk_sb[:, pksl * 704:pksl * 704 + w],
                    sh2[0], 32768, sh2[1], ALU.bitwise_and), deps=[])
                d3 = P.op("vector", lambda v, w=w: v.tensor_scalar(
                    dq3[:, 0:w], dq1[:, 0:w], 64, 16128, ALU.is_ge, ALU.mult),
                    deps=[])
                d4 = P.op("vector", lambda v, w=w: v.tensor_single_scalar(
                    dq4[:, 0:w], dq1[:, 0:w], 128, ALU.is_ge), deps=[])
                d4b = P.op("vector", lambda v, w=w: v.tensor_mul(
                    dq4[:, 0:w], dq4[:, 0:w], dq1[:, 0:w]), deps=[])
                d5 = P.op("vector", lambda v, w=w: v.tensor_add(
                    dq5[:, 0:w], dq3[:, 0:w], dq4[:, 0:w]), deps=[])
                last_bits = P.op("vector", lambda v, sg=sg, w=w: v.tensor_sub(
                    bitsB[:, sg:8 * w:8], dq5[:, 0:w], dq2[:, 0:w]), deps=[])
            nonlocal_bits = []
            mul_ops = []
            for g in range(ngr):
                mul_ops.append(P.op("vector", lambda v, g=g: v.tensor_scalar_mul(
                    denseB[:, g * 128:(g + 1) * 128],
                    bitsB[:, g * 128:(g + 1) * 128].bitcast(BF16),
                    scfB_sb[:, g:g + 1]),
                    deps=[last_bits, cast] + ([dense_user] if g == 0 else [])))
            last_tr = None
            for ic in range(ngr):
                sl2 = ic % 2
                ev = transpose_to(trB_ps, sl2, denseB, ic * 128,
                                  trbB, sl2 * 128, [mul_ops[ic], trb_user[sl2]])
                rows, cols = doff_fn(ic)
                dd = P.dmaop("scalar", lambda sc2, rows=rows, cols=cols, sl2=sl2: sc2.dma_start(
                    out=dst[rows[0]:rows[1], cols[0]:cols[1]],
                    in_=trbB[:, sl2 * 128:sl2 * 128 + 128]), deps=[ev])
                trb_user[sl2] = dd
                out_dmas.append(dd)
                last_tr = ev
            dense_user = last_tr
            bits_war = mul_ops[-1]
            return last_tr

        trb_user = [None, None]
        for oc in range(11):
            sl = oc % 2
            ld_pk = P.dmaop("sync", lambda s2, oc=oc, sl=sl: s2.dma_start(
                out=pk_sb[:, sl * 704:sl * 704 + 256],
                in_=gup_pk[oc * 128:(oc + 1) * 128, :]),
                deps=barA5 + [pk_user[sl]])
            ld_sc = P.dmaop("sync", lambda s2, oc=oc: s2.dma_start(
                out=scB_sb[:, 0:16], in_=gup_scT[oc * 128:(oc + 1) * 128, :]),
                deps=barA5 + [sc_user])
            last = dequant_chunk(
                ld_pk, ld_sc, sl, 256, 16, wgq_dmas, wg_gin,
                lambda ic, oc=oc: ((ic * 128, (ic + 1) * 128),
                                   (oc * 128, (oc + 1) * 128)))
            pk_user[sl] = last
        wdq_dmas = []
        for oc in range(2):
            sl = oc % 2
            ld_pk = P.dmaop("sync", lambda s2, oc=oc, sl=sl: s2.dma_start(
                out=pk_sb[:, sl * 704:sl * 704 + 704],
                in_=dn_pk[oc * 128:(oc + 1) * 128, :]),
                deps=[pk_user[sl]])
            ld_sc = P.dmaop("sync", lambda s2, oc=oc: s2.dma_start(
                out=scB_sb[:, 0:44], in_=dn_scT[oc * 128:(oc + 1) * 128, :]),
                deps=[sc_user])
            last = dequant_chunk(
                ld_pk, ld_sc, sl, 704, 44, wdq_dmas, wd_gin,
                lambda ic, oc=oc: ((ic * 128, (ic + 1) * 128),
                                   (oc * 128, (oc + 1) * 128)))
            pk_user[sl] = last

        ag3a = P.op("gpsimd", lambda g: g.collective_compute(
            "AllGather", ALU.bypass, replica_groups=[list(range(NCORE))],
            ins=[wg_gin.opt()], outs=[wg_gout.opt()]), deps=wgq_dmas)
        ag3b = P.op("gpsimd", lambda g: g.collective_compute(
            "AllGather", ALU.bypass, replica_groups=[list(range(NCORE))],
            ins=[wd_gin.opt()], outs=[wd_gout.opt()]), deps=wdq_dmas)

        barB1 = P.barrier()
        B1.close()

        B2 = ExitStack()

        tm_bfB = sb(B2, "tm_bfB", [128, D], BF16)
        n2_fm = sb(B2, "n2_fm", [128, 16 * TOK], BF16)
        h_fm = sb(B2, "h_fm", [128, 44 * 512], BF16)
        wt_bf = sb(B2, "wt_bf", [128, 3 * 32 * 128], BF16)
        wdt_bf = sb(B2, "wdt_bf", [128, 4 * 512], BF16)
        sil_bf = sb(B2, "sil_bf", [128, 512], BF16)
        ffn_bnc = sb(B2, "ffn_bnc", [128, 2 * 512], F32)
        ffn_t = sb(B2, "ffn_t", [128, D], F32)
        ln_tmpB = ffn_t
        sq4 = sb(B2, "sq4", [128, 16], F32)
        stB = alloc_stats(B2, "stB", NT)
        stB2 = alloc_stats(B2, "stB2", NT)
        B2t = ExitStack()
        trB2_ps = psb(B2t, "trB2_ps", 2)

        n2_stats = ln_stats(stB, xres, [(t, t * D) for t in range(NT)],
                            barB1, D)
        n2_done = []
        prev_applyB = None
        for t in range(NT):
            a = P.op("vector", lambda v, t=t, tm_bfB=tm_bfB: v.tensor_scalar(
                tm_bfB[:, 0:D], xres[:, t * D:(t + 1) * D],
                stB["mean"][:, t:t + 1], stB["rstd"][:, t:t + 1],
                ALU.subtract, ALU.mult), deps=n2_stats + [prev_applyB])
            last = None
            for ic in range(16):
                last = transpose_to(trB2_ps, ic % 2, tm_bfB, ic * 128,
                                    n2_fm, ic * TOK + t * 128, [a])
            prev_applyB = last
            n2_done.append(last)
        n2_all = n2_done[-1]
        B2t.close()
        ps_g = psf(B2, "ps_g", 2)
        ps_u = psf(B2, "ps_u", 2)
        ps_d = psf(B2, "ps_d", 4)

        wt_user = [None] * 3
        sil_user = None
        gps_user = [None, None]
        ups_user = [None, None]
        h_user = None
        bnc_user = [None, None]
        rd_user = [None] * 4
        prev_sq = None
        wdt_user = [None] * 4
        dps_user = [None] * 4
        res_doneB = []
        prev_applyB2 = None
        for tc in range(2):
            wr_dmas = [[] for _ in range(4)]
            h_ops = []
            for ocp in range(44):
                wsl = ocp % 3
                lds = []
                for half, ch in ((0, ocp), (1, 44 + ocp)):
                    ro, colr = ch // 11, (ch % 11) * 128
                    lds.append(P.dmaop("sync" if half == 0 else "scalar",
                                       lambda s2, ro=ro, colr=colr, half=half, wsl=wsl: s2.dma_start(
                        out=wt_bf[:, (wsl * 32 + half * 16) * 128:(wsl * 32 + half * 16 + 16) * 128]
                            .rearrange("p (a c) -> p a c", c=128),
                        in_=wg_gout[ro * D:(ro + 1) * D, colr:colr + 128]
                            .rearrange("(a p) c -> p a c", p=128)),
                        deps=[ag3a, wt_user[wsl]]))
                gsl = ocp % 2
                mmg = None
                mmu = None
                for ic in range(16):
                    depg = (lds + [n2_all, gps_user[gsl]]) if ic == 0 else []
                    mmg = P.op("tensor", lambda te, ic=ic, tc=tc, wsl=wsl, gsl=gsl,
                               first=(ic == 0): te.matmul(
                        ps_g[:, gsl * 512:gsl * 512 + 512],
                        wt_bf[:, (wsl * 32 + ic) * 128:(wsl * 32 + ic) * 128 + 128],
                        n2_fm[:, ic * TOK + tc * 512:ic * TOK + tc * 512 + 512],
                        start=first, stop=(ic == 15)), deps=depg)
                for ic in range(16):
                    depu = [ups_user[gsl]] if ic == 0 else []
                    mmu = P.op("tensor", lambda te, ic=ic, tc=tc, wsl=wsl, gsl=gsl,
                               first=(ic == 0): te.matmul(
                        ps_u[:, gsl * 512:gsl * 512 + 512],
                        wt_bf[:, (wsl * 32 + 16 + ic) * 128:(wsl * 32 + 16 + ic) * 128 + 128],
                        n2_fm[:, ic * TOK + tc * 512:ic * TOK + tc * 512 + 512],
                        start=first, stop=(ic == 15)), deps=depu)
                sil = P.op("scalar", lambda s2, gsl=gsl: s2.activation(
                    sil_bf[:, :], ps_g[:, gsl * 512:gsl * 512 + 512], AF.Silu),
                    deps=[mmg, sil_user])
                gps_user[gsl] = sil
                hm = P.op("vector", lambda v, ocp=ocp, gsl=gsl: v.tensor_mul(
                    h_fm[:, ocp * 512:(ocp + 1) * 512], sil_bf[:, :],
                    ps_u[:, gsl * 512:gsl * 512 + 512]), deps=[sil, mmu, h_user])
                ups_user[gsl] = hm
                sil_user = hm
                wt_user[wsl] = hm
                h_ops.append(hm)
            ev_d = []
            for od in range(4):
                last_mm = {}
                for ic in range(44):
                    dsl = ic % 4
                    ldd = []
                    for hh in range(2):
                        ro2 = od * 2 + hh
                        ldd.append(P.dmaop("sync" if hh == 0 else "scalar",
                                           lambda s2, ro2=ro2, ic=ic, dsl=dsl, hh=hh: s2.dma_start(
                            out=wdt_bf[:, dsl * 512 + hh * 256:dsl * 512 + hh * 256 + 256],
                            in_=wd_gout[ro2 * I0 + ic * 128:ro2 * I0 + (ic + 1) * 128, :]),
                            deps=[ag3b, wdt_user[dsl]]))
                    for t128 in range(4):
                        dep = (ldd + h_ops + ([dps_user[t128]] if ic == 0 else [])) if (ic == 0 or t128 == 0) else []
                        if ic > 0 and t128 == 0:
                            dep = ldd
                        last_mm[t128] = P.op("tensor", lambda te, ic=ic, t128=t128, od=od, dsl=dsl,
                                             first=(ic == 0): te.matmul(
                            ps_d[:, t128 * 512:t128 * 512 + 512],
                            h_fm[:, ic * 512 + t128 * 128:ic * 512 + t128 * 128 + 128],
                            wdt_bf[:, dsl * 512:dsl * 512 + 512],
                            start=first, stop=(ic == 43)), deps=dep)
                    wdt_user[dsl] = last_mm[3]
                for t128 in range(4):
                    bsl = (od * 4 + t128) % 2
                    ev = P.op("vector", lambda v, t128=t128, bsl=bsl: v.tensor_copy(
                        ffn_bnc[:, bsl * 512:bsl * 512 + 512],
                        ps_d[:, t128 * 512:t128 * 512 + 512]),
                        deps=[last_mm[t128], bnc_user[bsl]])
                    wd2 = P.dmaop("scalar", lambda s2, t128=t128, od=od, bsl=bsl: s2.dma_start(
                        out=ffn_dram[t128 * 128:(t128 + 1) * 128, od * 512:od * 512 + 512],
                        in_=ffn_bnc[:, bsl * 512:bsl * 512 + 512]),
                        deps=[ev, rd_user[t128]])
                    bnc_user[bsl] = wd2
                    dps_user[t128] = ev
                    wr_dmas[t128].append(wd2)
                    ev_d.append(wd2)
            h_user = ev_d[-1]
            for t128 in range(4):
                t = tc * 4 + t128
                rdb = P.dmaop("sync", lambda s2, t128=t128: s2.dma_start(
                    out=ffn_t[:, :], in_=ffn_dram[t128 * 128:(t128 + 1) * 128, :]),
                    deps=wr_dmas[t128] + [prev_applyB2, prev_sq])
                rd_user[t128] = rdb
                rd_done = rdb
                sq = P.op("scalar", lambda s2, t128=t128: s2.activation(
                    SC["sq_dump"][:, 0:D], ffn_t[:, :],
                    AF.Square, accum_out=sq4[:, t128:t128 + 1]), deps=[rdb])
                prev_sq = sq
                c1 = P.op("vector", lambda v, t128=t128: v.tensor_scalar_mul(
                    sq4[:, 4 + t128:5 + t128], sq4[:, t128:t128 + 1], 1.0 / D),
                    deps=[sq])
                c2 = P.op("scalar", lambda s2, t128=t128: s2.activation(
                    sq4[:, 8 + t128:9 + t128], sq4[:, 4 + t128:5 + t128],
                    AF.Ln, bias=1e-6), deps=[c1, const_dep[1e-6]])
                c3 = P.op("scalar", lambda s2, t128=t128: s2.activation(
                    sq4[:, 12 + t128:13 + t128], sq4[:, 8 + t128:9 + t128],
                    AF.Exp, scale=-0.5), deps=[c2])
                y2 = P.op("vector", lambda v, t128=t128, ln_tmpB=ln_tmpB: v.tensor_scalar_mul(
                    ln_tmpB[:, 0:D], ffn_t[:, :],
                    sq4[:, 12 + t128:13 + t128]), deps=[c3, prev_applyB2, rdb])
                stt = ln_stats(stB2, ln_tmpB, [(t, 0)], [y2], D,
                               war_deps=[prev_applyB2])
                sqd = SC["sq_dump"]
                ap1 = P.op("vector", lambda v, t=t, ln_tmpB=ln_tmpB, sqd=sqd: v.tensor_scalar(
                    sqd[:, 0:D], ln_tmpB[:, 0:D],
                    stB2["mean"][:, t:t + 1], stB2["rstd"][:, t:t + 1],
                    ALU.subtract, ALU.mult), deps=stt)
                ap2 = P.op("vector", lambda v, t=t, sqd=sqd: v.tensor_add(
                    xres[:, t * D:(t + 1) * D], xres[:, t * D:(t + 1) * D],
                    sqd[:, 0:D]), deps=[ap1])
                prev_applyB2 = ap2
                res_doneB.append(ap2)
        res_done = res_doneB
        if DEBUG:
            P.dmaop("gpsimd", lambda g: g.dma_start(out=dbg_wg, in_=wg_gout), deps=[ag3a])
            P.dmaop("gpsimd", lambda g: g.dma_start(out=dbg_wd, in_=wd_gout), deps=[ag3b])
            P.dmaop("gpsimd", lambda g: g.dma_start(out=dbg_ffn, in_=ffn_dram),
                    deps=res_doneB)
        barB = P.barrier()
        B2.close()
        barA5 = barB

    if PHASES >= 3:
        barB2 = res_done[-1]
        C0 = ExitStack()
        Q_fm = sb(C0, "Q_fm", [128, 16 * TOK], BF16)
        stC = alloc_stats(C0, "stC", NT)
        stC2 = alloc_stats(C0, "stC2", NT)

        C1 = ExitStack()
        tm_bfC = sb(C1, "tm_bfC", [128, D], BF16)
        n3_fm = sb(C1, "n3_fm", [128, 16 * TOK], BF16)
        wq_bf = sb(C1, "wq_bf", [128, 3 * 16 * 256], BF16)
        kbnc = sb(C1, "kbnc", [128, 2 * 512], BF16)
        vbnc = sb(C1, "vbnc", [128, 2 * 512], BF16)
        trC_ps = psb(C1, "trC_ps", 2)
        ps_qkv = psf(C1, "ps_qkv", 2)

        n3_stats = ln_stats(stC, xres, [(t, t * D) for t in range(NT)],
                            [barB2] + barA5, D)
        n3_done = []
        prev_applyC = None
        for t in range(NT):
            a = P.op("vector", lambda v, t=t, tm_bfC=tm_bfC: v.tensor_scalar(
                tm_bfC[:, 0:D], xres[:, t * D:(t + 1) * D],
                stC["mean"][:, t:t + 1], stC["rstd"][:, t:t + 1],
                ALU.subtract, ALU.mult), deps=n3_stats + [prev_applyC])
            last = None
            for ic in range(16):
                last = transpose_to(trC_ps, ic % 2, tm_bfC, ic * 128,
                                    n3_fm, ic * TOK + t * 128, [a])
            prev_applyC = last
            n3_done.append(last)
        n3_all = n3_done[-1]

        kin_dmas = []
        vdram_dmas = []
        vbnc_user = [None, None]
        wqsl_user = [None, None, None]
        qkv_ps_user = [None, None]
        kbnc_user = [None, None]
        for pi, part in enumerate((1, 0, 2)):
            for oc2 in range(8):
                wsl = (pi * 8 + oc2) % 3
                ldw = P.dmaop("gpsimd", lambda g, part=part, oc2=oc2, wsl=wsl: g.dma_start(
                    out=wq_bf[:, wsl * 4096:(wsl + 1) * 4096]
                        .rearrange("p (a c) -> p a c", c=256),
                    in_=w_qkv[:, part * D + oc2 * 256:part * D + oc2 * 256 + 256]
                        .rearrange("(a p) c -> p a c", p=128)),
                    deps=[wqsl_user[wsl]] + barA5)
                for sub in range(2):
                  oc = oc2 * 2 + sub
                  for tc in range(2):
                    psl = (oc * 2 + tc) % 2
                    mm = None
                    for ic in range(16):
                        dep = [ldw, n3_all, qkv_ps_user[psl]] if ic == 0 else []
                        mm = P.op("tensor", lambda te, ic=ic, tc=tc, wsl=wsl, psl=psl,
                                  sub=sub, first=(ic == 0): te.matmul(
                            ps_qkv[:, psl * 512:psl * 512 + 512],
                            wq_bf[:, wsl * 4096 + ic * 256 + sub * 128:wsl * 4096 + ic * 256 + sub * 128 + 128],
                            n3_fm[:, ic * TOK + tc * 512:ic * TOK + tc * 512 + 512],
                            start=first, stop=(ic == 15)), deps=dep)
                    if part == 1:
                        ksl = (oc * 2 + tc) % 2
                        ev = P.op("vector", lambda v, psl=psl, ksl=ksl: v.tensor_copy(
                            kbnc[:, ksl * 512:ksl * 512 + 512],
                            ps_qkv[:, psl * 512:psl * 512 + 512]),
                            deps=[mm, kbnc_user[ksl]])
                        kd = P.dmaop("scalar", lambda s2, oc=oc, tc=tc, ksl=ksl: s2.dma_start(
                            out=kin[oc * 128:(oc + 1) * 128, tc * 512:tc * 512 + 512],
                            in_=kbnc[:, ksl * 512:ksl * 512 + 512]), deps=[ev])
                        kbnc_user[ksl] = kd
                        kin_dmas.append(kd)
                        qkv_ps_user[psl] = ev
                    elif part == 0:
                        ev = P.op("vector", lambda v, oc=oc, tc=tc, psl=psl: v.tensor_copy(
                            Q_fm[:, oc * TOK + tc * 512:oc * TOK + tc * 512 + 512],
                            ps_qkv[:, psl * 512:psl * 512 + 512]), deps=[mm])
                        qkv_ps_user[psl] = ev
                    else:
                        vsl = (oc * 2 + tc) % 2
                        ev = P.op("vector", lambda v, psl=psl, vsl=vsl: v.tensor_copy(
                            vbnc[:, vsl * 512:vsl * 512 + 512],
                            ps_qkv[:, psl * 512:psl * 512 + 512]),
                            deps=[mm, vbnc_user[vsl]])
                        vd = P.dmaop("scalar", lambda s2, oc=oc, tc=tc, vsl=vsl: s2.dma_start(
                            out=vdram[oc * 128:(oc + 1) * 128, tc * 512:tc * 512 + 512],
                            in_=vbnc[:, vsl * 512:vsl * 512 + 512]), deps=[ev])
                        vbnc_user[vsl] = vd
                        vdram_dmas.append(vd)
                        qkv_ps_user[psl] = ev
                    wqsl_user[wsl] = mm

        ag4 = P.op("gpsimd", lambda g: g.collective_compute(
            "AllGather", ALU.bypass, replica_groups=[list(range(NCORE))],
            ins=[kin.opt()], outs=[kfull.opt()]), deps=kin_dmas)

        barC1 = P.barrier()
        C1.close()

        C2 = ExitStack()
        K_h = sb(C2, "K_h", [128, 2 * T], BF16)
        P_bf = sb(C2, "P_bf", [128, 2 * T], BF16)
        P_acc = sb(C2, "P_acc", [128, T], BF16)
        dacc = sb(C2, "dacc", [128, 16], F32)
        dnm = sb(C2, "dnm", [128, 4], F32)
        cs_bnc = sb(C2, "cs_bnc", [1, 2 * 512], F32)
        ps_sC = psf(C2, "ps_sC", 4)
        ps_cs = C2.enter_context(nc.psum_tensor("ps_cs", [1, 1024], F32))

        rs_dmas = []
        last_csdma = None
        kh_user = [None, None]
        sC_user = [None, None]
        p_user = [None, None]
        stt_user = None
        csps_user = [None, None]
        cs_dma_user = [None, None]
        for h in range(H0):
            khsl = h % 2
            kh_lds = []
            for ro in range(NCORE):
                kh_lds.append(P.dmaop("sync", lambda s2, ro=ro, h=h, khsl=khsl: s2.dma_start(
                    out=K_h[:, khsl * T + ro * TOK:khsl * T + (ro + 1) * TOK],
                    in_=kfull[ro * D + h * 128:ro * D + (h + 1) * 128, :]),
                    deps=[ag4, kh_user[khsl]]))
            last_stt = None
            for qt in range(NT):
                psl = qt % 2
                dsl = (qt % 2) * 8
                ex_user = None
                for kc2 in range(8):
                    sl = kc2 % 2
                    dep = (kh_lds + [sC_user[sl]]) if kc2 <= 1 else [sC_user[sl]]
                    mma = P.op("tensor", lambda te, h=h, qt=qt, kc2=kc2, sl=sl, khsl=khsl: te.matmul(
                        ps_sC[:, sl * 1024:sl * 1024 + 512],
                        Q_fm[:, h * TOK + qt * 128:h * TOK + qt * 128 + 128],
                        K_h[:, khsl * T + kc2 * 1024:khsl * T + kc2 * 1024 + 512],
                        start=True, stop=True), deps=dep)
                    mmb = P.op("tensor", lambda te, h=h, qt=qt, kc2=kc2, sl=sl, khsl=khsl: te.matmul(
                        ps_sC[:, sl * 1024 + 512:sl * 1024 + 1024],
                        Q_fm[:, h * TOK + qt * 128:h * TOK + qt * 128 + 128],
                        K_h[:, khsl * T + kc2 * 1024 + 512:khsl * T + kc2 * 1024 + 1024],
                        start=True, stop=True), deps=[])
                    ex = P.op("scalar", lambda s2, kc2=kc2, sl=sl, psl=psl, dsl=dsl: s2.activation(
                        P_bf[:, psl * T + kc2 * 1024:psl * T + kc2 * 1024 + 1024],
                        ps_sC[:, sl * 1024:sl * 1024 + 1024], AF.Exp,
                        scale=1.0 / math.sqrt(128.0),
                        accum_out=dacc[:, dsl + kc2:dsl + kc2 + 1]),
                        deps=[mmb, const_dep[1.0 / math.sqrt(128.0)]]
                             + ([p_user[psl]] if kc2 <= 1 else []))
                    sC_user[sl] = ex
                    ex_user = ex
                rd = P.op("vector", lambda v, dsl=dsl: v.reduce_sum(
                    dnm[:, 0:1], dacc[:, dsl:dsl + 8], axis=AX.X), deps=[ex_user])
                rc = P.op("vector", lambda v: v.reciprocal(
                    dnm[:, 1:2], dnm[:, 0:1]), deps=[rd])
                if qt == 0:
                    stt = P.op("vector", lambda v, psl=psl: v.tensor_scalar_mul(
                        P_acc[:, :], P_bf[:, psl * T:psl * T + T], dnm[:, 1:2]),
                        deps=[rc, last_csdma])
                else:
                    stt = P.op("vector", lambda v, psl=psl: v.scalar_tensor_tensor(
                        P_acc[:, :], P_bf[:, psl * T:psl * T + T], dnm[:, 1:2],
                        P_acc[:, :], ALU.mult, ALU.add), deps=[rc, stt_user])
                p_user[psl] = stt
                stt_user = stt
                last_stt = stt
            for kc in range(16):
                csl = kc % 2
                csm = P.op("tensor", lambda te, kc=kc, csl=csl: te.matmul(
                    ps_cs[:, csl * 512:csl * 512 + 512], ones_col[:, :],
                    P_acc[:, kc * 512:kc * 512 + 512],
                    start=True, stop=True),
                    deps=[last_stt, op_ones_col, csps_user[csl]])
                cse = P.op("vector", lambda v2, csl=csl: v2.tensor_copy(
                    cs_bnc[:, csl * 512:csl * 512 + 512],
                    ps_cs[:, csl * 512:csl * 512 + 512]),
                    deps=[csm, cs_dma_user[csl]])
                csps_user[csl] = cse
                csd = P.dmaop("scalar", lambda s2, h=h, kc=kc, csl=csl: s2.dma_start(
                    out=rs_in[(kc // 2) * 16 + h:(kc // 2) * 16 + h + 1,
                              (kc % 2) * 512:(kc % 2) * 512 + 512],
                    in_=cs_bnc[:, csl * 512:csl * 512 + 512]), deps=[cse])
                rs_dmas.append(csd)
                cs_dma_user[csl] = csd
                last_csdma = csd
            kh_user[khsl] = ex_user

        rs = P.op("gpsimd", lambda g: g.collective_compute(
            "ReduceScatter", ALU.add, replica_groups=[list(range(NCORE))],
            ins=[rs_in.opt()], outs=[rs_out.opt()]), deps=rs_dmas)

        barC2 = P.barrier()
        C2.close()

        C3 = ExitStack()
        V_rd = sb(C3, "V_rd", [128, 16 * TOK], BF16)
        cs1 = sb(C3, "cs1", [1, TOK], F32)
        cs1b = sb(C3, "cs1b", [1, TOK], BF16)
        csb = sb(C3, "csb", [128, TOK], BF16)
        wo_bf = sb(C3, "wo_bf", [128, 3 * 16 * 128], BF16)
        attn1_fm = sb(C3, "attn1_fm", [128, 16 * TOK], BF16)
        attn_tmC = sb(C3, "attn_tmC", [128, D], BF16)
        trC2_ps = psb(C3, "trC2_ps", 2)
        ps_bc = psf(C3, "ps_bc", 1)
        ps_wo = psf(C3, "ps_wo", 2)

        ld_v = P.dmaop("sync", lambda s2: s2.dma_start(
            out=V_rd[:, :].rearrange("p (a c) -> p a c", c=TOK),
            in_=vdram.rearrange("(a p) c -> p a c", p=128)),
            deps=vdram_dmas + barC2)
        cs1_user = None
        bc_userC = None
        vs_done = []
        for h in range(H0):
            ld1 = P.dmaop("sync", lambda s2, h=h: s2.dma_start(
                out=cs1[:, :], in_=rs_out[h:h + 1, :]), deps=[rs, cs1_user])
            cp1 = P.op("vector", lambda v: v.tensor_copy(cs1b[:, :], cs1[:, :]),
                       deps=[ld1, cs1_user])
            bch = []
            for half in range(2):
                bcm = P.op("tensor", lambda te, half=half: te.matmul(
                    ps_bc[:, 0:512], ones_row[:, :],
                    cs1b[:, half * 512:half * 512 + 512],
                    start=True, stop=True), deps=[cp1, op_ones, bc_userC] + barC2)
                bce = P.op("vector", lambda v, half=half: v.tensor_copy(
                    csb[:, half * 512:half * 512 + 512], ps_bc[:, 0:512]),
                    deps=[bcm])
                bc_userC = bce
                bch.append(bce)
            vsc = P.op("vector", lambda v, h=h: v.tensor_mul(
                V_rd[:, h * TOK:(h + 1) * TOK], V_rd[:, h * TOK:(h + 1) * TOK],
                csb[:, :]), deps=bch + [ld_v])
            cs1_user = vsc
            vs_done.append(vsc)

        woldu = [None, None, None]
        wops_user = [None, None]
        a1fm_done = []
        for oc in range(16):
            wsl = oc % 3
            ldwo = P.dmaop("gpsimd", lambda g, oc=oc, wsl=wsl: g.dma_start(
                out=wo_bf[:, wsl * 2048:(wsl + 1) * 2048]
                    .rearrange("p (a c) -> p a c", c=128),
                in_=w_o[:, oc * 128:(oc + 1) * 128]
                    .rearrange("(a p) c -> p a c", p=128)),
                deps=[woldu[wsl]] + barC2)
            for tc in range(2):
                psl = (oc * 2 + tc) % 2
                mm = None
                for hh in range(16):
                    dep = ([ldwo, wops_user[psl]] + vs_done) if hh == 0 else []
                    mm = P.op("tensor", lambda te, hh=hh, tc=tc, wsl=wsl, psl=psl,
                              first=(hh == 0): te.matmul(
                        ps_wo[:, psl * 512:psl * 512 + 512],
                        wo_bf[:, wsl * 2048 + hh * 128:wsl * 2048 + hh * 128 + 128],
                        V_rd[:, hh * TOK + tc * 512:hh * TOK + tc * 512 + 512],
                        start=first, stop=(hh == 15)), deps=dep)
                ev = P.op("vector", lambda v, oc=oc, tc=tc, psl=psl: v.tensor_copy(
                    attn1_fm[:, oc * TOK + tc * 512:oc * TOK + tc * 512 + 512],
                    ps_wo[:, psl * 512:psl * 512 + 512]), deps=[mm])
                wops_user[psl] = ev
                woldu[wsl] = mm
                a1fm_done.append(ev)

        res_doneC = []
        prev_applyC2 = None
        for t in range(NT):
            trs = []
            for ic in range(16):
                last = transpose_to(trC2_ps, ic % 2, attn1_fm, ic * TOK + t * 128,
                                    attn_tmC, ic * 128,
                                    [a1fm_done[-1], prev_applyC2])
                trs.append(last)
            stt2 = ln_stats(stC2, attn_tmC, [(t, 0)], trs, D,
                            war_deps=[prev_applyC2])
            sqd = SC["sq_dump"]
            ap1 = P.op("vector", lambda v, t=t, sqd=sqd: v.tensor_scalar(
                sqd[:, 0:D], attn_tmC[:, 0:D],
                stC2["mean"][:, t:t + 1], stC2["rstd"][:, t:t + 1],
                ALU.subtract, ALU.mult), deps=stt2)
            ap2 = P.op("vector", lambda v, t=t, sqd=sqd: v.tensor_add(
                xres[:, t * D:(t + 1) * D], xres[:, t * D:(t + 1) * D],
                sqd[:, 0:D]), deps=[ap1])
            prev_applyC2 = ap2
            res_doneC.append(ap2)
        res_done = res_doneC
        barC = P.barrier()
        C3.close()
        C0.close()
        barA5 = barC

    if PHASES >= 4:
        barC_ = res_done[-1]
        DD = ExitStack()
        n4_fm = sb(DD, "n4_fm", [128, 16 * 512], BF16)
        h1_fm = sb(DD, "h1_fm", [128, 32 * 512], BF16)
        h2_fm = sb(DD, "h2_fm", [128, 32 * 512], BF16)
        w1_bf = sb(DD, "w1_bf", [128, 2 * 16 * 128], BF16)
        w2_bf = sb(DD, "w2_bf", [128, 2 * 32 * 128], BF16)
        w3_bf = sb(DD, "w3_bf", [128, 4 * 512], BF16)
        b1_sb = sb(DD, "b1_sb", [128, 32], F32)
        b2_sb = sb(DD, "b2_sb", [128, 32], F32)
        stD = alloc_stats(DD, "stD", NT)
        stD2 = alloc_stats(DD, "stD2", NT)
        Dt = ExitStack()
        tm_bfD = sb(Dt, "tm_bfD", [128, D], BF16)
        trD_ps = psb(Dt, "trD_ps", 2)

        ld_b1 = P.dmaop("sync", lambda s2: s2.dma_start(out=b1_sb[:, :], in_=b1_cols),
                        deps=barA5)
        ld_b2D = P.dmaop("sync", lambda s2: s2.dma_start(out=b2_sb[:, :], in_=b2_cols),
                         deps=barA5)
        ld_b3 = P.dmaop("gpsimd", lambda g: g.dma_start(out=brow_bf[:, :], in_=b3_row),
                        deps=barA5)

        n4b = sb(Dt, "n4b", [128, 2 * 128], BF16)
        n4_stats = ln_stats(stD, xres, [(t, t * D) for t in range(NT)],
                            [barC_] + barA5, D)
        n4_dmas = []
        n4b_user = [None, None]
        prev_applyD = None
        for t in range(NT):
            a = P.op("vector", lambda v, t=t, tm_bfD=tm_bfD: v.tensor_scalar(
                tm_bfD[:, 0:D], xres[:, t * D:(t + 1) * D],
                stD["mean"][:, t:t + 1], stD["rstd"][:, t:t + 1],
                ALU.subtract, ALU.mult), deps=n4_stats + [prev_applyD])
            last = None
            for ic in range(16):
                sl9 = ic % 2
                ev = transpose_to(trD_ps, sl9, tm_bfD, ic * 128,
                                  n4b, sl9 * 128, [a, n4b_user[sl9]])
                dd = P.dmaop("scalar", lambda s2, ic=ic, t=t, sl9=sl9: s2.dma_start(
                    out=n4_dram[ic * 128:(ic + 1) * 128, t * 128:(t + 1) * 128],
                    in_=n4b[:, sl9 * 128:sl9 * 128 + 128]), deps=[ev])
                n4b_user[sl9] = dd
                n4_dmas.append(dd)
                last = ev
            prev_applyD = last
        Dt.close()
        ffn_tD = sb(DD, "ffn_tD", [128, D], F32)
        ffn_bncD = sb(DD, "ffn_bncD", [128, 2 * 512], F32)
        ps_h = psf(DD, "ps_h", 2)
        ps_d4 = psf(DD, "ps_d4", 4)

        w1_user = [None] * 2
        w2_user = [None] * 2
        w3_user = [None] * 4
        hps_user = [None, None]
        d4ps_user = [None] * 4
        h1_user = None
        h2_user = None
        bncD_user = [None, None]
        rdD_user = None
        prev_sqD = None
        prev_applyD2 = None
        res_doneD = []
        n4ld_user = None
        for tc in range(2):
            wrD = [[] for _ in range(4)]
            ldn4 = P.dmaop("sync", lambda s2, tc=tc: s2.dma_start(
                out=n4_fm[:, :].rearrange("p (a c) -> p a c", c=512),
                in_=n4_dram[:, tc * 512:(tc + 1) * 512]
                    .rearrange("(a p) c -> p a c", p=128)),
                deps=n4_dmas + [n4ld_user])
            h1_ops = []
            for oc in range(32):
                wsl1 = oc % 2
                ldw = P.dmaop("gpsimd", lambda g, oc=oc, wsl1=wsl1: g.dma_start(
                    out=w1_bf[:, wsl1 * 2048:(wsl1 + 1) * 2048]
                        .rearrange("p (a c) -> p a c", c=128),
                    in_=w1T[:, oc * 128:(oc + 1) * 128]
                        .rearrange("(a p) c -> p a c", p=128)),
                    deps=[w1_user[wsl1]] + barA5)
                psl = oc % 2
                mm = None
                for ic in range(16):
                    dep = [ldw, ldn4, hps_user[psl]] if ic == 0 else []
                    mm = P.op("tensor", lambda te, ic=ic, tc=tc, psl=psl, wsl1=wsl1,
                              first=(ic == 0): te.matmul(
                        ps_h[:, psl * 512:psl * 512 + 512],
                        w1_bf[:, wsl1 * 2048 + ic * 128:wsl1 * 2048 + ic * 128 + 128],
                        n4_fm[:, ic * 512:ic * 512 + 512],
                        start=first, stop=(ic == 15)), deps=dep)
                ev = P.op("scalar", lambda s2, oc=oc, psl=psl: s2.activation(
                    h1_fm[:, oc * 512:(oc + 1) * 512],
                    ps_h[:, psl * 512:psl * 512 + 512], AF.Relu,
                    bias=b1_sb[:, oc:oc + 1]),
                    deps=[mm, ld_b1, h1_user])
                hps_user[psl] = ev
                w1_user[wsl1] = mm
                h1_ops.append(ev)
            h2_ops = []
            for oc in range(32):
                wsl2 = oc % 2
                ldw = P.dmaop("gpsimd", lambda g, oc=oc, wsl2=wsl2: g.dma_start(
                    out=w2_bf[:, wsl2 * 4096:(wsl2 + 1) * 4096]
                        .rearrange("p (a c) -> p a c", c=128),
                    in_=w2T[:, oc * 128:(oc + 1) * 128]
                        .rearrange("(a p) c -> p a c", p=128)),
                    deps=[w2_user[wsl2]] + barA5)
                psl = oc % 2
                mm = None
                for ic in range(32):
                    dep = ([ldw, hps_user[psl]] + h1_ops) if ic == 0 else []
                    mm = P.op("tensor", lambda te, ic=ic, psl=psl, wsl2=wsl2,
                              first=(ic == 0): te.matmul(
                        ps_h[:, psl * 512:psl * 512 + 512],
                        w2_bf[:, wsl2 * 4096 + ic * 128:wsl2 * 4096 + ic * 128 + 128],
                        h1_fm[:, ic * 512:(ic + 1) * 512],
                        start=first, stop=(ic == 31)), deps=dep)
                ev = P.op("scalar", lambda s2, oc=oc, psl=psl: s2.activation(
                    h2_fm[:, oc * 512:(oc + 1) * 512],
                    ps_h[:, psl * 512:psl * 512 + 512], AF.Relu,
                    bias=b2_sb[:, oc:oc + 1]),
                    deps=[mm, ld_b2D, h2_user])
                hps_user[psl] = ev
                w2_user[wsl2] = mm
                h2_ops.append(ev)
            h1_user = h2_ops[-1]
            n4ld_user = [w1_user[0], w1_user[1]]
            evD = []
            for od in range(4):
                last_mm = {}
                for ic in range(32):
                    dsl = ic % 4
                    ldw3 = P.dmaop("gpsimd", lambda s2, od=od, ic=ic, dsl=dsl: s2.dma_start(
                        out=w3_bf[:, dsl * 512:(dsl + 1) * 512],
                        in_=w3T[ic * 128:(ic + 1) * 128, od * 512:od * 512 + 512]),
                        deps=[w3_user[dsl]] + barA5)
                    for t128 in range(4):
                        if ic == 0:
                            dep = [ldw3, d4ps_user[t128]] + h2_ops
                        elif t128 == 0:
                            dep = [ldw3]
                        else:
                            dep = []
                        last_mm[t128] = P.op("tensor", lambda te, ic=ic, t128=t128, dsl=dsl,
                                             first=(ic == 0): te.matmul(
                            ps_d4[:, t128 * 512:t128 * 512 + 512],
                            h2_fm[:, ic * 512 + t128 * 128:ic * 512 + t128 * 128 + 128],
                            w3_bf[:, dsl * 512:(dsl + 1) * 512],
                            start=first, stop=False), deps=dep)
                    w3_user[dsl] = last_mm[3]
                for t128 in range(4):
                    mmb = P.op("tensor", lambda te, t128=t128, od=od: te.matmul(
                        ps_d4[:, t128 * 512:t128 * 512 + 512], ones_row[:, :],
                        brow_bf[:, od * 512:od * 512 + 512],
                        start=False, stop=True),
                        deps=[last_mm[t128], ld_b3, op_ones])
                    bsl = (od * 4 + t128) % 2
                    ev = P.op("vector", lambda v, t128=t128, bsl=bsl: v.tensor_copy(
                        ffn_bncD[:, bsl * 512:bsl * 512 + 512],
                        ps_d4[:, t128 * 512:t128 * 512 + 512]),
                        deps=[mmb, bncD_user[bsl]])
                    wd2 = P.dmaop("scalar", lambda s2, t128=t128, od=od, bsl=bsl: s2.dma_start(
                        out=ffnD_dram[t128 * 128:(t128 + 1) * 128, od * 512:od * 512 + 512],
                        in_=ffn_bncD[:, bsl * 512:bsl * 512 + 512]),
                        deps=[ev, rdD_user])
                    bncD_user[bsl] = wd2
                    d4ps_user[t128] = ev
                    wrD[t128].append(wd2)
            h2_user = d4ps_user[3]
            for t128 in range(4):
                t = tc * 4 + t128
                rdb = P.dmaop("sync", lambda s2, t128=t128: s2.dma_start(
                    out=ffn_tD[:, :], in_=ffnD_dram[t128 * 128:(t128 + 1) * 128, :]),
                    deps=wrD[t128] + [prev_applyD2, prev_sqD])
                rdD_user = rdb
                stt3 = ln_stats(stD2, ffn_tD, [(t, 0)], [rdb], D,
                                war_deps=[prev_applyD2])
                prev_sqD = stt3[0]
                sqd = SC["sq_dump"]
                ap1 = P.op("vector", lambda v, t=t, sqd=sqd: v.tensor_scalar(
                    sqd[:, 0:D], ffn_tD[:, 0:D],
                    stD2["mean"][:, t:t + 1], stD2["rstd"][:, t:t + 1],
                    ALU.subtract, ALU.mult), deps=stt3)
                ap2 = P.op("vector", lambda v, t=t, sqd=sqd: v.tensor_add(
                    xres[:, t * D:(t + 1) * D], xres[:, t * D:(t + 1) * D],
                    sqd[:, 0:D]), deps=[ap1])
                prev_applyD2 = ap2
                res_doneD.append(ap2)
        res_done = res_doneD
        barD = P.barrier()
        DD.close()
        barA5 = barD

    st_dmas = []
    for t in range(NT):
        st_dmas.append(P.dmaop("sync", lambda s, t=t: s.dma_start(
            out=out[t * 128:(t + 1) * 128, :],
            in_=xres[:, t * D:(t + 1) * D]), deps=[res_done[t]] + barA5))
    P.op("sync", "nop", deps=st_dmas)

    P.emit()
    top.close()
    return nc



def _bf(x):
    return np.ascontiguousarray(np.asarray(x).astype(ml_dtypes.bfloat16))


def _f32(x):
    return np.ascontiguousarray(np.asarray(x).astype(np.float32))


def _prep_inputs(inputs):
    x = np.asarray(inputs["L0_attn_x"]).reshape(T, D)
    wq_a = np.asarray(inputs["L0_attn_wq_a"])
    bq_a = np.asarray(inputs["L0_attn_bq_a"])
    g_q = np.asarray(inputs["L0_attn_g_q"])
    b_q = np.asarray(inputs["L0_attn_b_q"])
    wq_b = np.asarray(inputs["L0_attn_wq_b"])
    bq_b = np.asarray(inputs["L0_attn_bq_b"])
    wkv_a = np.asarray(inputs["L0_attn_wkv_a"])
    bkv_a = np.asarray(inputs["L0_attn_bkv_a"])
    wkv_b = np.asarray(inputs["L0_attn_wkv_b"])
    bkv_b = np.asarray(inputs["L0_attn_bkv_b"])
    wout = np.asarray(inputs["L0_attn_wout"])
    bout = np.asarray(inputs["L0_attn_bout"])

    j = np.arange(32, dtype=np.float64)
    inv = 1.0 / (10000.0 ** (2.0 * j / 64.0))
    tpos = np.arange(T, dtype=np.float64)
    ang = inv[:, None] * tpos[None, :]
    cos_t = np.cos(ang).astype(np.float32)
    sin_t = np.sin(ang).astype(np.float32)
    csA = _f32(np.vstack([cos_t, sin_t]))
    csB = _f32(np.vstack([sin_t, cos_t]))

    mask = np.full((128, 1920), -400.0, np.float32)
    for p in range(128):
        mask[p, 896 + p:] = 0.0

    shared = {
        "wqaT": _f32(wq_a.T),
        "bqa_row": _f32(bq_a.reshape(1, QLR)),
        "gq_bc": _f32(np.broadcast_to(g_q.reshape(1, QLR), (128, QLR))),
        "bq_bc": _f32(np.broadcast_to(b_q.reshape(1, QLR), (128, QLR))),
        "wkvaT": _f32(wkv_a.T),
        "bkva_cols": _f32(bkv_a[:KVLR].reshape(4, 128).T),
        "bkva_rope": _f32(bkv_a[KVLR:].reshape(1, 64).T),
        "woutT": _f32(np.concatenate(
            [wout.T[h * 128:(h + 1) * 128]
             for h in list(range(0, 16, 2)) + list(range(1, 16, 2))])),
        "bout_row": _f32(bout.reshape(1, D)),
        "csA_t": csA,
        "csB_t": csB,
        "mask_h": _bf(mask),
        "eye_h": _bf(np.eye(128, dtype=np.float32)),
    }
    if PHASES >= 3:
        shared["w_qkv"] = _f32(inputs["L1_attn_w_qkv"])
        shared["w_o"] = _f32(inputs["L1_attn_w_o"])
    if PHASES >= 4:
        shared["w1T"] = _f32(np.asarray(inputs["L1_ffn_W1"]).T)
        shared["b1_cols"] = _f32(np.asarray(inputs["L1_ffn_b1"]).reshape(32, 128).T)
        shared["w2T"] = _f32(np.asarray(inputs["L1_ffn_W2"]).T)
        shared["b2_cols"] = _f32(np.asarray(inputs["L1_ffn_b2"]).reshape(32, 128).T)
        shared["w3T"] = _f32(np.asarray(inputs["L1_ffn_W3"]).T)
        shared["b3_row"] = _f32(np.asarray(inputs["L1_ffn_b3"]).reshape(1, D))

    in_maps = []
    for r in range(NCORE):
        m = dict(shared)
        m["x_own"] = _f32(x[r * TOK:(r + 1) * TOK])
        m["csA_own"] = _f32(csA[:, r * TOK:(r + 1) * TOK])
        m["csB_own"] = _f32(csB[:, r * TOK:(r + 1) * TOK])
        h0 = 2 * r
        m["wqbT"] = _f32(wq_b[h0 * DQK:(h0 + 2) * DQK, :].T)
        bslice = bq_b[h0 * DQK:(h0 + 2) * DQK].reshape(2, DQK)
        m["bqb_rope"] = _f32(bslice[:, :DR].T)
        m["bqb_nope"] = _f32(bslice[:, DR:].T)
        m["wkvbT"] = _f32(wkv_b[h0 * (DN + DV):(h0 + 2) * (DN + DV), :].T)
        if PHASES >= 2:
            gup = np.asarray(inputs["L0_ffn_gate_up_packed"])
            gsc = np.asarray(inputs["L0_ffn_gate_up_scales"])
            dpk = np.asarray(inputs["L0_ffn_down_packed"])
            dsc = np.asarray(inputs["L0_ffn_down_scales"])
            m["gup_pk"] = np.ascontiguousarray(gup[r * (GU // 8):(r + 1) * (GU // 8)])
            m["gup_scT"] = np.ascontiguousarray(gsc.T[r * (GU // 8):(r + 1) * (GU // 8)])
            m["dn_pk"] = np.ascontiguousarray(dpk[r * (D // 8):(r + 1) * (D // 8)])
            m["dn_scT"] = np.ascontiguousarray(dsc.T[r * (D // 8):(r + 1) * (D // 8)])
        kb = bkv_b[h0 * (DN + DV):(h0 + 2) * (DN + DV)].reshape(2, DN + DV)
        m["bkvb_nope"] = _f32(kb[:, :DN].T)
        m["bkvb_v"] = _f32(kb[:, DN:].reshape(1, 2 * DV))
        in_maps.append(m)
    return in_maps


def kernel(**inputs):
    in_maps = _prep_inputs(inputs)
    nc = build_program()
    res = run_bass_kernel_spmd(nc, in_maps, core_ids=list(range(NCORE)))
    rows = [res.results[r]["out"] for r in range(NCORE)]
    return np.concatenate(rows, axis=0).reshape(1, T, D).astype(np.float32)
